# version 22
# baseline (speedup 1.0000x reference)
import os
import numpy as np
import ml_dtypes
from collections import defaultdict
import contextlib

import concourse.bass as bass
import concourse.mybir as mybir
from concourse.bass_utils import run_bass_kernel_spmd

F32 = mybir.dt.float32
BF16 = mybir.dt.bfloat16
I32 = mybir.dt.int32
AF = mybir.ActivationFunctionType
ALU = mybir.AluOpType

D = 1024
DH = 64
T = 4096
CH = 128
NCH = T // CH
SUP = 2
FF = 4096
NLVL = 4
BS = [4, 8, 16, 32]
RK = [4, 8, 8, 8]
NN = [8, 4, 2, 1]
NLEAF = 16
MAGIC = 0x5F3759DF
DEBUG_TAPS = False
STRICT_ENGINE_ORDER = False
STRICT_ENGINES = ("dv",)
RELAX_OK = os.environ.get("BASS_SIM_FP64") != "1"


def _rank_cols():
    out = {}
    c = 0
    for l in range(NLVL):
        for n in range(NN[l]):
            out[(l, n)] = (c, c + RK[l])
            c += RK[l]
    return out

RANK_COLS = _rank_cols()

def _build_rounds():
    rounds = []
    for g in range(4):
        terms = []
        for j in range(4):
            n = 4 * g + j
            terms.append(dict(kind="leaf", K=4, tile=j, base=32 * (g % 4), n=n))
        rounds.append(terms)
    groups = [[(0, 0), (0, 1), (0, 2), (0, 3)],
              [(0, 4), (0, 5), (0, 6), (0, 7)],
              [(1, 0), (1, 1), (1, 2), (1, 3)],
              [(2, 0), (2, 1), (3, 0)]]
    for kind in ("top", "bot"):
        for gl, grp in enumerate(groups):
            terms = []
            for i, (l, n) in enumerate(grp):
                c0, c1 = RANK_COLS[(l, n)]
                terms.append(dict(kind=kind, K=RK[l], tile=i, base=32 * (gl % 4),
                                  l=l, n=n, c0=c0, c1=c1))
            rounds.append(terms)
    return rounds

ROUNDS = _build_rounds()
NROUND = len(ROUNDS)

def _node_spans(l, n):
    b = BS[l]
    L = slice(2 * b * n, 2 * b * n + b)
    R = slice(2 * b * n + b, 2 * b * (n + 1))
    return L, R


def build_constants(inp):
    UL = [inp[f"L0_attn_UL{l}"] for l in range(NLVL)]
    VR = [inp[f"L0_attn_VR{l}"] for l in range(NLVL)]
    UR = [inp[f"L0_attn_UR{l}"] for l in range(NLVL)]
    VL = [inp[f"L0_attn_VL{l}"] for l in range(NLVL)]

    ULbd = np.zeros((64, 88), np.float32)
    VRbd = np.zeros((64, 88), np.float32)
    URbd = np.zeros((64, 88), np.float32)
    VLbd = np.zeros((64, 88), np.float32)
    msk88 = np.zeros((88, 88), np.float32)
    for l in range(NLVL):
        for n in range(NN[l]):
            L, R = _node_spans(l, n)
            c0, c1 = RANK_COLS[(l, n)]
            ULbd[L, c0:c1] = UL[l]
            VLbd[L, c0:c1] = VL[l]
            VRbd[R, c0:c1] = VR[l]
            URbd[R, c0:c1] = UR[l]
            msk88[c0:c1, c0:c1] = 1.0
    leafm = np.zeros((64, 64), np.float32)
    for n in range(NLEAF):
        leafm[4 * n:4 * n + 4, 4 * n:4 * n + 4] = 1.0

    spvr = np.zeros((64, 512), np.float32)
    spul = np.zeros((64, 512), np.float32)
    spvl = np.zeros((64, 512), np.float32)
    spleaf = np.zeros((64, 512), np.float32)
    for terms in ROUNDS:
        for t in terms:
            tile, base = t["tile"], t["base"]
            c = tile * 128 + base
            if t["kind"] == "leaf":
                n = t["n"]
                for i in range(4):
                    spleaf[4 * n + i, c + i] = 1.0
            elif t["kind"] == "top":
                l, n = t["l"], t["n"]
                L, R = _node_spans(l, n)
                spvr[R, c:c + RK[l]] = VR[l]
            else:
                pass
    for g, terms in enumerate(ROUNDS):
        for t in terms:
            if t["kind"] == "bot":
                l, n = t["l"], t["n"]
                L, R = _node_spans(l, n)
                c = t["tile"] * 128 + t["base"]
                spul[L, c:c + RK[l]] = UL[l]
                spvl[L, c:c + RK[l]] = VL[l]

    proj = np.zeros((128, NROUND * 64), np.float32)
    for g, terms in enumerate(ROUNDS):
        for terms_i, t in enumerate(terms):
            base = 32 * terms_i
            if t["kind"] == "leaf":
                n = t["n"]
                for i in range(4):
                    proj[base + i, g * 64 + 4 * n + i] = 1.0
            else:
                fac = ULbd if t["kind"] == "top" else URbd
                c0, c1 = t["c0"], t["c1"]
                proj[base:base + (c1 - c0), g * 64:(g + 1) * 64] = fac[:, c0:c1].T

    cmask = np.tile((np.arange(CH)[:, None] <= np.arange(CH)[None, :])
                    .astype(ml_dtypes.bfloat16), (1, 4))
    ones64 = np.zeros((128, 128), ml_dtypes.bfloat16)
    ones64[64, :] = 1.0
    cmask = np.concatenate([cmask, ones64], axis=1)

    wqkv = np.concatenate([inp["L0_attn_Wq"], inp["L0_attn_Wk"],
                           inp["L0_attn_Wv"]], axis=1)
    wqkv_r = np.ascontiguousarray(
        wqkv.reshape(8, 128, 192).transpose(1, 0, 2).reshape(128, 8 * 192))
    b1r = np.ascontiguousarray(
        inp["L0_ffn_b1"].reshape(32, 128).T)

    return dict(
        c_wqkv=wqkv_r.astype(np.float32),
        c_wo=np.ascontiguousarray(inp["L0_attn_Wo"]).astype(np.float32),
        c_spvr=spvr, c_spul=spul, c_spvl=spvl, c_spleaf=spleaf,
        c_ulbd=ULbd, c_vrbd=VRbd,
        c_ulbdT=np.ascontiguousarray(ULbd.T), c_vrbdT=np.ascontiguousarray(VRbd.T),
        c_urbdT=np.ascontiguousarray(URbd.T), c_vlbdT=np.ascontiguousarray(VLbd.T),
        c_proj=proj, c_msk88=msk88, c_leafm=leafm, c_cmask=cmask,
        c_idn=np.eye(128, dtype=np.float32),
        c_b1r=b1r.astype(np.float32),
        c_b2=np.ascontiguousarray(inp["L0_ffn_b2"].reshape(1, 1024)).astype(np.float32),
    )


class Prog:
    ENGS = ("pe", "dv", "ac", "po", "sy")

    def __init__(self, nc, sems):
        self.nc = nc
        self.sems = sems
        self.cnt = defaultdict(int)
        self.hwm = defaultdict(lambda: defaultdict(int))
        self.ops = {e: [] for e in self.ENGS}
        self.bankR = defaultdict(list)
        self.bankW = defaultdict(list)

    SELF_SERIAL = ("dv", "ac", "po")
    strict = False
    strict_engines = ()

    def op(self, eng, emit, inc=None, amt=1, tiny=False, relax=False):
        if not RELAX_OK:
            relax = False
        strict_here = self.strict or (eng in self.strict_engines and not relax)
        if (strict_here or tiny) and eng in self.SELF_SERIAL and inc is None:
            inc = eng
        if ((strict_here or tiny) and eng in self.SELF_SERIAL and inc == eng
                and self.cnt[eng] > 0):
            self.wait(eng, (eng, self.cnt[eng]))
        if inc is not None:
            self.cnt[inc] += amt
            val = self.cnt[inc]
            sem = self.sems[inc]
            def f(e, emit=emit, sem=sem, amt=amt):
                emit(e).then_inc(sem, amt)
            self.ops[eng].append(f)
            return (inc, val)
        self.ops[eng].append(lambda e, emit=emit: emit(e))
        return None

    DMA_SEMS = ("dc", "dx0", "dx1", "dw", "dw1", "dw2", "ds")

    def wait(self, eng, ev):
        if ev is None:
            return
        name, val = ev
        if self.hwm[eng][name] >= val:
            return
        if name in self.DMA_SEMS:
            assert val == self.cnt[name], (
                f"partial DMA wait on {name}: {val} != {self.cnt[name]}")
        self.hwm[eng][name] = val
        sem = self.sems[name]
        self.ops[eng].append(lambda e, sem=sem, val=val: e.wait_ge(sem, val))

    def waits(self, eng, evs):
        best = {}
        for ev in evs:
            if ev is None:
                continue
            name, val = ev
            if name not in best or best[name] < val:
                best[name] = val
        for name, val in best.items():
            self.wait(eng, (name, val))

    def bank_write(self, eng, bank):
        for (e2, ev) in self.bankR[bank]:
            if e2 != eng:
                self.wait(eng, ev)
        for (e2, ev) in self.bankW[bank]:
            if e2 != eng:
                self.wait(eng, ev)
        self.bankR[bank] = []

    def bank_note_write(self, eng, bank, ev):
        self.bankW[bank] = [(eng, ev)]

    def bank_read(self, eng, bank):
        for (e2, ev) in self.bankW[bank]:
            if e2 != eng:
                self.wait(eng, ev)

    def bank_note_read(self, eng, bank, ev):
        self.bankR[bank].append((eng, ev))


def build_program(debug_taps=False, do_prep=True, nsuper=NCH // SUP, do_ffn=True, scan_parts=99):
    nc = bass.Bass("TRN2", target_bir_lowering=False, debug=False)

    x = nc.dram_tensor("x", [T, D], F32, kind="ExternalInput").ap()
    W1 = nc.dram_tensor("W1", [FF, D], F32, kind="ExternalInput").ap()
    W2 = nc.dram_tensor("W2", [D, FF], F32, kind="ExternalInput").ap()
    cons = {}
    cons["c_cmask"] = nc.dram_tensor("c_cmask", [128, 640], BF16,
                                     kind="ExternalInput").ap()
    for name, shape in [
        ("c_wqkv", [128, 8 * 192]), ("c_wo", [64, 1024]),
        ("c_spvr", [64, 512]), ("c_spul", [64, 512]), ("c_spvl", [64, 512]),
        ("c_spleaf", [64, 512]), ("c_ulbd", [64, 88]), ("c_vrbd", [64, 88]),
        ("c_ulbdT", [88, 64]), ("c_vrbdT", [88, 64]), ("c_urbdT", [88, 64]),
        ("c_vlbdT", [88, 64]), ("c_proj", [128, NROUND * 64]),
        ("c_msk88", [88, 88]), ("c_leafm", [64, 64]),
        ("c_idn", [128, 128]), ("c_b1r", [128, 32]), ("c_b2", [1, 1024]),
    ]:
        cons[name] = nc.dram_tensor(name, shape, F32, kind="ExternalInput").ap()
    out = nc.dram_tensor("out", [T, D], F32, kind="ExternalOutput").ap()
    taps = {}
    if debug_taps:
        for name, shape in [("dbg_qk", [128, 128]), ("dbg_vt", [64, 128]),
                            ("dbg_y", [64, 128]), ("dbg_den", [1, 128]),
                            ("dbg_ot", [64, 128]), ("dbg_attn", [128, 1024]),
                            ("dbg_x1", [128, 1024]), ("dbg_st", [64, 64]),
                            ("dbg_ht", [128, 256]), ("dbg_w1t", [128, 512]),
                            ("dbg_scale", [1, 2])]:
            taps[name] = nc.dram_tensor(name, shape, F32, kind="ExternalOutput").ap()

    st = contextlib.ExitStack()
    def sb(name, shape, dtype=F32):
        return st.enter_context(nc.sbuf_tensor(name, shape, dtype))
    def ps(name, shape, dtype=F32):
        return st.enter_context(nc.psum_tensor(name, shape, dtype))

    with st:
        sW1T = sb("sW1T", [128, 8 * FF], BF16)
        sW2T = sb("sW2T", [128, 32 * 1024], BF16)
        sHT = sb("sHT", [128, 32 * 256], BF16)
        sHT32 = sHT[:].bitcast(F32)
        sX = sb("sX", [128, 2 * 1024], F32)
        sXT = sb("sXT", [128, 1024], BF16)
        sXBF = sb("sXBF", [128, 1024], BF16)
        sR = sb("sR", [128, 1024], F32)
        sX1 = sb("sX1", [128, 3 * 1024], BF16)
        sX1T = sb("sX1T", [128, 8 * 256], BF16)
        sOUT = sb("sOUT", [128, 1024], F32)
        sQT = sb("sQT", [64, 128], BF16)
        sKT = sb("sKT", [64, 128], BF16)
        sVT = sb("sVT", [64, 128], BF16)
        sKV = sb("sKV", [128, 128], BF16)
        sKR = sb("sKR", [128, 128], BF16)
        sVR_ = sb("sVR_", [128, 128], BF16)
        ssp = {nm: sb("ssp_" + nm, [128, 512], BF16)
               for nm in ("vl", "ql", "v", "q", "k", "qb")}
        sCHAT = sb("sCHAT", [128, 8 * 128], BF16)
        sRDB = sb("sRDB", [1, 128], BF16)
        sJNK = sb("sJNK", [128, 512], BF16)
        sYSP = sb("sYSP", [128, 3 * 512], BF16)
        sOT = sb("sOT", [64, 128], BF16)
        sRD = sb("sRD", [64, 128], BF16)
        sST = sb("sST", [64, 64], F32)
        sSTB = sb("sSTB", [64, 64], BF16)
        sZ = sb("sZ", [64, 1], F32)
        sZB = sb("sZB", [64, 1], BF16)
        sG = sb("sG", [64, 64], BF16)
        sGT = sb("sGT", [64, 64], BF16)
        sT1 = sb("sT1", [64, 88], BF16)
        sT1b = sb("sT1b", [64, 88], BF16)
        sWm = sb("sWm", [88, 88], BF16)
        sWmT = sb("sWmT", [88, 88], BF16)
        sF1 = sb("sF1", [88, 64], BF16)
        sE1 = sb("sE1", [88, 64], BF16)
        sTMP = sb("sTMP", [128, 512], F32)
        sOR = sb("sOR", [1, 128], F32)
        sST1 = sb("sST1", [128, 20], F32)
        sST1I = sST1[:].bitcast(I32)
        sSC = sb("sSC", [4, 4], F32)
        sRS = sb("sRS", [128, 2], F32)
        sABS = sb("sABS", [128, 4], F32)

        sWQKV = sb("sWQKV", [128, 8 * 192], BF16)
        sWOB = sb("sWOB", [128, 1024], BF16)
        sSPc = {nm: sb("sSP_" + nm, [64, 512], BF16)
                for nm in ("vr", "ul", "vl", "leaf")}
        sULbd = sb("sULbd", [64, 88], BF16)
        sVRbd = sb("sVRbd", [64, 88], BF16)
        sULbdT = sb("sULbdT", [88, 64], BF16)
        sVRbdT = sb("sVRbdT", [88, 64], BF16)
        sURbdT = sb("sURbdT", [88, 64], BF16)
        sVLbdT = sb("sVLbdT", [88, 64], BF16)
        sPROJ = sb("sPROJ", [128, NROUND * 64], BF16)
        sMSK88 = sb("sMSK88", [88, 88], F32)
        sLEAFM = sb("sLEAFM", [64, 64], F32)
        sCMASK = sb("sCMASK", [128, 640], BF16)
        sIDN = sb("sIDN", [128, 128], BF16)
        sB1 = sb("sB1", [128, 32], F32)
        sONE = sb("sONE", [128, 1], BF16)
        sONER = sb("sONER", [1, 128], BF16)


        P1 = ps("P1", [128, 512], F32)
        P2 = ps("P2", [128, 512], F32)
        P3 = ps("P3", [128, 512], F32)
        P4 = ps("P4", [128, 512], F32)
        P5 = ps("P5", [128, 512], F32)
        P6 = ps("P6", [128, 512], F32)
        PT = ps("PT", [128, 1024], BF16)
        PS = ps("PS", [128, 512], F32)

        sems = {}
        for nm in ("pe", "dv", "ac", "po", "dc", "dx0", "dx1", "dw", "dw1", "dw2", "ds"):
            sems[nm] = st.enter_context(nc.semaphore(nm))
        b = Prog(nc, sems)
        b.strict = STRICT_ENGINE_ORDER
        b.strict_engines = STRICT_ENGINES

        const_plan = [
            ("c_wqkv", sWQKV, [128, 8 * 192]),
            ("c_wo", None, [64, 1024]),
            ("c_spvr", sSPc["vr"], [64, 512]),
            ("c_spul", sSPc["ul"], [64, 512]),
            ("c_spvl", sSPc["vl"], [64, 512]),
            ("c_spleaf", sSPc["leaf"], [64, 512]),
            ("c_ulbd", sULbd, [64, 88]),
            ("c_vrbd", sVRbd, [64, 88]),
            ("c_ulbdT", sULbdT, [88, 64]),
            ("c_vrbdT", sVRbdT, [88, 64]),
            ("c_urbdT", sURbdT, [88, 64]),
            ("c_vlbdT", sVLbdT, [88, 64]),
            ("c_proj", sPROJ, [128, NROUND * 64]),
            ("c_idn", sIDN, [128, 128]),
            ("c_b2", None, [1, 1024]),
        ]
        f32_plan = [("c_msk88", sMSK88), ("c_leafm", sLEAFM),
                    ("c_b1r", sB1)]
        bf_plan = [("c_cmask", sCMASK)]

        for name, dst in f32_plan + bf_plan:
            b.op("sy", lambda e, name=name, dst=dst:
                 e.dma_start(out=dst[:], in_=cons[name][:]), inc="dc", amt=16)
        cast_done = None
        batch, off = [], 0
        plan = list(const_plan)
        while plan:
            if plan and off + plan[0][2][1] <= 4096:
                name, dst, shp = plan.pop(0)
                ev = b.op("sy", lambda e, name=name, off=off, shp=shp:
                          e.dma_start(out=sHT32[0:shp[0], off:off + shp[1]],
                                      in_=cons[name][:]), inc="dc", amt=16)
                batch.append((name, dst, shp, off, ev))
                off += shp[1]
            if not plan or (plan and off + plan[0][2][1] > 4096):
                b.waits("dv", [bb[4] for bb in batch])
                for name, dst, shp, o2, _ in batch:
                    if dst is None:
                        dst = (sWOB[0:64, :] if name == "c_wo"
                               else sWOB[64:65, :])
                        shp = [shp[0], shp[1]]
                    cast_done = b.op("dv", lambda e, dst=dst, o2=o2, shp=shp:
                                     e.tensor_copy(
                                         out=dst[0:shp[0], 0:shp[1]],
                                         in_=sHT32[0:shp[0], o2:o2 + shp[1]]),
                                     inc="dv")
                if plan:
                    b.wait("sy", cast_done)
                batch, off = [], 0
        b.op("dv", lambda e: e.memset(sONE[:], 1.0))
        b.op("dv", lambda e: e.memset(sONER[:], 1.0))
        b.op("dv", lambda e: e.memset(sABS[:, 2:3], 1.0))
        cast_done = b.op("dv", lambda e: e.memset(sOR[:], 1.0),
                         inc="dv")
        b.op("dv", lambda e: e.memset(sKR[:], 0.0))
        b.op("dv", lambda e: e.memset(sVR_[:], 0.0))
        b.op("dv", lambda e: e.memset(sST[:], 0.0))
        b.op("dv", lambda e: e.memset(sSTB[:], 0.0))
        b.op("dv", lambda e: e.memset(sZ[:], 0.0))
        state_init = b.op("dv", lambda e: e.memset(sZB[:], 0.0), inc="dv")

        b.op("dv", lambda e: e.memset(sABS[:, 0:2], 0.0))
        absinit = b.op("dv", lambda e: e.memset(sABS[:, 3:4], 0.0), inc="dv")
        b.wait("ac", cast_done)
        b.wait("ac", absinit)
        b.wait("po", cast_done)

        W1r = W1.rearrange("(s p) d -> s p d", p=128)
        W2r = W2.rearrange("(s p) f -> s p f", p=128)

        PT6 = P6[:].bitcast(BF16)
        prep_out = {}

        def prep_pass1():
            items = []
            for s in range(0, 32, 2):
                items.append((0, s))
            for s in range(8):
                for h in range(2):
                    items.append((1, (s, h)))
            loadev = [None] * len(items)
            absev = [None] * len(items)
            addev = absinit
            for i, (kind, key) in enumerate(items):
                buf = (i % 2) * 2048
                if i >= 2:
                    b.wait("po", absev[i - 2])
                if kind == 0:
                    def em(e, s=key, buf=buf):
                        return e.dma_start(
                            out=sHT32[:, buf:buf + 2048].rearrange(
                                "p (s d) -> p s d", s=2),
                            in_=W1r[s:s + 2].rearrange("s p d -> p s d"))
                else:
                    s, h = key
                    def em(e, s=s, h=h, buf=buf):
                        return e.dma_start(
                            out=sHT32[:, buf:buf + 2048],
                            in_=W2r[s][:, 2048 * h:2048 * (h + 1)])
                loadev[i] = b.op("po", em, inc="dw" if i % 2 == 0 else "dw1",
                                 amt=16)
                b.wait("ac", loadev[i])
                b.wait("ac", addev)
                absev[i] = b.op("ac", lambda e, buf=buf: e.activation(
                    out=sW2T[:, 0:2048], in_=sHT32[:, buf:buf + 2048],
                    func=AF.Abs, accum_out=sABS[:, 3:4]), inc="ac")
                col = 0 if kind == 0 else 1
                b.wait("dv", absev[i])
                addev = b.op("dv", lambda e, col=col: e.tensor_tensor(
                    out=sABS[:, col:col + 1], in0=sABS[:, col:col + 1],
                    in1=sABS[:, 3:4], op=ALU.add), inc="dv", tiny=True)
                yield
            prep_out["p1"] = addev

        def prep_scale():
            b.waits("pe", [prep_out.get("p1", absinit)])
            b.bank_write("pe", "PS")
            mev = b.op("pe", lambda e: e.matmul(PS[0:1, 0:2], sABS[:, 2:3], sABS[:, 0:2],
                                            start=True, stop=True), inc="pe")
            b.bank_note_write("pe", "PS", mev)
            b.wait("dv", mev)
            b.bank_read("dv", "PS")
            b.op("dv", lambda e: e.tensor_scalar(
            out=sSC[0:1, 0:2], in0=PS[0:1, 0:2], scalar1=1.0 / (FF * D),
            scalar2=1e-8, op0=ALU.mult, op1=ALU.max), tiny=True)
            rev = b.op("dv", lambda e: e.reciprocal(out=sSC[0:1, 2:4],
                                                in_=sSC[0:1, 0:2]),
                       inc="dv", tiny=True)
            b.bank_note_read("dv", "PS", rev)
            if debug_taps:
                b.wait("sy", rev)
                b.op("sy", lambda e: e.dma_start(
                    out=taps["dbg_scale"][:], in_=sSC[0:1, 2:4]),
                    inc="ds", amt=16)
            b.wait("pe", rev)
            b.bank_write("pe", "PS")
            mev = b.op("pe", lambda e: e.matmul(PS[:, 0:2], sOR[:],
                                            sSC[0:1, 2:4], start=True, stop=True),
                   inc="pe")
            b.bank_note_write("pe", "PS", mev)
            b.wait("dv", mev)
            b.bank_read("dv", "PS")
            rsev = b.op("dv", lambda e: e.tensor_copy(out=sRS[:], in_=PS[:, 0:2]),
                        inc="dv")
            b.bank_note_read("dv", "PS", rsev)
            prep_out["rsev"] = rsev


        tern_tmp = sHT32[:, 2048:3072]
        tern_bf = sX1T

        def prep_pass2():
            b.wait("po", prep_out["rsev"])
            units = [("w1", s) for s in range(32)]
            units += [("w2", (s, q)) for q in range(4) for s in range(8)]
            e2s = [None] * len(units)
            e3s = [None] * len(units)
            tevs = [None] * len(units)
            last_evac = None
            BUFS = (0, 1024, 3072)
            for i, (kind, key) in enumerate(units):
                buf = BUFS[i % 3]
                rcol = 0 if kind == "w1" else 1
                if i >= 3:
                    b.wait("po", e2s[i - 3])
                if kind == "w1":
                    def em(e, s=key, buf=buf):
                        return e.dma_start(out=sHT32[:, buf:buf + 1024],
                                           in_=W1r[s])
                else:
                    s, q = key
                    def em(e, s=s, q=q, buf=buf):
                        return e.dma_start(out=sHT32[:, buf:buf + 1024],
                                           in_=W2r[s][:, 1024 * q:1024 * (q + 1)])
                lw = b.op("po", em, inc=("dw", "dw1", "dw2")[i % 3], amt=16)
                obuf = (i % 2) * 1024
                b.wait("dv", lw)
                if i == 0:
                    b.wait("dv", prep_out["rsev"])
                e1 = b.op("dv", lambda e, buf=buf, rcol=rcol: e.tensor_scalar(
                    out=tern_tmp[:, 0:1024],
                    in0=sHT32[:, buf:buf + 1024],
                    scalar1=sRS[:, rcol:rcol + 1], scalar2=-0.5,
                    op0=ALU.mult, op1=ALU.is_le), inc="dv", relax=True)
                if i >= 2:
                    b.wait("dv", tevs[i - 2])
                e2s[i] = b.op("dv", lambda e, buf=buf, obuf=obuf, rcol=rcol:
                              e.tensor_scalar(
                    out=tern_bf[:, obuf:obuf + 1024],
                    in0=sHT32[:, buf:buf + 1024],
                    scalar1=sRS[:, rcol:rcol + 1], scalar2=0.5,
                    op0=ALU.mult, op1=ALU.is_ge), inc="dv", relax=True)
                e3s[i] = b.op("dv", lambda e, obuf=obuf: e.tensor_tensor(
                    out=tern_bf[:, obuf:obuf + 1024],
                    in0=tern_bf[:, obuf:obuf + 1024],
                    in1=tern_tmp[:, 0:1024],
                    op=ALU.subtract), inc="dv", relax=True)
                b.wait("pe", e3s[i])
                b.bank_write("pe", "P6")
                tev = None
                for j in range(8):
                    tev = b.op("pe", lambda e, j=j, obuf=obuf: e.transpose(
                        PT6[:, j * 128:(j + 1) * 128],
                        tern_bf[:, obuf + j * 128:obuf + (j + 1) * 128], sIDN[:]),
                        inc="pe" if j == 7 else None)
                tevs[i] = tev
                b.bank_note_write("pe", "P6", tev)
                b.wait("ac", tev)
                b.bank_read("ac", "P6")
                if kind == "w1":
                    dst = sW1T[:].rearrange("p (j f) -> p j f", j=8)
                    dsl = dst[:, :, 128 * key:128 * key + 128]
                else:
                    s, q = key
                    dst = sW2T[:].rearrange("p (f d) -> p f d", f=32)
                    dsl = dst[:, 8 * q:8 * q + 8, 128 * s:128 * s + 128]
                last_evac = b.op("ac", lambda e, dsl=dsl: e.copy(
                    out=dsl, in_=PT6.rearrange("p (j f) -> p j f", j=8)),
                    inc="ac")
                b.bank_note_read("ac", "P6", last_evac)
                if kind == "w1":
                    prep_out["w1ev"][key] = last_evac
                else:
                    s, q = key
                    if s == 7:
                        prep_out["w2ev"][q] = last_evac
                yield
            prep_out["w1"] = last_evac
            prep_out["w2"] = last_evac

        def prep_gen():
            prep_out["w1ev"] = [None] * 32
            prep_out["w2ev"] = [None] * 4
            if do_prep:
                for _ in prep_pass1():
                    yield
            prep_scale()
            if do_prep:
                for _ in prep_pass2():
                    yield
            else:
                prep_out["w1"] = prep_out["w2"] = prep_out["rsev"]
                prep_out["w1ev"] = [prep_out["rsev"]] * 32
                prep_out["w2ev"] = [prep_out["rsev"]] * 4

        xin = [None, None]
        xin[0] = b.op("sy", lambda e: e.dma_start(
            out=sX[:, 0:1024], in_=x[0:128, :]), inc="dx0", amt=16)

        sx1_consumed = [None] * 3
        sx1_chunk = [-1] * 3
        gen_pool = []
        pool_hooks = {"refill": lambda: None, "done": lambda g: None}
        def advance_pool():
            for g_ in list(gen_pool):
                try:
                    next(g_)
                except StopIteration:
                    gen_pool.remove(g_)
                    pool_hooks["done"](g_)
                    pool_hooks["refill"]()
            if not gen_pool:
                pool_hooks["refill"]()
        sR_last_ac = [None]
        dbg_last = [None]
        sout_stored = [None]
        x_consumed = [None, None]
        xcast_ev = [None, None]
        x1t_free = [None]
        pg_holder = [None]
        zb_ev = [None]
        chat_ring_ev = [None] * 8
        stb_ev = [None]
        chain_gens = []
        chunk_x1 = {}

        def scan_chunk(c):
            guard = 0
            while c >= 3 and sx1_chunk[c % 3] < c - 3:
                advance_pool()
                guard += 1
                assert guard < 100000, "x1 ring consumer never emitted"
            pm, PM = ("P5", P5) if c % 2 == 0 else ("PS", PS)
            zb_prev = zb_ev[0]
            buf = c % 2
            nbuf = (c + 1) % 2
            if c + 1 < NCH:
                b.wait("sy", x_consumed[nbuf])
                xin[nbuf] = b.op("sy", lambda e, c=c, nbuf=nbuf: e.dma_start(
                    out=sX[:, nbuf * 1024:nbuf * 1024 + 1024],
                    in_=x[(c + 1) * 128:(c + 2) * 128, :]),
                    inc="dx0" if nbuf == 0 else "dx1", amt=16)

            xs = sX[:, buf * 1024:(buf + 1) * 1024]
            if c == 0:
                b.wait("ac", xin[buf])
                xcast_ev[buf] = b.op("ac", lambda e, xs=xs: e.copy(
                    out=sXBF[:], in_=xs), inc="ac")
            b.wait("pe", xcast_ev[buf])
            b.bank_write("pe", "PT")
            tev = None
            for j in range(8):
                tev = b.op("pe", lambda e, j=j: e.transpose(
                    PT[:, j * 128:(j + 1) * 128],
                    sXBF[:, j * 128:(j + 1) * 128], sIDN[:]),
                    inc="pe" if j == 7 else None)
            b.bank_note_write("pe", "PT", tev)
            b.wait("ac", tev)
            b.bank_read("ac", "PT")
            xtev = b.op("ac", lambda e: e.copy(out=sXT[:], in_=PT[:]), inc="ac")
            b.bank_note_read("ac", "PT", xtev)
            if c + 1 < NCH:
                nxs = sX[:, nbuf * 1024:(nbuf + 1) * 1024]
                b.wait("ac", xin[nbuf])
                xcast_ev[nbuf] = b.op("ac", lambda e, nxs=nxs: e.copy(
                    out=sXBF[:], in_=nxs), inc="ac")
            yield

            b.wait("pe", xtev)
            b.bank_write("pe", pm)
            qkvev = None
            for which, (r0, c0, wcol) in enumerate(
                    [(0, 0, 0), (64, 0, 64), (0, 128, 128)]):
                for j in range(8):
                    qkvev = b.op("pe", lambda e, r0=r0, c0=c0, wcol=wcol, j=j:
                                 e.matmul(
                        PM[r0:r0 + 64, c0:c0 + 128],
                        sWQKV[:, j * 192 + wcol:j * 192 + wcol + 64],
                        sXT[:, j * 128:(j + 1) * 128],
                        start=(j == 0), stop=(j == 7)),
                        inc="pe" if (which == 2 and j == 7) else None)
            b.bank_note_write("pe", pm, qkvev)
            yield

            b.wait("dv", qkvev)
            b.bank_read("dv", pm)
            b.op("dv", lambda e: e.tensor_scalar(
                out=sTMP[:, 0:128], in0=PM[:, 0:128], scalar1=0.0, scalar2=None,
                op0=ALU.min), relax=True)
            uev = b.op("dv", lambda e: e.tensor_copy(out=sVT[:], in_=PM[0:64, 128:256]),
                       inc="dv", relax=True)
            b.bank_note_read("dv", pm, uev)
            b.wait("ac", uev)
            b.bank_read("ac", pm)
            thev = b.op("ac", lambda e: e.activation(
                out=sTMP[:, 128:256], in_=sTMP[:, 0:128], func=AF.Tanh, scale=0.5),
                inc="ac")
            rlev = b.op("ac", lambda e: e.activation(
                out=sTMP[:, 256:384], in_=PM[:, 0:128], func=AF.Relu), inc="ac")
            b.bank_note_read("ac", pm, rlev)
            b.waits("dv", [thev, rlev])
            b.op("dv", lambda e: e.tensor_scalar(
                out=sTMP[:, 384:512], in0=sTMP[:, 128:256], scalar1=-1.0,
                scalar2=1.0, op0=ALU.mult, op1=ALU.add), relax=True)
            b.op("dv", lambda e: e.reciprocal(out=sTMP[:, 384:512],
                                              in_=sTMP[:, 384:512]), relax=True)
            b.op("dv", lambda e: e.tensor_scalar(
                out=sTMP[:, 0:128], in0=sTMP[:, 128:256], scalar1=1.0,
                scalar2=None, op0=ALU.add), relax=True)
            b.op("dv", lambda e: e.tensor_tensor(
                out=sTMP[:, 0:128], in0=sTMP[:, 0:128], in1=sTMP[:, 384:512],
                op=ALU.mult), relax=True)
            b.op("dv", lambda e: e.tensor_tensor(
                out=sQT[:], in0=sTMP[0:64, 0:128], in1=sTMP[0:64, 256:384],
                op=ALU.add), relax=True)
            qkev = b.op("dv", lambda e: e.tensor_tensor(
                out=sKT[:], in0=sTMP[64:128, 0:128], in1=sTMP[64:128, 256:384],
                op=ALU.add), inc="dv", relax=True)
            yield

            b.wait("pe", qkev)
            b.bank_write("pe", "PT")
            b.op("pe", lambda e: e.transpose(PT[:, 0:64], sKT[:],
                                             sIDN[0:64, 0:64]))
            kvtev = b.op("pe", lambda e: e.transpose(PT[:, 64:128], sVT[:],
                                                     sIDN[0:64, 0:64]), inc="pe")
            b.bank_note_write("pe", "PT", kvtev)
            b.wait("ac", kvtev)
            b.bank_read("ac", "PT")
            kvev = b.op("ac", lambda e: e.copy(out=sKV[:], in_=PT[:, 0:128]),
                        inc="ac")
            b.bank_note_read("ac", "PT", kvev)

            b.bank_write("pe", pm)
            b.op("pe", lambda e: e.matmul(PM[:, 256:344], sKT[:],
                                          sULbd[0:64, :], start=True, stop=True))
            b.op("pe", lambda e: e.matmul(PM[:, 344:432], sVT[:], sVRbd[0:64, :],
                                          start=True, stop=True))
            chzev = b.op("pe", lambda e: e.matmul(
                PM[:, 0:128], sKT[:], sQT[:], start=True, stop=True), inc="pe")
            b.bank_note_write("pe", pm, chzev)
            b.wait("ac", chzev)
            b.bank_read("ac", pm)
            b.op("ac", lambda e: e.copy(out=sKR[:, 0:88], in_=PM[:, 256:344]))
            krev = b.op("ac", lambda e: e.copy(out=sVR_[:, 0:88], in_=PM[:, 344:432]),
                        inc="ac")
            b.bank_note_read("ac", pm, krev)
            b.wait("dv", krev)
            b.bank_read("dv", pm)
            chzm = b.op("dv", lambda e: e.tensor_tensor(
                out=sCHAT[:, 7 * 128:8 * 128], in0=PM[:, 0:128],
                in1=sCMASK[:, 0:128], op=ALU.mult), inc="dv", relax=True)
            b.bank_note_read("dv", pm, chzm)
            yield

            yield
            b.wait("pe", chzm)
            b.wait("pe", zb_prev)
            b.bank_write("pe", pm)
            b.op("pe", lambda e: e.matmul(PM[0:1, 128:256], sONE[:],
                                          sCHAT[:, 7 * 128:8 * 128],
                                          start=True, stop=False))
            denev = b.op("pe", lambda e: e.matmul(
                PM[0:1, 128:256], sZB[:], sQT[:], start=False, stop=True),
                inc="pe")
            b.bank_note_write("pe", pm, denev)
            b.wait("dv", denev)
            b.bank_read("dv", pm)
            b.op("dv", lambda e: e.tensor_scalar(
                out=sTMP[0:1, 0:128], in0=PM[0:1, 128:256], scalar1=1e-6,
                scalar2=None, op0=ALU.max))
            rdev = b.op("dv", lambda e: e.reciprocal(out=sTMP[0:1, 0:128],
                                                     in_=sTMP[0:1, 0:128]), inc="dv")
            rdbf = b.op("dv", lambda e: e.tensor_copy(
                out=sRDB[:], in_=sTMP[0:1, 0:128]), inc="dv")
            b.bank_note_read("dv", pm, rdbf)
            if debug_taps and c == 0:
                b.wait("dv", dbg_last[0])
                dcp = b.op("dv", lambda e: e.tensor_copy(
                    out=sOUT[0:1, 0:128], in_=sTMP[0:1, 0:128]))
                b.wait("sy", dcp)
                dbg_last[0] = b.op("sy", lambda e: e.dma_start(
                    out=taps["dbg_den"][:], in_=sOUT[0:1, 0:128]),
                    inc="ds", amt=16)

            yield
            b.wait("pe", rdbf)
            b.bank_write("pe", pm)
            rdbev = b.op("pe", lambda e: e.matmul(
                PM[0:64, 128:256], sONER[0:1, 0:64], sRDB[:],
                start=True, stop=True), inc="pe")
            b.bank_note_write("pe", pm, rdbev)
            b.wait("dv", rdbev)
            b.bank_read("dv", pm)
            rdcp = b.op("dv", lambda e: e.tensor_copy(out=sRD[:],
                                                      in_=PM[0:64, 128:256]),
                        inc="dv", relax=True)
            b.bank_note_read("dv", pm, rdcp)
            yield

            b.wait("pe", kvev)
            b.bank_write("pe", pm)
            b.op("pe", lambda e: e.matmul(PM[0:64, 0:64], sKV[:, 0:64],
                                          sKV[:, 64:128], start=True, stop=True))
            gev = b.op("pe", lambda e: e.matmul(
                PM[0:64, 64:128], sKV[:, 64:128], sKV[:, 0:64],
                start=True, stop=True), inc="pe")
            b.bank_note_write("pe", pm, gev)
            b.wait("dv", gev)
            b.bank_read("dv", pm)
            b.op("dv", lambda e: e.tensor_copy(out=sG[:], in_=PM[0:64, 0:64]),
                 relax=True)
            gtev = b.op("dv", lambda e: e.tensor_copy(out=sGT[:],
                                                      in_=PM[0:64, 64:128]),
                        inc="dv", relax=True)
            b.op("dv", lambda e: e.tensor_tensor(
                out=sTMP[0:64, 0:64], in0=PM[0:64, 64:128], in1=sLEAFM[:],
                op=ALU.mult), tiny=True)
            stleaf = b.op("dv", lambda e: e.tensor_tensor(
                out=sST[:], in0=sST[:], in1=sTMP[0:64, 0:64], op=ALU.add),
                inc="dv", tiny=True)
            b.bank_note_read("dv", pm, stleaf)
            b.op("dv", lambda e: e.tensor_reduce(
                out=sTMP[0:64, 64:65], in_=sKT[:], axis=mybir.AxisListType.X,
                op=ALU.add), tiny=True)
            zev = b.op("dv", lambda e: e.tensor_tensor(
                out=sZ[:], in0=sZ[:], in1=sTMP[0:64, 64:65], op=ALU.add),
                inc="dv", tiny=True)
            zbev = b.op("dv", lambda e: e.tensor_copy(out=sZB[:], in_=sZ[:]),
                        inc="dv", tiny=True)
            zb_ev[0] = zbev
            yield

            b.wait("pe", qkev)
            spev = {}
            srcs = dict(vl=(sSPc["leaf"], sVT[0:64, :]),
                        ql=(sSPc["leaf"], sQT[:]),
                        v=(sSPc["vr"], sVT[0:64, :]),
                        q=(sSPc["vr"], sQT[:]),
                        k=(sSPc["ul"], sKT[:]),
                        qb=(sSPc["vl"], sQT[:]))
            names = list(srcs.keys())
            for idx, nm in enumerate(names):
                pk = "P1" if idx % 2 == 0 else "P2"
                P = P1 if idx % 2 == 0 else P2
                con, src_ = srcs[nm]
                b.bank_write("pe", pk)
                ev = None
                for tl in range(4):
                    ev = b.op("pe", lambda e, con=con, src_=src_, tl=tl, P=P:
                              e.matmul(P[:, tl * 128:(tl + 1) * 128],
                                       con[:, tl * 128:(tl + 1) * 128], src_,
                                       start=True, stop=True),
                              inc="pe" if tl == 3 else None)
                b.bank_note_write("pe", pk, ev)
                b.wait("ac", ev)
                b.bank_read("ac", pk)
                sev = b.op("ac", lambda e, nm=nm, P=P: e.copy(
                    out=ssp[nm][:], in_=P[:]), inc="ac")
                b.bank_note_read("ac", pk, sev)
                spev[nm] = sev
                if idx % 2 == 1:
                    yield

            ysp_evts = []
            ysp_sb_slices = []
            chat_ring_ev[7] = denev

            def emit_coeff(g):
                terms = ROUNDS[g]
                pk = "P1" if g % 2 == 0 else "P2"
                P = P1 if g % 2 == 0 else P2
                b.waits("pe", [spev[t_side] for t_side in
                               ({"leaf": ("vl", "ql"), "top": ("v", "q"),
                                 "bot": ("k", "qb")}[terms[0]["kind"]])])
                b.bank_write("pe", pk)
                ev = None
                for i, t_ in enumerate(terms):
                    lhs_nm, rhs_nm = {"leaf": ("vl", "ql"), "top": ("v", "q"),
                                      "bot": ("k", "qb")}[t_["kind"]]
                    tl, ba, K = t_["tile"], t_["base"], t_["K"]
                    def em(e, lhs_nm=lhs_nm, rhs_nm=rhs_nm, tl=tl, ba=ba, K=K,
                           i=i, P=P):
                        return e.matmul(
                            P[:, i * 128:(i + 1) * 128],
                            ssp[lhs_nm][ba:ba + K, tl * 128:(tl + 1) * 128],
                            ssp[rhs_nm][ba:ba + K, tl * 128:(tl + 1) * 128],
                            start=True, stop=True,
                            tile_position=(ba, 0))
                    ev = b.op("pe", em, inc="pe" if i == len(terms) - 1 else None)
                b.bank_note_write("pe", pk, ev)
                return ev

            def emit_mask(g, ev):
                terms = ROUNDS[g]
                nt = len(terms)
                pk = "P1" if g % 2 == 0 else "P2"
                P = P1 if g % 2 == 0 else P2
                b.wait("dv", ev)
                b.bank_read("dv", pk)
                slot = (g * 4) % 8
                for i in range(nt):
                    b.wait("dv", chat_ring_ev[(slot + i) % 8])
                meva = b.op("dv", lambda e, slot=slot, P=P: e.tensor_tensor(
                    out=sCHAT[:, slot * 128:(slot + 2) * 128],
                    in0=P[:, 0:256],
                    in1=sCMASK[:, 0:256],
                    op=ALU.mult), inc="dv", relax=True)
                mevb = b.op("dv", lambda e, nt=nt, slot=slot, P=P: e.tensor_tensor(
                    out=sCHAT[:, (slot + 2) * 128:(slot + nt) * 128],
                    in0=P[:, 256:nt * 128],
                    in1=sCMASK[:, 256:nt * 128],
                    op=ALU.mult), inc="dv", relax=True)
                b.bank_note_read("dv", pk, mevb)
                return (meva, mevb)

            def emit_apply(g, mev2):
                terms = ROUNDS[g]
                nt = len(terms)
                slot = (g * 4) % 8
                ypos = g % 4
                b.wait("pe", krev)
                b.wait("pe", mev2[0])
                if ypos == 0:
                    b.bank_write("pe", "P3")
                aev = None
                for i, t_ in enumerate(terms):
                    if i == 2:
                        b.wait("pe", mev2[1])
                    if t_["kind"] == "leaf":
                        lhs = sKV[:, 4 * t_["n"]:4 * t_["n"] + 32]
                    elif t_["kind"] == "top":
                        lhs = sKR[:, t_["c0"]:t_["c0"] + 32]
                    else:
                        lhs = sVR_[:, t_["c0"]:t_["c0"] + 32]
                    def em(e, lhs=lhs, i=i, slot=slot, ypos=ypos):
                        return e.matmul(
                            P3[32 * i:32 * i + 32, ypos * 128:(ypos + 1) * 128],
                            lhs, sCHAT[:, (slot + i) * 128:(slot + i + 1) * 128],
                            start=True, stop=True, tile_position=(0, 32 * i))
                    aev = b.op("pe", em, inc="pe" if (i == nt - 1 and nt == 4)
                               else None)
                if nt == 3:
                    def em(e, slot=slot, ypos=ypos):
                        return e.matmul(
                            P3[96:128, ypos * 128:(ypos + 1) * 128],
                            sKR[:, 96:128],
                            sCHAT[:, slot * 128:(slot + 1) * 128],
                            start=True, stop=True, tile_position=(0, 96))
                    aev = b.op("pe", em, inc="pe")
                for i in range(nt):
                    chat_ring_ev[(slot + i) % 8] = aev
                if ypos == 3 or g == NROUND - 1:
                    b.bank_note_write("pe", "P3", aev)
                    third = g // 4
                    b.wait("ac", aev)
                    b.bank_read("ac", "P3")
                    yev = b.op("ac", lambda e, third=third: e.copy(
                        out=sYSP[:, third * 512:(third + 1) * 512], in_=P3[:]),
                        inc="ac")
                    b.bank_note_read("ac", "P3", yev)
                    ysp_evts.append(yev)
                    for gg in range(g - ypos, g + 1):
                        ysp_sb_slices.append(
                            (sYSP[:, third * 512 + (gg % 4) * 128:
                                  third * 512 + (gg % 4 + 1) * 128], gg))

            cev = {0: emit_coeff(0)}
            for g in range(NROUND):
                if g + 1 < NROUND:
                    cev[g + 1] = emit_coeff(g + 1)
                mev2 = emit_mask(g, cev[g])
                emit_apply(g, mev2)
                yield

            while chain_gens:
                g0 = chain_gens[0]
                try:
                    next(g0)
                except StopIteration:
                    chain_gens.pop(0)
            b.wait("pe", stb_ev[0])
            b.waits("pe", ysp_evts)
            b.waits("pe", [stleaf, rdcp])
            b.bank_write("pe", pm)
            b.op("pe", lambda e: e.matmul(PM[0:64, 384:512], sSTB[:],
                                          sQT[:], start=True, stop=False))
            yev = None
            for (ysl, gg) in ysp_sb_slices:
                yev = b.op("pe", lambda e, ysl=ysl, gg=gg: e.matmul(
                    PM[0:64, 384:512], sPROJ[:, gg * 64:(gg + 1) * 64], ysl,
                    start=False, stop=(gg == NROUND - 1)),
                    inc="pe" if gg == NROUND - 1 else None)
            b.bank_note_write("pe", pm, yev)
            if debug_taps and c == 0:
                b.wait("dv", yev)
                b.bank_read("dv", pm)
                b.wait("dv", dbg_last[0])
                dvy = b.op("dv", lambda e: e.tensor_copy(out=sOUT[0:64, 0:128],
                                                         in_=PM[0:64, 384:512]),
                           inc="dv")
                b.bank_note_read("dv", pm, dvy)
                b.wait("sy", dvy)
                dbg_last[0] = b.op("sy", lambda e: e.dma_start(
                    out=taps["dbg_y"][:], in_=sOUT[0:64, 0:128]),
                    inc="ds", amt=16)

            b.wait("dv", yev)
            b.wait("dv", rdcp)
            b.bank_read("dv", pm)
            otev = b.op("dv", lambda e: e.tensor_tensor(
                out=sOT[:], in0=PM[0:64, 384:512], in1=sRD[:], op=ALU.mult),
                inc="dv", relax=True)
            b.bank_note_read("dv", pm, otev)
            yield

            yield
            b.wait("pe", otev)
            b.bank_write("pe", pm)
            aaev = b.op("pe", lambda e: e.matmul(
                PM[:, 0:512], sOT[:], sWOB[0:64, 0:512], start=True, stop=True),
                inc="pe")
            b.bank_note_write("pe", pm, aaev)
            b.waits("dv", [aaev])
            b.wait("dv", sR_last_ac[0])
            b.bank_read("dv", pm)
            r1a = b.op("dv", lambda e, xs=xs: e.scalar_tensor_tensor(
                out=sR[:, 0:512], in0=xs[:, 0:512], scalar=1.0, in1=PM[:, 0:512],
                op0=ALU.mult, op1=ALU.add, accum_out=sST1[:, 0:1]), inc="dv",
                relax=True)
            b.bank_note_read("dv", pm, r1a)
            yield
            b.wait("pe", r1a)
            b.bank_write("pe", pm)
            abev = b.op("pe", lambda e: e.matmul(
                PM[:, 0:512], sOT[:], sWOB[0:64, 512:1024], start=True, stop=True),
                inc="pe")
            b.bank_note_write("pe", pm, abev)
            b.wait("dv", abev)
            b.bank_read("dv", pm)
            r1ev = b.op("dv", lambda e, xs=xs: e.scalar_tensor_tensor(
                out=sR[:, 512:1024], in0=xs[:, 512:1024], scalar=1.0,
                in1=PM[:, 0:512], op0=ALU.mult, op1=ALU.add,
                accum_out=sST1[:, 1:2]), inc="dv", relax=True)
            b.bank_note_read("dv", pm, r1ev)
            x_consumed[buf] = r1ev
            if debug_taps and c == 0:
                b.wait("dv", dbg_last[0])
                dra = b.op("dv", lambda e: e.tensor_copy(
                    out=sOUT[:, 0:1024], in_=sR[:]))
                b.wait("sy", dra)
                dbg_last[0] = b.op("sy", lambda e: e.dma_start(
                    out=taps["dbg_attn"][:], in_=sOUT[:, 0:1024]),
                    inc="ds", amt=16)
            b.wait("ac", r1ev)
            sq1 = b.op("ac", lambda e: e.activation(
                out=sJNK[:], in_=sR[:, 0:512],
                func=AF.Square, accum_out=sST1[:, 2:3]), inc="ac")
            b.wait("ac", sq1)
            sqev = b.op("ac", lambda e: e.activation(
                out=sJNK[:], in_=sR[:, 512:1024],
                func=AF.Square, accum_out=sST1[:, 8:9]), inc="ac")
            sR_last_ac[0] = sqev
            b.wait("dv", sqev)
            b.op("dv", lambda e: e.tensor_tensor(
                out=sST1[:, 2:3], in0=sST1[:, 2:3], in1=sST1[:, 8:9], op=ALU.add), tiny=True)
            b.op("dv", lambda e: e.tensor_tensor(
                out=sST1[:, 0:1], in0=sST1[:, 0:1], in1=sST1[:, 1:2], op=ALU.add), tiny=True)
            b.op("dv", lambda e: e.tensor_scalar(
                out=sST1[:, 3:4], in0=sST1[:, 0:1], scalar1=1.0 / D, scalar2=None,
                op0=ALU.mult), tiny=True)
            b.op("dv", lambda e: e.tensor_tensor(
                out=sST1[:, 4:5], in0=sST1[:, 3:4], in1=sST1[:, 3:4], op=ALU.mult), tiny=True)
            b.op("dv", lambda e: e.tensor_scalar(
                out=sST1[:, 2:3], in0=sST1[:, 2:3], scalar1=1.0 / D,
                scalar2=None, op0=ALU.mult), tiny=True)
            b.op("dv", lambda e: e.tensor_tensor(
                out=sST1[:, 4:5], in0=sST1[:, 2:3], in1=sST1[:, 4:5],
                op=ALU.subtract), tiny=True)
            b.op("dv", lambda e: e.tensor_scalar(
                out=sST1[:, 4:5], in0=sST1[:, 4:5], scalar1=1e-5, scalar2=None,
                op0=ALU.add), tiny=True)
            _rsqrt(b, sST1, sST1I, 4, 5)
            b.op("dv", lambda e: e.tensor_tensor(
                out=sST1[:, 6:7], in0=sST1[:, 3:4], in1=sST1[:, 5:6], op=ALU.mult), tiny=True)
            b.op("dv", lambda e: e.tensor_scalar(
                out=sST1[:, 6:7], in0=sST1[:, 6:7], scalar1=-1.0, scalar2=None,
                op0=ALU.mult), tiny=True)
            b.wait("dv", sx1_consumed[c % 3])
            x1ev = b.op("dv", lambda e, c=c: e.tensor_scalar(
                out=sX1[:, (c % 3) * 1024:(c % 3 + 1) * 1024], in0=sR[:],
                scalar1=sST1[:, 5:6], scalar2=sST1[:, 6:7],
                op0=ALU.mult, op1=ALU.add), inc="dv")
            chunk_x1[c] = x1ev
            if debug_taps and c == 0:
                b.wait("dv", dbg_last[0])
                dx1 = b.op("dv", lambda e: e.tensor_copy(out=sOUT[:, 0:512],
                                                         in_=sX1[:, 0:512]))
                b.wait("sy", dx1)
                dbg_last[0] = b.op("sy", lambda e: e.dma_start(
                    out=taps["dbg_x1"][:, 0:512], in_=sOUT[:, 0:512]),
                    inc="ds", amt=16)
            yield

            def chain_gen(pm=pm, PM=PM, gtev=gtev, zev=zev, c=c):
                def chain_mm(region, lhsT, rhs, dst=None, mask=None):
                    b.bank_write("pe", pm)
                    e1 = b.op("pe", lambda e, region=region, lhsT=lhsT, rhs=rhs:
                              e.matmul(region, lhsT, rhs, start=True, stop=True),
                              inc="pe")
                    b.bank_note_write("pe", pm, e1)
                    b.wait("dv", e1)
                    b.bank_read("dv", pm)
                    if mask is None:
                        c1 = b.op("dv", lambda e, dst=dst, region=region:
                                  e.tensor_copy(out=dst, in_=region), inc="dv",
                                  relax=True)
                    else:
                        c1 = b.op("dv", lambda e, dst=dst, region=region, mask=mask:
                                  e.tensor_tensor(out=dst, in0=region, in1=mask,
                                                  op=ALU.mult), inc="dv",
                                  relax=True)
                    b.bank_note_read("dv", pm, c1)
                    return c1

                b.wait("pe", gtev)
                chain_mm(PM[0:64, 0:88], sG[:], sULbd[0:64, :], dst=sT1[:])
                yield
                chain_mm(PM[0:64, 128:216], sGT[:], sVRbd[0:64, :], dst=sT1b[:])
                yield
                chain_mm(PM[0:88, 256:344], sVRbd[0:64, :], sT1[:], dst=sWm[:],
                         mask=sMSK88[:])
                yield
                chain_mm(PM[0:88, 344:432], sULbd[0:64, :], sT1b[:], dst=sWmT[:],
                         mask=sMSK88[:])
                yield
                chain_mm(PM[0:88, 0:64], sWmT[:], sULbdT[:], dst=sF1[:])
                yield
                chain_mm(PM[0:88, 128:192], sWm[:], sURbdT[:], dst=sE1[:])
                yield
                b.bank_write("pe", pm)
                b.op("pe", lambda e: e.matmul(PM[0:64, 256:320], sVRbdT[:], sF1[:],
                                              start=True, stop=False))
                e7 = b.op("pe", lambda e: e.matmul(PM[0:64, 256:320], sVLbdT[:],
                                                   sE1[:], start=False, stop=True),
                          inc="pe")
                b.bank_note_write("pe", pm, e7)
                b.wait("dv", e7)
                b.wait("dv", zev)
                b.bank_read("dv", pm)
                stev = b.op("dv", lambda e: e.tensor_tensor(
                    out=sST[:], in0=sST[:], in1=PM[0:64, 256:320], op=ALU.add),
                    inc="dv", tiny=True)
                b.bank_note_read("dv", pm, stev)
                stbev = b.op("dv", lambda e: e.tensor_copy(out=sSTB[:], in_=sST[:]),
                             inc="dv", tiny=True)
                stb_ev[0] = stbev
                if debug_taps and c == 0:
                    b.wait("dv", dbg_last[0])
                    dst_ = b.op("dv", lambda e: e.tensor_copy(out=sOUT[0:64, 0:64],
                                                              in_=sST[:]))
                    b.wait("sy", dst_)
                    dbg_last[0] = b.op("sy", lambda e: e.dma_start(
                        out=taps["dbg_st"][:], in_=sOUT[0:64, 0:64]),
                        inc="ds", amt=16)

            chain_gens.append(chain_gen())

        def ffn_super(sc):
            c0 = 2 * sc
            x1evs = [chunk_x1[c0], chunk_x1[c0 + 1]]
            evx = []
            for ci in range(SUP):
                c = c0 + ci
                b.wait("pe", x1evs[ci])
                b.bank_write("pe", "PT")
                tev = None
                for j in range(8):
                    def em(e, c=c, j=j):
                        return e.transpose(
                            PT[:, j * 128:(j + 1) * 128],
                            sX1[:, (c % 3) * 1024 + j * 128:
                                (c % 3) * 1024 + (j + 1) * 128], sIDN[:])
                    tev = b.op("pe", em, inc="pe" if j == 7 else None)
                b.bank_note_write("pe", "PT", tev)
                b.wait("ac", tev)
                b.wait("ac", x1t_free[0])
                b.bank_read("ac", "PT")
                dst = sX1T[:].rearrange("p (j t) -> p j t", j=8)
                eev = b.op("ac", lambda e, ci=ci, dst=dst: e.copy(
                    out=dst[:, :, ci * 128:(ci + 1) * 128],
                    in_=PT[:].rearrange("p (j t) -> p j t", j=8)), inc="ac")
                b.bank_note_read("ac", "PT", eev)
                evx.append(eev)
                yield

            b.waits("pe", evx)
            for i in range(32):
                while prep_out["w1ev"][i] is None:
                    try:
                        next(pg_holder[0])
                    except StopIteration:
                        break
                b.wait("pe", prep_out["w1ev"][i])
                pk = "P6" if i % 2 == 0 else "P4"
                P = P6 if i % 2 == 0 else P4
                b.bank_write("pe", pk)
                ev = None
                for j in range(8):
                    def em(e, i=i, j=j, P=P):
                        return e.matmul(
                            P[:, 0:256],
                            sW1T[:, j * FF + i * 128:j * FF + i * 128 + 128],
                            sX1T[:, j * 256:(j + 1) * 256],
                            start=(j == 0), stop=(j == 7))
                    ev = b.op("pe", em, inc="pe" if j == 7 else None)
                b.bank_note_write("pe", pk, ev)
                b.wait("ac", ev)
                b.bank_read("ac", pk)
                gev = b.op("ac", lambda e, i=i, P=P: e.activation(
                    out=sHT[:, i * 256:(i + 1) * 256], in_=P[:, 0:256],
                    func=AF.Gelu, bias=sB1[:, i:i + 1]), inc="ac")
                b.bank_note_read("ac", pk, gev)
                yield
            x1t_free[0] = ev
            hlast = gev
            if debug_taps and sc == 0:
                b.wait("dv", hlast)
                b.wait("dv", dbg_last[0])
                dh = b.op("dv", lambda e: e.tensor_copy(out=sOUT[:, 0:256],
                                                        in_=sHT[:, 0:256]),
                          inc="dv")
                b.wait("sy", dh)
                dbg_last[0] = b.op("sy", lambda e: e.dma_start(
                    out=taps["dbg_ht"][:], in_=sOUT[:, 0:256]),
                    inc="ds", amt=16)

            b.wait("pe", hlast)
            for ci in range(SUP):
                c = c0 + ci
                for half in range(2):
                    b.bank_write("pe", "P4")
                    b.op("pe", lambda e, half=half: e.matmul(
                        P4[:], sCMASK[64:65, 512:640],
                        sWOB[64:65, half * 512:(half + 1) * 512],
                        start=True, stop=False))
                    ev = None
                    for i in range(32):
                        while prep_out["w2ev"][i // 8] is None:
                            try:
                                next(pg_holder[0])
                            except StopIteration:
                                break
                        b.wait("pe", prep_out["w2ev"][i // 8])
                        def em(e, i=i, ci=ci, half=half):
                            return e.matmul(
                                P4[:],
                                sHT[:, i * 256 + ci * 128:i * 256 + ci * 128 + 128],
                                sW2T[:, i * 1024 + half * 512:
                                     i * 1024 + (half + 1) * 512],
                                start=False, stop=(i == 31))
                        ev = b.op("pe", em, inc="pe" if i == 31 else None)
                    b.bank_note_write("pe", "P4", ev)
                    b.wait("dv", ev)
                    if half == 0:
                        b.wait("dv", sout_stored[0])
                    b.bank_read("dv", "P4")
                    r2ev = b.op("dv", lambda e, c=c, half=half: e.scalar_tensor_tensor(
                        out=sOUT[:, half * 512:(half + 1) * 512],
                        in0=sX1[:, (c % 3) * 1024 + half * 512:
                               (c % 3) * 1024 + (half + 1) * 512],
                        scalar=1.0, in1=P4[:], op0=ALU.mult, op1=ALU.add,
                        accum_out=sST1[:, 9 + half:10 + half]), inc="dv",
                        relax=True)
                    b.bank_note_read("dv", "P4", r2ev)
                    yield
                sx1_consumed[c % 3] = r2ev
                sx1_chunk[c % 3] = c
                b.wait("ac", r2ev)
                sq1 = b.op("ac", lambda e: e.activation(
                    out=sJNK[:], in_=sOUT[:, 0:512],
                    func=AF.Square, accum_out=sST1[:, 11:12]), inc="ac")
                b.wait("ac", sq1)
                sqev = b.op("ac", lambda e: e.activation(
                    out=sJNK[:], in_=sOUT[:, 512:1024],
                    func=AF.Square, accum_out=sST1[:, 12:13]), inc="ac")
                b.wait("dv", sqev)
                b.op("dv", lambda e: e.tensor_tensor(
                    out=sST1[:, 11:12], in0=sST1[:, 11:12], in1=sST1[:, 12:13],
                    op=ALU.add), tiny=True)
                b.op("dv", lambda e: e.tensor_tensor(
                    out=sST1[:, 9:10], in0=sST1[:, 9:10], in1=sST1[:, 10:11],
                    op=ALU.add), tiny=True)
                b.op("dv", lambda e: e.tensor_scalar(
                    out=sST1[:, 13:14], in0=sST1[:, 9:10], scalar1=1.0 / D,
                    scalar2=None, op0=ALU.mult), tiny=True)
                b.op("dv", lambda e: e.tensor_tensor(
                    out=sST1[:, 14:15], in0=sST1[:, 13:14], in1=sST1[:, 13:14],
                    op=ALU.mult), tiny=True)
                b.op("dv", lambda e: e.tensor_scalar(
                    out=sST1[:, 11:12], in0=sST1[:, 11:12], scalar1=1.0 / D,
                    scalar2=None, op0=ALU.mult), tiny=True)
                b.op("dv", lambda e: e.tensor_tensor(
                    out=sST1[:, 14:15], in0=sST1[:, 11:12], in1=sST1[:, 14:15],
                    op=ALU.subtract), tiny=True)
                b.op("dv", lambda e: e.tensor_scalar(
                    out=sST1[:, 14:15], in0=sST1[:, 14:15], scalar1=1e-5,
                    scalar2=None, op0=ALU.add), tiny=True)
                _rsqrt(b, sST1, sST1I, 14, 15, scr=17)
                b.op("dv", lambda e: e.tensor_tensor(
                    out=sST1[:, 16:17], in0=sST1[:, 13:14], in1=sST1[:, 15:16],
                    op=ALU.mult), tiny=True)
                nev = b.op("dv", lambda e: e.tensor_scalar(
                    out=sST1[:, 16:17], in0=sST1[:, 16:17], scalar1=-1.0,
                    scalar2=None, op0=ALU.mult), inc="dv", tiny=True)
                b.wait("ac", nev)
                if debug_taps:
                    b.wait("ac", dbg_last[0])
                oev = b.op("ac", lambda e: e.activation(
                    out=sOUT[:], in_=sOUT[:],
                    func=AF.Identity, bias=sST1[:, 16:17], scale=sST1[:, 15:16]),
                    inc="ac")
                b.wait("sy", oev)
                sout_stored[0] = b.op("sy", lambda e, c=c: e.dma_start(
                    out=out[c * 128:(c + 1) * 128, :], in_=sOUT[:]),
                    inc="ds", amt=16)
                yield

        b.wait("pe", state_init)
        b.wait("dv", state_init)

        pg = prep_gen()
        pg_holder[0] = pg
        others = gen_pool
        others.append(pg)
        pending = []
        done_ffn = [0]

        def refill():
            if pg not in others and pending and len(others) == 0:
                others.append(ffn_super(pending.pop(0)))

        def on_done(g_):
            if g_ is not pg:
                done_ffn[0] += 1

        pool_hooks["refill"] = refill
        pool_hooks["done"] = on_done

        def pump(target):
            while True:
                try:
                    next(target)
                except StopIteration:
                    return
                for g_ in list(chain_gens):
                    try:
                        next(g_)
                    except StopIteration:
                        chain_gens.remove(g_)
                advance_pool()
                advance_pool()

        for sc in range(nsuper):
            for ci in range(SUP):
                pump(scan_chunk(2 * sc + ci))
            pending.append(sc)
            refill()
        while others or pending:
            advance_pool()
        assert not pending

        b.wait("sy", sout_stored[0])
        total_st = b.cnt["ds"]
        b.ops["sy"].append(lambda e: e.wait_ge(sems["ds"], total_st))

        with nc.Block() as blk:
            @blk.sync
            def _(e):
                for f in b.ops["sy"]:
                    f(e)

            @blk.vector
            def _(e):
                for f in b.ops["dv"]:
                    f(e)

            @blk.scalar
            def _(e):
                for f in b.ops["ac"]:
                    f(e)

            @blk.gpsimd
            def _(e):
                for f in b.ops["po"]:
                    f(e)

            @blk.tensor
            def _(e):
                for f in b.ops["pe"]:
                    f(e)

    return nc


def _rsqrt(b, sST1, sST1I, src, dst, scr=7):
    b.op("dv", lambda e: e.tensor_scalar(
        out=sST1I[:, dst:dst + 1], in0=sST1I[:, src:src + 1], scalar1=1,
        scalar2=None, op0=ALU.arith_shift_right), tiny=True)
    b.op("dv", lambda e: e.tensor_scalar(
        out=sST1I[:, dst:dst + 1], in0=sST1I[:, dst:dst + 1], scalar1=-1,
        scalar2=MAGIC, op0=ALU.mult, op1=ALU.add), tiny=True)
    for _ in range(3):
        b.op("dv", lambda e: e.tensor_tensor(
            out=sST1[:, scr:scr + 1], in0=sST1[:, dst:dst + 1], in1=sST1[:, dst:dst + 1],
            op=ALU.mult), tiny=True)
        b.op("dv", lambda e: e.tensor_tensor(
            out=sST1[:, scr:scr + 1], in0=sST1[:, scr:scr + 1], in1=sST1[:, src:src + 1],
            op=ALU.mult), tiny=True)
        b.op("dv", lambda e: e.tensor_scalar(
            out=sST1[:, scr:scr + 1], in0=sST1[:, scr:scr + 1], scalar1=-0.5, scalar2=1.5,
            op0=ALU.mult, op1=ALU.add), tiny=True)
        b.op("dv", lambda e: e.tensor_tensor(
            out=sST1[:, dst:dst + 1], in0=sST1[:, dst:dst + 1], in1=sST1[:, scr:scr + 1],
            op=ALU.mult), tiny=True)



_NC_CACHE = {}

def kernel(**inputs):
    key = ("prog", DEBUG_TAPS)
    if key not in _NC_CACHE:
        _NC_CACHE[key] = build_program(debug_taps=DEBUG_TAPS)
    nc = _NC_CACHE[key]
    consts = build_constants(inputs)
    x = np.ascontiguousarray(inputs["L0_attn_x"].astype(np.float32, copy=False))
    shared = dict(W1=np.ascontiguousarray(inputs["L0_ffn_W1"]),
                  W2=np.ascontiguousarray(inputs["L0_ffn_W2"]), **consts)
    in_maps = [dict(x=np.ascontiguousarray(x[bb]), **shared) for bb in range(8)]
    res = run_bass_kernel_spmd(nc, in_maps, core_ids=list(range(8)))
    out = np.stack([res.results[bb]["out"] for bb in range(8)], axis=0)
    return out.astype(np.float32, copy=False)


# version 23
# speedup vs baseline: 1.0776x; 1.0776x over previous
import os
import numpy as np
import ml_dtypes
from collections import defaultdict
import contextlib

import concourse.bass as bass
import concourse.mybir as mybir
from concourse.bass_utils import run_bass_kernel_spmd

F32 = mybir.dt.float32
BF16 = mybir.dt.bfloat16
I32 = mybir.dt.int32
AF = mybir.ActivationFunctionType
ALU = mybir.AluOpType

D = 1024
DH = 64
T = 4096
CH = 128
NCH = T // CH
SUP = 2
FF = 4096
NLVL = 4
BS = [4, 8, 16, 32]
RK = [4, 8, 8, 8]
NN = [8, 4, 2, 1]
NLEAF = 16
MAGIC = 0x5F3759DF
DEBUG_TAPS = False
STRICT_ENGINE_ORDER = False
STRICT_ENGINES = ("dv",)
RELAX_OK = os.environ.get("BASS_SIM_FP64") != "1"


def _rank_cols():
    out = {}
    c = 0
    for l in range(NLVL):
        for n in range(NN[l]):
            out[(l, n)] = (c, c + RK[l])
            c += RK[l]
    return out

RANK_COLS = _rank_cols()

def _build_rounds():
    rounds = []
    for g in range(4):
        terms = []
        for j in range(4):
            n = 4 * g + j
            terms.append(dict(kind="leaf", K=4, tile=j, base=32 * (g % 4), n=n))
        rounds.append(terms)
    groups = [[(0, 0), (0, 1), (0, 2), (0, 3)],
              [(0, 4), (0, 5), (0, 6), (0, 7)],
              [(1, 0), (1, 1), (1, 2), (1, 3)],
              [(2, 0), (2, 1), (3, 0)]]
    for kind in ("top", "bot"):
        for gl, grp in enumerate(groups):
            terms = []
            for i, (l, n) in enumerate(grp):
                c0, c1 = RANK_COLS[(l, n)]
                terms.append(dict(kind=kind, K=RK[l], tile=i, base=32 * (gl % 4),
                                  l=l, n=n, c0=c0, c1=c1))
            rounds.append(terms)
    return rounds

ROUNDS = _build_rounds()
NROUND = len(ROUNDS)

def _node_spans(l, n):
    b = BS[l]
    L = slice(2 * b * n, 2 * b * n + b)
    R = slice(2 * b * n + b, 2 * b * (n + 1))
    return L, R


def build_constants(inp):
    UL = [inp[f"L0_attn_UL{l}"] for l in range(NLVL)]
    VR = [inp[f"L0_attn_VR{l}"] for l in range(NLVL)]
    UR = [inp[f"L0_attn_UR{l}"] for l in range(NLVL)]
    VL = [inp[f"L0_attn_VL{l}"] for l in range(NLVL)]

    ULbd = np.zeros((64, 88), np.float32)
    VRbd = np.zeros((64, 88), np.float32)
    URbd = np.zeros((64, 88), np.float32)
    VLbd = np.zeros((64, 88), np.float32)
    msk88 = np.zeros((88, 88), np.float32)
    for l in range(NLVL):
        for n in range(NN[l]):
            L, R = _node_spans(l, n)
            c0, c1 = RANK_COLS[(l, n)]
            ULbd[L, c0:c1] = UL[l]
            VLbd[L, c0:c1] = VL[l]
            VRbd[R, c0:c1] = VR[l]
            URbd[R, c0:c1] = UR[l]
            msk88[c0:c1, c0:c1] = 1.0
    leafm = np.zeros((64, 64), np.float32)
    for n in range(NLEAF):
        leafm[4 * n:4 * n + 4, 4 * n:4 * n + 4] = 1.0

    spvr = np.zeros((64, 512), np.float32)
    spul = np.zeros((64, 512), np.float32)
    spvl = np.zeros((64, 512), np.float32)
    spleaf = np.zeros((64, 512), np.float32)
    for terms in ROUNDS:
        for t in terms:
            tile, base = t["tile"], t["base"]
            c = tile * 128 + base
            if t["kind"] == "leaf":
                n = t["n"]
                for i in range(4):
                    spleaf[4 * n + i, c + i] = 1.0
            elif t["kind"] == "top":
                l, n = t["l"], t["n"]
                L, R = _node_spans(l, n)
                spvr[R, c:c + RK[l]] = VR[l]
            else:
                pass
    for g, terms in enumerate(ROUNDS):
        for t in terms:
            if t["kind"] == "bot":
                l, n = t["l"], t["n"]
                L, R = _node_spans(l, n)
                c = t["tile"] * 128 + t["base"]
                spul[L, c:c + RK[l]] = UL[l]
                spvl[L, c:c + RK[l]] = VL[l]

    proj = np.zeros((128, NROUND * 64), np.float32)
    for g, terms in enumerate(ROUNDS):
        for terms_i, t in enumerate(terms):
            base = 32 * terms_i
            if t["kind"] == "leaf":
                n = t["n"]
                for i in range(4):
                    proj[base + i, g * 64 + 4 * n + i] = 1.0
            else:
                fac = ULbd if t["kind"] == "top" else URbd
                c0, c1 = t["c0"], t["c1"]
                proj[base:base + (c1 - c0), g * 64:(g + 1) * 64] = fac[:, c0:c1].T

    cmask = np.tile((np.arange(CH)[:, None] <= np.arange(CH)[None, :])
                    .astype(ml_dtypes.bfloat16), (1, 4))
    ones64 = np.zeros((128, 128), ml_dtypes.bfloat16)
    ones64[64, :] = 1.0
    cmask = np.concatenate([cmask, ones64], axis=1)

    wqkv = np.concatenate([inp["L0_attn_Wq"], inp["L0_attn_Wk"],
                           inp["L0_attn_Wv"]], axis=1)
    wqkv_r = np.ascontiguousarray(
        wqkv.reshape(8, 128, 192).transpose(1, 0, 2).reshape(128, 8 * 192))
    b1r = np.ascontiguousarray(
        inp["L0_ffn_b1"].reshape(32, 128).T)

    return dict(
        c_wqkv=wqkv_r.astype(np.float32),
        c_wo=np.ascontiguousarray(inp["L0_attn_Wo"]).astype(np.float32),
        c_spvr=spvr, c_spul=spul, c_spvl=spvl, c_spleaf=spleaf,
        c_ulbd=ULbd, c_vrbd=VRbd,
        c_ulbdT=np.ascontiguousarray(ULbd.T), c_vrbdT=np.ascontiguousarray(VRbd.T),
        c_urbdT=np.ascontiguousarray(URbd.T), c_vlbdT=np.ascontiguousarray(VLbd.T),
        c_proj=proj, c_msk88=msk88, c_leafm=leafm, c_cmask=cmask,
        c_idn=np.eye(128, dtype=np.float32),
        c_b1r=b1r.astype(np.float32),
        c_b2=np.ascontiguousarray(inp["L0_ffn_b2"].reshape(1, 1024)).astype(np.float32),
    )


class Prog:
    ENGS = ("pe", "dv", "ac", "po", "sy")

    def __init__(self, nc, sems):
        self.nc = nc
        self.sems = sems
        self.cnt = defaultdict(int)
        self.hwm = defaultdict(lambda: defaultdict(int))
        self.ops = {e: [] for e in self.ENGS}
        self.bankR = defaultdict(list)
        self.bankW = defaultdict(list)

    SELF_SERIAL = ("dv", "ac", "po")
    strict = False
    strict_engines = ()

    def op(self, eng, emit, inc=None, amt=1, tiny=False, relax=False):
        if not RELAX_OK:
            relax = False
        strict_here = self.strict or (eng in self.strict_engines and not relax)
        if (strict_here or tiny) and eng in self.SELF_SERIAL and inc is None:
            inc = eng
        if ((strict_here or tiny) and eng in self.SELF_SERIAL and inc == eng
                and self.cnt[eng] > 0):
            self.wait(eng, (eng, self.cnt[eng]))
        if inc is not None:
            self.cnt[inc] += amt
            val = self.cnt[inc]
            sem = self.sems[inc]
            def f(e, emit=emit, sem=sem, amt=amt):
                emit(e).then_inc(sem, amt)
            self.ops[eng].append(f)
            return (inc, val)
        self.ops[eng].append(lambda e, emit=emit: emit(e))
        return None

    DMA_SEMS = ("dc", "dx0", "dx1", "dw", "dw1", "dw2", "ds")

    def wait(self, eng, ev):
        if ev is None:
            return
        name, val = ev
        if self.hwm[eng][name] >= val:
            return
        if name in self.DMA_SEMS:
            assert val == self.cnt[name], (
                f"partial DMA wait on {name}: {val} != {self.cnt[name]}")
        self.hwm[eng][name] = val
        sem = self.sems[name]
        self.ops[eng].append(lambda e, sem=sem, val=val: e.wait_ge(sem, val))

    def waits(self, eng, evs):
        best = {}
        for ev in evs:
            if ev is None:
                continue
            name, val = ev
            if name not in best or best[name] < val:
                best[name] = val
        for name, val in best.items():
            self.wait(eng, (name, val))

    def bank_write(self, eng, bank):
        for (e2, ev) in self.bankR[bank]:
            if e2 != eng:
                self.wait(eng, ev)
        for (e2, ev) in self.bankW[bank]:
            if e2 != eng:
                self.wait(eng, ev)
        self.bankR[bank] = []

    def bank_note_write(self, eng, bank, ev):
        self.bankW[bank] = [(eng, ev)]

    def bank_read(self, eng, bank):
        for (e2, ev) in self.bankW[bank]:
            if e2 != eng:
                self.wait(eng, ev)

    def bank_note_read(self, eng, bank, ev):
        self.bankR[bank].append((eng, ev))


def build_program(debug_taps=False, do_prep=True, nsuper=NCH // SUP, do_ffn=True, scan_parts=99):
    nc = bass.Bass("TRN2", target_bir_lowering=False, debug=False)

    x = nc.dram_tensor("x", [T, D], F32, kind="ExternalInput").ap()
    W1 = nc.dram_tensor("W1", [FF, D], F32, kind="ExternalInput").ap()
    W2 = nc.dram_tensor("W2", [D, FF], F32, kind="ExternalInput").ap()
    cons = {}
    cons["c_cmask"] = nc.dram_tensor("c_cmask", [128, 640], BF16,
                                     kind="ExternalInput").ap()
    for name, shape in [
        ("c_wqkv", [128, 8 * 192]), ("c_wo", [64, 1024]),
        ("c_spvr", [64, 512]), ("c_spul", [64, 512]), ("c_spvl", [64, 512]),
        ("c_spleaf", [64, 512]), ("c_ulbd", [64, 88]), ("c_vrbd", [64, 88]),
        ("c_ulbdT", [88, 64]), ("c_vrbdT", [88, 64]), ("c_urbdT", [88, 64]),
        ("c_vlbdT", [88, 64]), ("c_proj", [128, NROUND * 64]),
        ("c_msk88", [88, 88]), ("c_leafm", [64, 64]),
        ("c_idn", [128, 128]), ("c_b1r", [128, 32]), ("c_b2", [1, 1024]),
    ]:
        cons[name] = nc.dram_tensor(name, shape, F32, kind="ExternalInput").ap()
    out = nc.dram_tensor("out", [T, D], F32, kind="ExternalOutput").ap()
    taps = {}
    if debug_taps:
        for name, shape in [("dbg_qk", [128, 128]), ("dbg_vt", [64, 128]),
                            ("dbg_y", [64, 128]), ("dbg_den", [1, 128]),
                            ("dbg_ot", [64, 128]), ("dbg_attn", [128, 1024]),
                            ("dbg_x1", [128, 1024]), ("dbg_st", [64, 64]),
                            ("dbg_ht", [128, 256]), ("dbg_w1t", [128, 512]),
                            ("dbg_scale", [1, 2])]:
            taps[name] = nc.dram_tensor(name, shape, F32, kind="ExternalOutput").ap()

    st = contextlib.ExitStack()
    def sb(name, shape, dtype=F32):
        return st.enter_context(nc.sbuf_tensor(name, shape, dtype))
    def ps(name, shape, dtype=F32):
        return st.enter_context(nc.psum_tensor(name, shape, dtype))

    with st:
        sW1T = sb("sW1T", [128, 8 * FF], BF16)
        sW2T = sb("sW2T", [128, 32 * 1024], BF16)
        sHT = sb("sHT", [128, 32 * 256], BF16)
        sHT32 = sHT[:].bitcast(F32)
        sX = sb("sX", [128, 2 * 1024], F32)
        sXT = sb("sXT", [128, 1024], BF16)
        sXBF = sb("sXBF", [128, 1024], BF16)
        sR = sb("sR", [128, 1024], F32)
        sX1 = sb("sX1", [128, 3 * 1024], BF16)
        sX1T = sb("sX1T", [128, 8 * 256], BF16)
        sOUT = sb("sOUT", [128, 1024], F32)
        sQT = sb("sQT", [64, 128], BF16)
        sKT = sb("sKT", [64, 128], BF16)
        sVT = sb("sVT", [64, 128], BF16)
        sKV = sb("sKV", [128, 128], BF16)
        sKR = sb("sKR", [128, 128], BF16)
        sVR_ = sb("sVR_", [128, 128], BF16)
        ssp = {nm: sb("ssp_" + nm, [128, 512], BF16)
               for nm in ("vl", "ql", "v", "q", "k", "qb")}
        sCHAT = sb("sCHAT", [128, 8 * 128], BF16)
        sRDB = sb("sRDB", [1, 128], BF16)
        sJNK = sb("sJNK", [128, 512], BF16)
        sYSP = sb("sYSP", [128, 3 * 512], BF16)
        sOT = sb("sOT", [64, 128], BF16)
        sRD = sb("sRD", [64, 128], BF16)
        sST = sb("sST", [64, 64], F32)
        sSTB = sb("sSTB", [64, 64], BF16)
        sZ = sb("sZ", [64, 1], F32)
        sZB = sb("sZB", [64, 1], BF16)
        sG = sb("sG", [64, 64], BF16)
        sGT = sb("sGT", [64, 64], BF16)
        sT1 = sb("sT1", [64, 88], BF16)
        sT1b = sb("sT1b", [64, 88], BF16)
        sWm = sb("sWm", [88, 88], BF16)
        sWmT = sb("sWmT", [88, 88], BF16)
        sF1 = sb("sF1", [88, 64], BF16)
        sE1 = sb("sE1", [88, 64], BF16)
        sTMP = sb("sTMP", [128, 512], F32)
        sOR = sb("sOR", [1, 128], F32)
        sST1 = sb("sST1", [128, 20], F32)
        sST1I = sST1[:].bitcast(I32)
        sSC = sb("sSC", [4, 4], F32)
        sRS = sb("sRS", [128, 2], F32)
        sABS = sb("sABS", [128, 4], F32)

        sWQKV = sb("sWQKV", [128, 8 * 192], BF16)
        sWOB = sb("sWOB", [128, 1024], BF16)
        sSPc = {nm: sb("sSP_" + nm, [64, 512], BF16)
                for nm in ("vr", "ul", "vl", "leaf")}
        sULbd = sb("sULbd", [64, 88], BF16)
        sVRbd = sb("sVRbd", [64, 88], BF16)
        sULbdT = sb("sULbdT", [88, 64], BF16)
        sVRbdT = sb("sVRbdT", [88, 64], BF16)
        sURbdT = sb("sURbdT", [88, 64], BF16)
        sVLbdT = sb("sVLbdT", [88, 64], BF16)
        sPROJ = sb("sPROJ", [128, NROUND * 64], BF16)
        sMSK88 = sb("sMSK88", [88, 88], F32)
        sLEAFM = sb("sLEAFM", [64, 64], F32)
        sCMASK = sb("sCMASK", [128, 640], BF16)
        sIDN = sb("sIDN", [128, 128], BF16)
        sB1 = sb("sB1", [128, 32], F32)
        sONE = sb("sONE", [128, 1], BF16)
        sONER = sb("sONER", [1, 128], BF16)


        P1 = ps("P1", [128, 512], F32)
        P2 = ps("P2", [128, 512], F32)
        P3 = ps("P3", [128, 512], F32)
        P4 = ps("P4", [128, 512], F32)
        P5 = ps("P5", [128, 512], F32)
        P6 = ps("P6", [128, 512], F32)
        PT = ps("PT", [128, 1024], BF16)
        PS = ps("PS", [128, 512], F32)

        sems = {}
        for nm in ("pe", "dv", "ac", "po", "dc", "dx0", "dx1", "dw", "dw1", "dw2", "ds"):
            sems[nm] = st.enter_context(nc.semaphore(nm))
        b = Prog(nc, sems)
        b.strict = STRICT_ENGINE_ORDER
        b.strict_engines = STRICT_ENGINES

        const_plan = [
            ("c_wqkv", sWQKV, [128, 8 * 192]),
            ("c_wo", None, [64, 1024]),
            ("c_spvr", sSPc["vr"], [64, 512]),
            ("c_spul", sSPc["ul"], [64, 512]),
            ("c_spvl", sSPc["vl"], [64, 512]),
            ("c_spleaf", sSPc["leaf"], [64, 512]),
            ("c_ulbd", sULbd, [64, 88]),
            ("c_vrbd", sVRbd, [64, 88]),
            ("c_ulbdT", sULbdT, [88, 64]),
            ("c_vrbdT", sVRbdT, [88, 64]),
            ("c_urbdT", sURbdT, [88, 64]),
            ("c_vlbdT", sVLbdT, [88, 64]),
            ("c_proj", sPROJ, [128, NROUND * 64]),
            ("c_idn", sIDN, [128, 128]),
            ("c_b2", None, [1, 1024]),
        ]
        f32_plan = [("c_msk88", sMSK88), ("c_leafm", sLEAFM),
                    ("c_b1r", sB1)]
        bf_plan = [("c_cmask", sCMASK)]

        for name, dst in f32_plan + bf_plan:
            b.op("sy", lambda e, name=name, dst=dst:
                 e.dma_start(out=dst[:], in_=cons[name][:]), inc="dc", amt=16)
        cast_done = None
        batch, off = [], 0
        plan = list(const_plan)
        while plan:
            if plan and off + plan[0][2][1] <= 4096:
                name, dst, shp = plan.pop(0)
                ev = b.op("sy", lambda e, name=name, off=off, shp=shp:
                          e.dma_start(out=sHT32[0:shp[0], off:off + shp[1]],
                                      in_=cons[name][:]), inc="dc", amt=16)
                batch.append((name, dst, shp, off, ev))
                off += shp[1]
            if not plan or (plan and off + plan[0][2][1] > 4096):
                b.waits("dv", [bb[4] for bb in batch])
                for name, dst, shp, o2, _ in batch:
                    if dst is None:
                        dst = (sWOB[0:64, :] if name == "c_wo"
                               else sWOB[64:65, :])
                        shp = [shp[0], shp[1]]
                    cast_done = b.op("dv", lambda e, dst=dst, o2=o2, shp=shp:
                                     e.tensor_copy(
                                         out=dst[0:shp[0], 0:shp[1]],
                                         in_=sHT32[0:shp[0], o2:o2 + shp[1]]),
                                     inc="dv")
                if plan:
                    b.wait("sy", cast_done)
                batch, off = [], 0
        b.op("dv", lambda e: e.memset(sONE[:], 1.0))
        b.op("dv", lambda e: e.memset(sONER[:], 1.0))
        b.op("dv", lambda e: e.memset(sABS[:, 2:3], 1.0))
        cast_done = b.op("dv", lambda e: e.memset(sOR[:], 1.0),
                         inc="dv")
        b.op("dv", lambda e: e.memset(sKR[:], 0.0))
        b.op("dv", lambda e: e.memset(sVR_[:], 0.0))
        b.op("dv", lambda e: e.memset(sST[:], 0.0))
        b.op("dv", lambda e: e.memset(sSTB[:], 0.0))
        b.op("dv", lambda e: e.memset(sZ[:], 0.0))
        state_init = b.op("dv", lambda e: e.memset(sZB[:], 0.0), inc="dv")

        b.op("dv", lambda e: e.memset(sABS[:, 0:2], 0.0))
        absinit = b.op("dv", lambda e: e.memset(sABS[:, 3:4], 0.0), inc="dv")
        b.wait("ac", cast_done)
        b.wait("ac", absinit)
        b.wait("po", cast_done)

        W1r = W1.rearrange("(s p) d -> s p d", p=128)
        W2r = W2.rearrange("(s p) f -> s p f", p=128)

        PT6 = P6[:].bitcast(BF16)
        prep_out = {}

        def prep_pass1():
            items = []
            for s in range(0, 32, 2):
                items.append((0, s))
            for s in range(8):
                for h in range(2):
                    items.append((1, (s, h)))
            loadev = [None] * len(items)
            absev = [None] * len(items)
            addev = absinit
            for i, (kind, key) in enumerate(items):
                buf = (i % 2) * 2048
                if i >= 2:
                    b.wait("po", absev[i - 2])
                if kind == 0:
                    def em(e, s=key, buf=buf):
                        return e.dma_start(
                            out=sHT32[:, buf:buf + 2048].rearrange(
                                "p (s d) -> p s d", s=2),
                            in_=W1r[s:s + 2].rearrange("s p d -> p s d"))
                else:
                    s, h = key
                    def em(e, s=s, h=h, buf=buf):
                        return e.dma_start(
                            out=sHT32[:, buf:buf + 2048],
                            in_=W2r[s][:, 2048 * h:2048 * (h + 1)])
                loadev[i] = b.op("po", em, inc="dw" if i % 2 == 0 else "dw1",
                                 amt=16)
                b.wait("ac", loadev[i])
                b.wait("ac", addev)
                absev[i] = b.op("ac", lambda e, buf=buf: e.activation(
                    out=sW2T[:, 0:2048], in_=sHT32[:, buf:buf + 2048],
                    func=AF.Abs, accum_out=sABS[:, 3:4]), inc="ac")
                col = 0 if kind == 0 else 1
                b.wait("dv", absev[i])
                addev = b.op("dv", lambda e, col=col: e.tensor_tensor(
                    out=sABS[:, col:col + 1], in0=sABS[:, col:col + 1],
                    in1=sABS[:, 3:4], op=ALU.add), inc="dv", tiny=True)
                yield
            prep_out["p1"] = addev

        def prep_scale():
            b.waits("pe", [prep_out.get("p1", absinit)])
            b.bank_write("pe", "PS")
            mev = b.op("pe", lambda e: e.matmul(PS[0:1, 0:2], sABS[:, 2:3], sABS[:, 0:2],
                                            start=True, stop=True), inc="pe")
            b.bank_note_write("pe", "PS", mev)
            b.wait("dv", mev)
            b.bank_read("dv", "PS")
            b.op("dv", lambda e: e.tensor_scalar(
            out=sSC[0:1, 0:2], in0=PS[0:1, 0:2], scalar1=1.0 / (FF * D),
            scalar2=1e-8, op0=ALU.mult, op1=ALU.max), tiny=True)
            rev = b.op("dv", lambda e: e.reciprocal(out=sSC[0:1, 2:4],
                                                in_=sSC[0:1, 0:2]),
                       inc="dv", tiny=True)
            b.bank_note_read("dv", "PS", rev)
            if debug_taps:
                b.wait("sy", rev)
                b.op("sy", lambda e: e.dma_start(
                    out=taps["dbg_scale"][:], in_=sSC[0:1, 2:4]),
                    inc="ds", amt=16)
            b.wait("pe", rev)
            b.bank_write("pe", "PS")
            mev = b.op("pe", lambda e: e.matmul(PS[:, 0:2], sOR[:],
                                            sSC[0:1, 2:4], start=True, stop=True),
                   inc="pe")
            b.bank_note_write("pe", "PS", mev)
            b.wait("dv", mev)
            b.bank_read("dv", "PS")
            rsev = b.op("dv", lambda e: e.tensor_copy(out=sRS[:], in_=PS[:, 0:2]),
                        inc="dv")
            b.bank_note_read("dv", "PS", rsev)
            prep_out["rsev"] = rsev


        tern_tmp = sHT32[:, 2048:3072]
        tern_bf = sX1T

        def prep_pass2():
            b.wait("po", prep_out["rsev"])
            units = [("w1", s) for s in range(32)]
            units += [("w2", (s, q)) for q in range(4) for s in range(8)]
            e2s = [None] * len(units)
            e3s = [None] * len(units)
            tevs = [None] * len(units)
            last_evac = None
            BUFS = (0, 1024, 3072)
            for i, (kind, key) in enumerate(units):
                buf = BUFS[i % 3]
                rcol = 0 if kind == "w1" else 1
                if i >= 3:
                    b.wait("po", e2s[i - 3])
                if kind == "w1":
                    def em(e, s=key, buf=buf):
                        return e.dma_start(out=sHT32[:, buf:buf + 1024],
                                           in_=W1r[s])
                else:
                    s, q = key
                    def em(e, s=s, q=q, buf=buf):
                        return e.dma_start(out=sHT32[:, buf:buf + 1024],
                                           in_=W2r[s][:, 1024 * q:1024 * (q + 1)])
                lw = b.op("po", em, inc=("dw", "dw1", "dw2")[i % 3], amt=16)
                obuf = (i % 2) * 1024
                b.wait("dv", lw)
                if i == 0:
                    b.wait("dv", prep_out["rsev"])
                e1 = b.op("dv", lambda e, buf=buf, rcol=rcol: e.tensor_scalar(
                    out=tern_tmp[:, 0:1024],
                    in0=sHT32[:, buf:buf + 1024],
                    scalar1=sRS[:, rcol:rcol + 1], scalar2=-0.5,
                    op0=ALU.mult, op1=ALU.is_le), inc="dv", relax=True)
                if i >= 2:
                    b.wait("dv", tevs[i - 2])
                e2s[i] = b.op("dv", lambda e, buf=buf, obuf=obuf, rcol=rcol:
                              e.tensor_scalar(
                    out=tern_bf[:, obuf:obuf + 1024],
                    in0=sHT32[:, buf:buf + 1024],
                    scalar1=sRS[:, rcol:rcol + 1], scalar2=0.5,
                    op0=ALU.mult, op1=ALU.is_ge), inc="dv", relax=True)
                e3s[i] = b.op("dv", lambda e, obuf=obuf: e.tensor_tensor(
                    out=tern_bf[:, obuf:obuf + 1024],
                    in0=tern_bf[:, obuf:obuf + 1024],
                    in1=tern_tmp[:, 0:1024],
                    op=ALU.subtract), inc="dv", relax=True)
                b.wait("pe", e3s[i])
                b.bank_write("pe", "P6")
                tev = None
                for j in range(8):
                    tev = b.op("pe", lambda e, j=j, obuf=obuf: e.transpose(
                        PT6[:, j * 128:(j + 1) * 128],
                        tern_bf[:, obuf + j * 128:obuf + (j + 1) * 128], sIDN[:]),
                        inc="pe" if j == 7 else None)
                tevs[i] = tev
                b.bank_note_write("pe", "P6", tev)
                b.wait("ac", tev)
                b.bank_read("ac", "P6")
                if kind == "w1":
                    dst = sW1T[:].rearrange("p (j f) -> p j f", j=8)
                    dsl = dst[:, :, 128 * key:128 * key + 128]
                else:
                    s, q = key
                    dst = sW2T[:].rearrange("p (f d) -> p f d", f=32)
                    dsl = dst[:, 8 * q:8 * q + 8, 128 * s:128 * s + 128]
                last_evac = b.op("ac", lambda e, dsl=dsl: e.copy(
                    out=dsl, in_=PT6.rearrange("p (j f) -> p j f", j=8)),
                    inc="ac")
                b.bank_note_read("ac", "P6", last_evac)
                if kind == "w1":
                    prep_out["w1ev"][key] = last_evac
                else:
                    s, q = key
                    if s == 7:
                        prep_out["w2ev"][q] = last_evac
                yield
            prep_out["w1"] = last_evac
            prep_out["w2"] = last_evac

        def prep_gen():
            prep_out["w1ev"] = [None] * 32
            prep_out["w2ev"] = [None] * 4
            if do_prep:
                for _ in prep_pass1():
                    yield
            prep_scale()
            if do_prep:
                for _ in prep_pass2():
                    yield
            else:
                prep_out["w1"] = prep_out["w2"] = prep_out["rsev"]
                prep_out["w1ev"] = [prep_out["rsev"]] * 32
                prep_out["w2ev"] = [prep_out["rsev"]] * 4

        xin = [None, None]
        xin[0] = b.op("sy", lambda e: e.dma_start(
            out=sX[:, 0:1024], in_=x[0:128, :]), inc="dx0", amt=16)

        sx1_consumed = [None] * 3
        sx1_chunk = [-1] * 3
        gen_pool = []
        pool_hooks = {"refill": lambda: None, "done": lambda g: None}
        def advance_pool():
            for g_ in list(gen_pool):
                try:
                    next(g_)
                except StopIteration:
                    gen_pool.remove(g_)
                    pool_hooks["done"](g_)
                    pool_hooks["refill"]()
            if not gen_pool:
                pool_hooks["refill"]()
        sR_last_ac = [None]
        dbg_last = [None]
        sout_stored = [None]
        x_consumed = [None, None]
        xcast_ev = [None, None]
        x1t_free = [None]
        pg_holder = [None]
        zb_ev = [None]
        chat_ring_ev = [None] * 8
        stb_ev = [None]
        chain_gens = []
        chunk_x1 = {}

        def scan_chunk(c):
            guard = 0
            while c >= 3 and sx1_chunk[c % 3] < c - 3:
                advance_pool()
                guard += 1
                assert guard < 100000, "x1 ring consumer never emitted"
            pm, PM = ("P5", P5) if c % 2 == 0 else ("PS", PS)
            zb_prev = zb_ev[0]
            buf = c % 2
            nbuf = (c + 1) % 2
            if c + 1 < NCH:
                b.wait("sy", x_consumed[nbuf])
                xin[nbuf] = b.op("sy", lambda e, c=c, nbuf=nbuf: e.dma_start(
                    out=sX[:, nbuf * 1024:nbuf * 1024 + 1024],
                    in_=x[(c + 1) * 128:(c + 2) * 128, :]),
                    inc="dx0" if nbuf == 0 else "dx1", amt=16)

            xs = sX[:, buf * 1024:(buf + 1) * 1024]
            if c == 0:
                b.wait("ac", xin[buf])
                xcast_ev[buf] = b.op("ac", lambda e, xs=xs: e.copy(
                    out=sXBF[:], in_=xs), inc="ac")
            b.wait("pe", xcast_ev[buf])
            b.bank_write("pe", "PT")
            tev = None
            for j in range(8):
                tev = b.op("pe", lambda e, j=j: e.transpose(
                    PT[:, j * 128:(j + 1) * 128],
                    sXBF[:, j * 128:(j + 1) * 128], sIDN[:]),
                    inc="pe" if j == 7 else None)
            b.bank_note_write("pe", "PT", tev)
            b.wait("ac", tev)
            b.bank_read("ac", "PT")
            xtev = b.op("ac", lambda e: e.copy(out=sXT[:], in_=PT[:]), inc="ac")
            b.bank_note_read("ac", "PT", xtev)
            if c + 1 < NCH:
                nxs = sX[:, nbuf * 1024:(nbuf + 1) * 1024]
                b.wait("ac", xin[nbuf])
                xcast_ev[nbuf] = b.op("ac", lambda e, nxs=nxs: e.copy(
                    out=sXBF[:], in_=nxs), inc="ac")
            yield

            b.wait("pe", xtev)
            b.bank_write("pe", pm)
            qkvev = None
            for which, (r0, c0, wcol, w) in enumerate(
                    [(0, 0, 0, 128), (0, 128, 128, 64)]):
                for j in range(8):
                    qkvev = b.op("pe", lambda e, r0=r0, c0=c0, wcol=wcol,
                                 w=w, j=j: e.matmul(
                        PM[r0:r0 + w, c0:c0 + 128],
                        sWQKV[:, j * 192 + wcol:j * 192 + wcol + w],
                        sXT[:, j * 128:(j + 1) * 128],
                        start=(j == 0), stop=(j == 7)),
                        inc="pe" if (which == 1 and j == 7) else None)
            b.bank_note_write("pe", pm, qkvev)
            yield

            b.wait("dv", qkvev)
            b.bank_read("dv", pm)
            b.op("dv", lambda e: e.tensor_scalar(
                out=sTMP[:, 0:128], in0=PM[:, 0:128], scalar1=0.0, scalar2=None,
                op0=ALU.min), relax=True)
            uev = b.op("dv", lambda e: e.tensor_copy(out=sVT[:], in_=PM[0:64, 128:256]),
                       inc="dv", relax=True)
            b.bank_note_read("dv", pm, uev)
            b.wait("ac", uev)
            b.bank_read("ac", pm)
            thev = b.op("ac", lambda e: e.activation(
                out=sTMP[:, 128:256], in_=sTMP[:, 0:128], func=AF.Tanh, scale=0.5),
                inc="ac")
            rlev = b.op("ac", lambda e: e.activation(
                out=sTMP[:, 256:384], in_=PM[:, 0:128], func=AF.Relu), inc="ac")
            b.bank_note_read("ac", pm, rlev)
            b.waits("dv", [thev, rlev])
            b.op("dv", lambda e: e.tensor_scalar(
                out=sTMP[:, 384:512], in0=sTMP[:, 128:256], scalar1=-1.0,
                scalar2=1.0, op0=ALU.mult, op1=ALU.add), relax=True)
            b.op("dv", lambda e: e.reciprocal(out=sTMP[:, 384:512],
                                              in_=sTMP[:, 384:512]), relax=True)
            b.op("dv", lambda e: e.tensor_scalar(
                out=sTMP[:, 0:128], in0=sTMP[:, 128:256], scalar1=1.0,
                scalar2=None, op0=ALU.add), relax=True)
            b.op("dv", lambda e: e.tensor_tensor(
                out=sTMP[:, 0:128], in0=sTMP[:, 0:128], in1=sTMP[:, 384:512],
                op=ALU.mult), relax=True)
            b.op("dv", lambda e: e.tensor_tensor(
                out=sQT[:], in0=sTMP[0:64, 0:128], in1=sTMP[0:64, 256:384],
                op=ALU.add), relax=True)
            qkev = b.op("dv", lambda e: e.tensor_tensor(
                out=sKT[:], in0=sTMP[64:128, 0:128], in1=sTMP[64:128, 256:384],
                op=ALU.add), inc="dv", relax=True)
            yield

            b.wait("pe", qkev)
            b.bank_write("pe", "PT")
            b.op("pe", lambda e: e.transpose(PT[:, 0:64], sKT[:],
                                             sIDN[0:64, 0:64]))
            kvtev = b.op("pe", lambda e: e.transpose(PT[:, 64:128], sVT[:],
                                                     sIDN[0:64, 0:64]), inc="pe")
            b.bank_note_write("pe", "PT", kvtev)
            b.wait("ac", kvtev)
            b.bank_read("ac", "PT")
            kvev = b.op("ac", lambda e: e.copy(out=sKV[:], in_=PT[:, 0:128]),
                        inc="ac")
            b.bank_note_read("ac", "PT", kvev)

            b.bank_write("pe", pm)
            b.op("pe", lambda e: e.matmul(PM[:, 256:344], sKT[:],
                                          sULbd[0:64, :], start=True, stop=True))
            b.op("pe", lambda e: e.matmul(PM[:, 344:432], sVT[:], sVRbd[0:64, :],
                                          start=True, stop=True))
            chzev = b.op("pe", lambda e: e.matmul(
                PM[:, 0:128], sKT[:], sQT[:], start=True, stop=True), inc="pe")
            b.bank_note_write("pe", pm, chzev)
            b.wait("ac", chzev)
            b.bank_read("ac", pm)
            b.op("ac", lambda e: e.copy(out=sKR[:, 0:88], in_=PM[:, 256:344]))
            krev = b.op("ac", lambda e: e.copy(out=sVR_[:, 0:88], in_=PM[:, 344:432]),
                        inc="ac")
            b.bank_note_read("ac", pm, krev)
            b.wait("dv", krev)
            b.bank_read("dv", pm)
            chzm = b.op("dv", lambda e: e.tensor_tensor(
                out=sCHAT[:, 7 * 128:8 * 128], in0=PM[:, 0:128],
                in1=sCMASK[:, 0:128], op=ALU.mult), inc="dv", relax=True)
            b.bank_note_read("dv", pm, chzm)
            yield

            yield
            b.wait("pe", chzm)
            b.wait("pe", zb_prev)
            b.bank_write("pe", pm)
            b.op("pe", lambda e: e.matmul(PM[0:1, 128:256], sONE[:],
                                          sCHAT[:, 7 * 128:8 * 128],
                                          start=True, stop=False))
            denev = b.op("pe", lambda e: e.matmul(
                PM[0:1, 128:256], sZB[:], sQT[:], start=False, stop=True),
                inc="pe")
            b.bank_note_write("pe", pm, denev)
            b.wait("dv", denev)
            b.bank_read("dv", pm)
            b.op("dv", lambda e: e.tensor_scalar(
                out=sTMP[0:1, 0:128], in0=PM[0:1, 128:256], scalar1=1e-6,
                scalar2=None, op0=ALU.max))
            rdev = b.op("dv", lambda e: e.reciprocal(out=sTMP[0:1, 0:128],
                                                     in_=sTMP[0:1, 0:128]), inc="dv")
            rdbf = b.op("dv", lambda e: e.tensor_copy(
                out=sRDB[:], in_=sTMP[0:1, 0:128]), inc="dv")
            b.bank_note_read("dv", pm, rdbf)
            if debug_taps and c == 0:
                b.wait("dv", dbg_last[0])
                dcp = b.op("dv", lambda e: e.tensor_copy(
                    out=sOUT[0:1, 0:128], in_=sTMP[0:1, 0:128]))
                b.wait("sy", dcp)
                dbg_last[0] = b.op("sy", lambda e: e.dma_start(
                    out=taps["dbg_den"][:], in_=sOUT[0:1, 0:128]),
                    inc="ds", amt=16)

            yield
            b.wait("pe", rdbf)
            b.bank_write("pe", pm)
            rdbev = b.op("pe", lambda e: e.matmul(
                PM[0:64, 128:256], sONER[0:1, 0:64], sRDB[:],
                start=True, stop=True), inc="pe")
            b.bank_note_write("pe", pm, rdbev)
            b.wait("dv", rdbev)
            b.bank_read("dv", pm)
            rdcp = b.op("dv", lambda e: e.tensor_copy(out=sRD[:],
                                                      in_=PM[0:64, 128:256]),
                        inc="dv", relax=True)
            b.bank_note_read("dv", pm, rdcp)
            yield

            b.wait("pe", kvev)
            b.bank_write("pe", pm)
            b.op("pe", lambda e: e.matmul(PM[0:64, 0:64], sKV[:, 0:64],
                                          sKV[:, 64:128], start=True, stop=True))
            gev = b.op("pe", lambda e: e.matmul(
                PM[0:64, 64:128], sKV[:, 64:128], sKV[:, 0:64],
                start=True, stop=True), inc="pe")
            b.bank_note_write("pe", pm, gev)
            b.wait("dv", gev)
            b.bank_read("dv", pm)
            b.op("dv", lambda e: e.tensor_copy(out=sG[:], in_=PM[0:64, 0:64]),
                 relax=True)
            gtev = b.op("dv", lambda e: e.tensor_copy(out=sGT[:],
                                                      in_=PM[0:64, 64:128]),
                        inc="dv", relax=True)
            b.op("dv", lambda e: e.tensor_tensor(
                out=sTMP[0:64, 0:64], in0=PM[0:64, 64:128], in1=sLEAFM[:],
                op=ALU.mult), tiny=True)
            stleaf = b.op("dv", lambda e: e.tensor_tensor(
                out=sST[:], in0=sST[:], in1=sTMP[0:64, 0:64], op=ALU.add),
                inc="dv", tiny=True)
            b.bank_note_read("dv", pm, stleaf)
            b.op("dv", lambda e: e.tensor_reduce(
                out=sTMP[0:64, 64:65], in_=sKT[:], axis=mybir.AxisListType.X,
                op=ALU.add), tiny=True)
            zev = b.op("dv", lambda e: e.tensor_tensor(
                out=sZ[:], in0=sZ[:], in1=sTMP[0:64, 64:65], op=ALU.add),
                inc="dv", tiny=True)
            zbev = b.op("dv", lambda e: e.tensor_copy(out=sZB[:], in_=sZ[:]),
                        inc="dv", tiny=True)
            zb_ev[0] = zbev
            yield

            b.wait("pe", qkev)
            spev = {}
            srcs = dict(vl=(sSPc["leaf"], sVT[0:64, :]),
                        ql=(sSPc["leaf"], sQT[:]),
                        v=(sSPc["vr"], sVT[0:64, :]),
                        q=(sSPc["vr"], sQT[:]),
                        k=(sSPc["ul"], sKT[:]),
                        qb=(sSPc["vl"], sQT[:]))
            names = list(srcs.keys())
            for idx, nm in enumerate(names):
                pk = "P1" if idx % 2 == 0 else "P2"
                P = P1 if idx % 2 == 0 else P2
                con, src_ = srcs[nm]
                b.bank_write("pe", pk)
                ev = None
                for tl in range(4):
                    ev = b.op("pe", lambda e, con=con, src_=src_, tl=tl, P=P:
                              e.matmul(P[:, tl * 128:(tl + 1) * 128],
                                       con[:, tl * 128:(tl + 1) * 128], src_,
                                       start=True, stop=True),
                              inc="pe" if tl == 3 else None)
                b.bank_note_write("pe", pk, ev)
                b.wait("ac", ev)
                b.bank_read("ac", pk)
                sev = b.op("ac", lambda e, nm=nm, P=P: e.copy(
                    out=ssp[nm][:], in_=P[:]), inc="ac")
                b.bank_note_read("ac", pk, sev)
                spev[nm] = sev
                if idx % 2 == 1:
                    yield

            ysp_evts = []
            ysp_sb_slices = []
            chat_ring_ev[7] = denev

            def emit_coeff(g):
                terms = ROUNDS[g]
                pk = "P1" if g % 2 == 0 else "P2"
                P = P1 if g % 2 == 0 else P2
                b.waits("pe", [spev[t_side] for t_side in
                               ({"leaf": ("vl", "ql"), "top": ("v", "q"),
                                 "bot": ("k", "qb")}[terms[0]["kind"]])])
                b.bank_write("pe", pk)
                ev = None
                for i, t_ in enumerate(terms):
                    lhs_nm, rhs_nm = {"leaf": ("vl", "ql"), "top": ("v", "q"),
                                      "bot": ("k", "qb")}[t_["kind"]]
                    tl, ba, K = t_["tile"], t_["base"], t_["K"]
                    def em(e, lhs_nm=lhs_nm, rhs_nm=rhs_nm, tl=tl, ba=ba, K=K,
                           i=i, P=P):
                        return e.matmul(
                            P[:, i * 128:(i + 1) * 128],
                            ssp[lhs_nm][ba:ba + K, tl * 128:(tl + 1) * 128],
                            ssp[rhs_nm][ba:ba + K, tl * 128:(tl + 1) * 128],
                            start=True, stop=True,
                            tile_position=(ba, 0))
                    ev = b.op("pe", em, inc="pe" if i == len(terms) - 1 else None)
                b.bank_note_write("pe", pk, ev)
                return ev

            def emit_mask(g, ev):
                terms = ROUNDS[g]
                nt = len(terms)
                pk = "P1" if g % 2 == 0 else "P2"
                P = P1 if g % 2 == 0 else P2
                b.wait("dv", ev)
                b.bank_read("dv", pk)
                slot = (g * 4) % 8
                for i in range(nt):
                    b.wait("dv", chat_ring_ev[(slot + i) % 8])
                meva = b.op("dv", lambda e, slot=slot, P=P: e.tensor_tensor(
                    out=sCHAT[:, slot * 128:(slot + 2) * 128],
                    in0=P[:, 0:256],
                    in1=sCMASK[:, 0:256],
                    op=ALU.mult), inc="dv", relax=True)
                mevb = b.op("dv", lambda e, nt=nt, slot=slot, P=P: e.tensor_tensor(
                    out=sCHAT[:, (slot + 2) * 128:(slot + nt) * 128],
                    in0=P[:, 256:nt * 128],
                    in1=sCMASK[:, 256:nt * 128],
                    op=ALU.mult), inc="dv", relax=True)
                b.bank_note_read("dv", pk, mevb)
                return (meva, mevb)

            def emit_apply(g, mev2):
                terms = ROUNDS[g]
                nt = len(terms)
                slot = (g * 4) % 8
                ypos = g % 4
                b.wait("pe", krev)
                b.wait("pe", mev2[0])
                if ypos == 0:
                    b.bank_write("pe", "P3")
                aev = None
                for i, t_ in enumerate(terms):
                    if i == 2:
                        b.wait("pe", mev2[1])
                    if t_["kind"] == "leaf":
                        lhs = sKV[:, 4 * t_["n"]:4 * t_["n"] + 32]
                    elif t_["kind"] == "top":
                        lhs = sKR[:, t_["c0"]:t_["c0"] + 32]
                    else:
                        lhs = sVR_[:, t_["c0"]:t_["c0"] + 32]
                    def em(e, lhs=lhs, i=i, slot=slot, ypos=ypos):
                        return e.matmul(
                            P3[32 * i:32 * i + 32, ypos * 128:(ypos + 1) * 128],
                            lhs, sCHAT[:, (slot + i) * 128:(slot + i + 1) * 128],
                            start=True, stop=True, tile_position=(0, 32 * i))
                    aev = b.op("pe", em, inc="pe" if (i == nt - 1 and nt == 4)
                               else None)
                if nt == 3:
                    def em(e, slot=slot, ypos=ypos):
                        return e.matmul(
                            P3[96:128, ypos * 128:(ypos + 1) * 128],
                            sKR[:, 96:128],
                            sCHAT[:, slot * 128:(slot + 1) * 128],
                            start=True, stop=True, tile_position=(0, 96))
                    aev = b.op("pe", em, inc="pe")
                for i in range(nt):
                    chat_ring_ev[(slot + i) % 8] = aev
                if ypos == 3 or g == NROUND - 1:
                    b.bank_note_write("pe", "P3", aev)
                    third = g // 4
                    b.wait("ac", aev)
                    b.bank_read("ac", "P3")
                    yev = b.op("ac", lambda e, third=third: e.copy(
                        out=sYSP[:, third * 512:(third + 1) * 512], in_=P3[:]),
                        inc="ac")
                    b.bank_note_read("ac", "P3", yev)
                    ysp_evts.append(yev)
                    for gg in range(g - ypos, g + 1):
                        ysp_sb_slices.append(
                            (sYSP[:, third * 512 + (gg % 4) * 128:
                                  third * 512 + (gg % 4 + 1) * 128], gg))

            cev = {0: emit_coeff(0)}
            for g in range(NROUND):
                if g + 1 < NROUND:
                    cev[g + 1] = emit_coeff(g + 1)
                mev2 = emit_mask(g, cev[g])
                emit_apply(g, mev2)
                yield

            while chain_gens:
                g0 = chain_gens[0]
                try:
                    next(g0)
                except StopIteration:
                    chain_gens.pop(0)
            b.wait("pe", stb_ev[0])
            b.waits("pe", ysp_evts)
            b.waits("pe", [stleaf, rdcp])
            b.bank_write("pe", pm)
            b.op("pe", lambda e: e.matmul(PM[0:64, 384:512], sSTB[:],
                                          sQT[:], start=True, stop=False))
            yev = None
            for (ysl, gg) in ysp_sb_slices:
                yev = b.op("pe", lambda e, ysl=ysl, gg=gg: e.matmul(
                    PM[0:64, 384:512], sPROJ[:, gg * 64:(gg + 1) * 64], ysl,
                    start=False, stop=(gg == NROUND - 1)),
                    inc="pe" if gg == NROUND - 1 else None)
            b.bank_note_write("pe", pm, yev)
            if debug_taps and c == 0:
                b.wait("dv", yev)
                b.bank_read("dv", pm)
                b.wait("dv", dbg_last[0])
                dvy = b.op("dv", lambda e: e.tensor_copy(out=sOUT[0:64, 0:128],
                                                         in_=PM[0:64, 384:512]),
                           inc="dv")
                b.bank_note_read("dv", pm, dvy)
                b.wait("sy", dvy)
                dbg_last[0] = b.op("sy", lambda e: e.dma_start(
                    out=taps["dbg_y"][:], in_=sOUT[0:64, 0:128]),
                    inc="ds", amt=16)

            b.wait("dv", yev)
            b.wait("dv", rdcp)
            b.bank_read("dv", pm)
            otev = b.op("dv", lambda e: e.tensor_tensor(
                out=sOT[:], in0=PM[0:64, 384:512], in1=sRD[:], op=ALU.mult),
                inc="dv", relax=True)
            b.bank_note_read("dv", pm, otev)
            yield

            yield
            b.wait("pe", otev)
            b.bank_write("pe", pm)
            aaev = b.op("pe", lambda e: e.matmul(
                PM[:, 0:512], sOT[:], sWOB[0:64, 0:512], start=True, stop=True),
                inc="pe")
            b.bank_note_write("pe", pm, aaev)
            b.waits("dv", [aaev])
            b.wait("dv", sR_last_ac[0])
            b.bank_read("dv", pm)
            r1a = b.op("dv", lambda e, xs=xs: e.scalar_tensor_tensor(
                out=sR[:, 0:512], in0=xs[:, 0:512], scalar=1.0, in1=PM[:, 0:512],
                op0=ALU.mult, op1=ALU.add, accum_out=sST1[:, 0:1]), inc="dv",
                relax=True)
            b.bank_note_read("dv", pm, r1a)
            yield
            b.wait("pe", r1a)
            b.bank_write("pe", pm)
            abev = b.op("pe", lambda e: e.matmul(
                PM[:, 0:512], sOT[:], sWOB[0:64, 512:1024], start=True, stop=True),
                inc="pe")
            b.bank_note_write("pe", pm, abev)
            b.wait("dv", abev)
            b.bank_read("dv", pm)
            r1ev = b.op("dv", lambda e, xs=xs: e.scalar_tensor_tensor(
                out=sR[:, 512:1024], in0=xs[:, 512:1024], scalar=1.0,
                in1=PM[:, 0:512], op0=ALU.mult, op1=ALU.add,
                accum_out=sST1[:, 1:2]), inc="dv", relax=True)
            b.bank_note_read("dv", pm, r1ev)
            x_consumed[buf] = r1ev
            if debug_taps and c == 0:
                b.wait("dv", dbg_last[0])
                dra = b.op("dv", lambda e: e.tensor_copy(
                    out=sOUT[:, 0:1024], in_=sR[:]))
                b.wait("sy", dra)
                dbg_last[0] = b.op("sy", lambda e: e.dma_start(
                    out=taps["dbg_attn"][:], in_=sOUT[:, 0:1024]),
                    inc="ds", amt=16)
            b.wait("ac", r1ev)
            sq1 = b.op("ac", lambda e: e.activation(
                out=sJNK[:], in_=sR[:, 0:512],
                func=AF.Square, accum_out=sST1[:, 2:3]), inc="ac")
            b.wait("ac", sq1)
            sqev = b.op("ac", lambda e: e.activation(
                out=sJNK[:], in_=sR[:, 512:1024],
                func=AF.Square, accum_out=sST1[:, 8:9]), inc="ac")
            sR_last_ac[0] = sqev
            b.wait("dv", sqev)
            b.op("dv", lambda e: e.tensor_tensor(
                out=sST1[:, 2:3], in0=sST1[:, 2:3], in1=sST1[:, 8:9], op=ALU.add), tiny=True)
            b.op("dv", lambda e: e.tensor_tensor(
                out=sST1[:, 0:1], in0=sST1[:, 0:1], in1=sST1[:, 1:2], op=ALU.add), tiny=True)
            b.op("dv", lambda e: e.tensor_scalar(
                out=sST1[:, 3:4], in0=sST1[:, 0:1], scalar1=1.0 / D, scalar2=None,
                op0=ALU.mult), tiny=True)
            b.op("dv", lambda e: e.tensor_tensor(
                out=sST1[:, 4:5], in0=sST1[:, 3:4], in1=sST1[:, 3:4], op=ALU.mult), tiny=True)
            b.op("dv", lambda e: e.tensor_scalar(
                out=sST1[:, 2:3], in0=sST1[:, 2:3], scalar1=1.0 / D,
                scalar2=None, op0=ALU.mult), tiny=True)
            b.op("dv", lambda e: e.tensor_tensor(
                out=sST1[:, 4:5], in0=sST1[:, 2:3], in1=sST1[:, 4:5],
                op=ALU.subtract), tiny=True)
            b.op("dv", lambda e: e.tensor_scalar(
                out=sST1[:, 4:5], in0=sST1[:, 4:5], scalar1=1e-5, scalar2=None,
                op0=ALU.add), tiny=True)
            _rsqrt(b, sST1, sST1I, 4, 5)
            b.op("dv", lambda e: e.tensor_tensor(
                out=sST1[:, 6:7], in0=sST1[:, 3:4], in1=sST1[:, 5:6], op=ALU.mult), tiny=True)
            b.op("dv", lambda e: e.tensor_scalar(
                out=sST1[:, 6:7], in0=sST1[:, 6:7], scalar1=-1.0, scalar2=None,
                op0=ALU.mult), tiny=True)
            b.wait("dv", sx1_consumed[c % 3])
            x1ev = b.op("dv", lambda e, c=c: e.tensor_scalar(
                out=sX1[:, (c % 3) * 1024:(c % 3 + 1) * 1024], in0=sR[:],
                scalar1=sST1[:, 5:6], scalar2=sST1[:, 6:7],
                op0=ALU.mult, op1=ALU.add), inc="dv")
            chunk_x1[c] = x1ev
            if debug_taps and c == 0:
                b.wait("dv", dbg_last[0])
                dx1 = b.op("dv", lambda e: e.tensor_copy(out=sOUT[:, 0:512],
                                                         in_=sX1[:, 0:512]))
                b.wait("sy", dx1)
                dbg_last[0] = b.op("sy", lambda e: e.dma_start(
                    out=taps["dbg_x1"][:, 0:512], in_=sOUT[:, 0:512]),
                    inc="ds", amt=16)
            yield

            def chain_gen(pm=pm, PM=PM, gtev=gtev, zev=zev, c=c):
                def chain_mm(region, lhsT, rhs, dst=None, mask=None):
                    b.bank_write("pe", pm)
                    e1 = b.op("pe", lambda e, region=region, lhsT=lhsT, rhs=rhs:
                              e.matmul(region, lhsT, rhs, start=True, stop=True),
                              inc="pe")
                    b.bank_note_write("pe", pm, e1)
                    b.wait("dv", e1)
                    b.bank_read("dv", pm)
                    if mask is None:
                        c1 = b.op("dv", lambda e, dst=dst, region=region:
                                  e.tensor_copy(out=dst, in_=region), inc="dv",
                                  relax=True)
                    else:
                        c1 = b.op("dv", lambda e, dst=dst, region=region, mask=mask:
                                  e.tensor_tensor(out=dst, in0=region, in1=mask,
                                                  op=ALU.mult), inc="dv",
                                  relax=True)
                    b.bank_note_read("dv", pm, c1)
                    return c1

                b.wait("pe", gtev)
                chain_mm(PM[0:64, 0:88], sG[:], sULbd[0:64, :], dst=sT1[:])
                yield
                chain_mm(PM[0:64, 128:216], sGT[:], sVRbd[0:64, :], dst=sT1b[:])
                yield
                chain_mm(PM[0:88, 256:344], sVRbd[0:64, :], sT1[:], dst=sWm[:],
                         mask=sMSK88[:])
                yield
                chain_mm(PM[0:88, 344:432], sULbd[0:64, :], sT1b[:], dst=sWmT[:],
                         mask=sMSK88[:])
                yield
                chain_mm(PM[0:88, 0:64], sWmT[:], sULbdT[:], dst=sF1[:])
                yield
                chain_mm(PM[0:88, 128:192], sWm[:], sURbdT[:], dst=sE1[:])
                yield
                b.bank_write("pe", pm)
                b.op("pe", lambda e: e.matmul(PM[0:64, 256:320], sVRbdT[:], sF1[:],
                                              start=True, stop=False))
                e7 = b.op("pe", lambda e: e.matmul(PM[0:64, 256:320], sVLbdT[:],
                                                   sE1[:], start=False, stop=True),
                          inc="pe")
                b.bank_note_write("pe", pm, e7)
                b.wait("dv", e7)
                b.wait("dv", zev)
                b.bank_read("dv", pm)
                stev = b.op("dv", lambda e: e.tensor_tensor(
                    out=sST[:], in0=sST[:], in1=PM[0:64, 256:320], op=ALU.add),
                    inc="dv", tiny=True)
                b.bank_note_read("dv", pm, stev)
                stbev = b.op("dv", lambda e: e.tensor_copy(out=sSTB[:], in_=sST[:]),
                             inc="dv", tiny=True)
                stb_ev[0] = stbev
                if debug_taps and c == 0:
                    b.wait("dv", dbg_last[0])
                    dst_ = b.op("dv", lambda e: e.tensor_copy(out=sOUT[0:64, 0:64],
                                                              in_=sST[:]))
                    b.wait("sy", dst_)
                    dbg_last[0] = b.op("sy", lambda e: e.dma_start(
                        out=taps["dbg_st"][:], in_=sOUT[0:64, 0:64]),
                        inc="ds", amt=16)

            chain_gens.append(chain_gen())

        def ffn_super(sc):
            c0 = 2 * sc
            x1evs = [chunk_x1[c0], chunk_x1[c0 + 1]]
            evx = []
            for ci in range(SUP):
                c = c0 + ci
                b.wait("pe", x1evs[ci])
                b.bank_write("pe", "PT")
                tev = None
                for j in range(8):
                    def em(e, c=c, j=j):
                        return e.transpose(
                            PT[:, j * 128:(j + 1) * 128],
                            sX1[:, (c % 3) * 1024 + j * 128:
                                (c % 3) * 1024 + (j + 1) * 128], sIDN[:])
                    tev = b.op("pe", em, inc="pe" if j == 7 else None)
                b.bank_note_write("pe", "PT", tev)
                b.wait("ac", tev)
                b.wait("ac", x1t_free[0])
                b.bank_read("ac", "PT")
                dst = sX1T[:].rearrange("p (j t) -> p j t", j=8)
                eev = b.op("ac", lambda e, ci=ci, dst=dst: e.copy(
                    out=dst[:, :, ci * 128:(ci + 1) * 128],
                    in_=PT[:].rearrange("p (j t) -> p j t", j=8)), inc="ac")
                b.bank_note_read("ac", "PT", eev)
                evx.append(eev)
                yield

            b.waits("pe", evx)
            for i in range(32):
                while prep_out["w1ev"][i] is None:
                    try:
                        next(pg_holder[0])
                    except StopIteration:
                        break
                b.wait("pe", prep_out["w1ev"][i])
                pk = "P6" if i % 2 == 0 else "P4"
                P = P6 if i % 2 == 0 else P4
                b.bank_write("pe", pk)
                ev = None
                for j in range(8):
                    def em(e, i=i, j=j, P=P):
                        return e.matmul(
                            P[:, 0:256],
                            sW1T[:, j * FF + i * 128:j * FF + i * 128 + 128],
                            sX1T[:, j * 256:(j + 1) * 256],
                            start=(j == 0), stop=(j == 7))
                    ev = b.op("pe", em, inc="pe" if j == 7 else None)
                b.bank_note_write("pe", pk, ev)
                b.wait("ac", ev)
                b.bank_read("ac", pk)
                gev = b.op("ac", lambda e, i=i, P=P: e.activation(
                    out=sHT[:, i * 256:(i + 1) * 256], in_=P[:, 0:256],
                    func=AF.Gelu, bias=sB1[:, i:i + 1]), inc="ac")
                b.bank_note_read("ac", pk, gev)
                yield
            x1t_free[0] = ev
            hlast = gev
            if debug_taps and sc == 0:
                b.wait("dv", hlast)
                b.wait("dv", dbg_last[0])
                dh = b.op("dv", lambda e: e.tensor_copy(out=sOUT[:, 0:256],
                                                        in_=sHT[:, 0:256]),
                          inc="dv")
                b.wait("sy", dh)
                dbg_last[0] = b.op("sy", lambda e: e.dma_start(
                    out=taps["dbg_ht"][:], in_=sOUT[:, 0:256]),
                    inc="ds", amt=16)

            b.wait("pe", hlast)
            for ci in range(SUP):
                c = c0 + ci
                for half in range(2):
                    b.bank_write("pe", "P4")
                    b.op("pe", lambda e, half=half: e.matmul(
                        P4[:], sCMASK[64:65, 512:640],
                        sWOB[64:65, half * 512:(half + 1) * 512],
                        start=True, stop=False))
                    ev = None
                    for i in range(32):
                        while prep_out["w2ev"][i // 8] is None:
                            try:
                                next(pg_holder[0])
                            except StopIteration:
                                break
                        b.wait("pe", prep_out["w2ev"][i // 8])
                        def em(e, i=i, ci=ci, half=half):
                            return e.matmul(
                                P4[:],
                                sHT[:, i * 256 + ci * 128:i * 256 + ci * 128 + 128],
                                sW2T[:, i * 1024 + half * 512:
                                     i * 1024 + (half + 1) * 512],
                                start=False, stop=(i == 31))
                        ev = b.op("pe", em, inc="pe" if i == 31 else None)
                    b.bank_note_write("pe", "P4", ev)
                    b.wait("dv", ev)
                    if half == 0:
                        b.wait("dv", sout_stored[0])
                    b.bank_read("dv", "P4")
                    r2ev = b.op("dv", lambda e, c=c, half=half: e.scalar_tensor_tensor(
                        out=sOUT[:, half * 512:(half + 1) * 512],
                        in0=sX1[:, (c % 3) * 1024 + half * 512:
                               (c % 3) * 1024 + (half + 1) * 512],
                        scalar=1.0, in1=P4[:], op0=ALU.mult, op1=ALU.add,
                        accum_out=sST1[:, 9 + half:10 + half]), inc="dv",
                        relax=True)
                    b.bank_note_read("dv", "P4", r2ev)
                    yield
                sx1_consumed[c % 3] = r2ev
                sx1_chunk[c % 3] = c
                b.wait("ac", r2ev)
                sq1 = b.op("ac", lambda e: e.activation(
                    out=sJNK[:], in_=sOUT[:, 0:512],
                    func=AF.Square, accum_out=sST1[:, 11:12]), inc="ac")
                b.wait("ac", sq1)
                sqev = b.op("ac", lambda e: e.activation(
                    out=sJNK[:], in_=sOUT[:, 512:1024],
                    func=AF.Square, accum_out=sST1[:, 12:13]), inc="ac")
                b.wait("dv", sqev)
                b.op("dv", lambda e: e.tensor_tensor(
                    out=sST1[:, 11:12], in0=sST1[:, 11:12], in1=sST1[:, 12:13],
                    op=ALU.add), tiny=True)
                b.op("dv", lambda e: e.tensor_tensor(
                    out=sST1[:, 9:10], in0=sST1[:, 9:10], in1=sST1[:, 10:11],
                    op=ALU.add), tiny=True)
                b.op("dv", lambda e: e.tensor_scalar(
                    out=sST1[:, 13:14], in0=sST1[:, 9:10], scalar1=1.0 / D,
                    scalar2=None, op0=ALU.mult), tiny=True)
                b.op("dv", lambda e: e.tensor_tensor(
                    out=sST1[:, 14:15], in0=sST1[:, 13:14], in1=sST1[:, 13:14],
                    op=ALU.mult), tiny=True)
                b.op("dv", lambda e: e.tensor_scalar(
                    out=sST1[:, 11:12], in0=sST1[:, 11:12], scalar1=1.0 / D,
                    scalar2=None, op0=ALU.mult), tiny=True)
                b.op("dv", lambda e: e.tensor_tensor(
                    out=sST1[:, 14:15], in0=sST1[:, 11:12], in1=sST1[:, 14:15],
                    op=ALU.subtract), tiny=True)
                b.op("dv", lambda e: e.tensor_scalar(
                    out=sST1[:, 14:15], in0=sST1[:, 14:15], scalar1=1e-5,
                    scalar2=None, op0=ALU.add), tiny=True)
                _rsqrt(b, sST1, sST1I, 14, 15, scr=17)
                b.op("dv", lambda e: e.tensor_tensor(
                    out=sST1[:, 16:17], in0=sST1[:, 13:14], in1=sST1[:, 15:16],
                    op=ALU.mult), tiny=True)
                nev = b.op("dv", lambda e: e.tensor_scalar(
                    out=sST1[:, 16:17], in0=sST1[:, 16:17], scalar1=-1.0,
                    scalar2=None, op0=ALU.mult), inc="dv", tiny=True)
                b.wait("ac", nev)
                if debug_taps:
                    b.wait("ac", dbg_last[0])
                oev = b.op("ac", lambda e: e.activation(
                    out=sOUT[:], in_=sOUT[:],
                    func=AF.Identity, bias=sST1[:, 16:17], scale=sST1[:, 15:16]),
                    inc="ac")
                b.wait("sy", oev)
                sout_stored[0] = b.op("sy", lambda e, c=c: e.dma_start(
                    out=out[c * 128:(c + 1) * 128, :], in_=sOUT[:]),
                    inc="ds", amt=16)
                yield

        b.wait("pe", state_init)
        b.wait("dv", state_init)

        pg = prep_gen()
        pg_holder[0] = pg
        others = gen_pool
        others.append(pg)
        pending = []
        done_ffn = [0]

        def refill():
            if pg not in others and pending and len(others) == 0:
                others.append(ffn_super(pending.pop(0)))

        def on_done(g_):
            if g_ is not pg:
                done_ffn[0] += 1

        pool_hooks["refill"] = refill
        pool_hooks["done"] = on_done

        def pump(target):
            while True:
                try:
                    next(target)
                except StopIteration:
                    return
                for g_ in list(chain_gens):
                    try:
                        next(g_)
                    except StopIteration:
                        chain_gens.remove(g_)
                advance_pool()

        for sc in range(nsuper):
            for ci in range(SUP):
                pump(scan_chunk(2 * sc + ci))
            pending.append(sc)
            refill()
        while others or pending:
            advance_pool()
        assert not pending

        b.wait("sy", sout_stored[0])
        total_st = b.cnt["ds"]
        b.ops["sy"].append(lambda e: e.wait_ge(sems["ds"], total_st))

        with nc.Block() as blk:
            @blk.sync
            def _(e):
                for f in b.ops["sy"]:
                    f(e)

            @blk.vector
            def _(e):
                for f in b.ops["dv"]:
                    f(e)

            @blk.scalar
            def _(e):
                for f in b.ops["ac"]:
                    f(e)

            @blk.gpsimd
            def _(e):
                for f in b.ops["po"]:
                    f(e)

            @blk.tensor
            def _(e):
                for f in b.ops["pe"]:
                    f(e)

    return nc


def _rsqrt(b, sST1, sST1I, src, dst, scr=7):
    b.op("dv", lambda e: e.tensor_scalar(
        out=sST1I[:, dst:dst + 1], in0=sST1I[:, src:src + 1], scalar1=1,
        scalar2=None, op0=ALU.arith_shift_right), tiny=True)
    b.op("dv", lambda e: e.tensor_scalar(
        out=sST1I[:, dst:dst + 1], in0=sST1I[:, dst:dst + 1], scalar1=-1,
        scalar2=MAGIC, op0=ALU.mult, op1=ALU.add), tiny=True)
    for _ in range(3):
        b.op("dv", lambda e: e.tensor_tensor(
            out=sST1[:, scr:scr + 1], in0=sST1[:, dst:dst + 1], in1=sST1[:, dst:dst + 1],
            op=ALU.mult), tiny=True)
        b.op("dv", lambda e: e.tensor_tensor(
            out=sST1[:, scr:scr + 1], in0=sST1[:, scr:scr + 1], in1=sST1[:, src:src + 1],
            op=ALU.mult), tiny=True)
        b.op("dv", lambda e: e.tensor_scalar(
            out=sST1[:, scr:scr + 1], in0=sST1[:, scr:scr + 1], scalar1=-0.5, scalar2=1.5,
            op0=ALU.mult, op1=ALU.add), tiny=True)
        b.op("dv", lambda e: e.tensor_tensor(
            out=sST1[:, dst:dst + 1], in0=sST1[:, dst:dst + 1], in1=sST1[:, scr:scr + 1],
            op=ALU.mult), tiny=True)



_NC_CACHE = {}

def kernel(**inputs):
    key = ("prog", DEBUG_TAPS)
    if key not in _NC_CACHE:
        _NC_CACHE[key] = build_program(debug_taps=DEBUG_TAPS)
    nc = _NC_CACHE[key]
    consts = build_constants(inputs)
    x = np.ascontiguousarray(inputs["L0_attn_x"].astype(np.float32, copy=False))
    shared = dict(W1=np.ascontiguousarray(inputs["L0_ffn_W1"]),
                  W2=np.ascontiguousarray(inputs["L0_ffn_W2"]), **consts)
    in_maps = [dict(x=np.ascontiguousarray(x[bb]), **shared) for bb in range(8)]
    res = run_bass_kernel_spmd(nc, in_maps, core_ids=list(range(8)))
    out = np.stack([res.results[bb]["out"] for bb in range(8)], axis=0)
    return out.astype(np.float32, copy=False)


# version 24 (final)
# speedup vs baseline: 1.0967x; 1.0177x over previous
import os
import numpy as np
import ml_dtypes
from collections import defaultdict
import contextlib

import concourse.bass as bass
import concourse.mybir as mybir
from concourse.bass_utils import run_bass_kernel_spmd

F32 = mybir.dt.float32
BF16 = mybir.dt.bfloat16
I32 = mybir.dt.int32
AF = mybir.ActivationFunctionType
ALU = mybir.AluOpType

D = 1024
DH = 64
T = 4096
CH = 128
NCH = T // CH
SUP = 2
FF = 4096
NLVL = 4
BS = [4, 8, 16, 32]
RK = [4, 8, 8, 8]
NN = [8, 4, 2, 1]
NLEAF = 16
MAGIC = 0x5F3759DF
DEBUG_TAPS = False
STRICT_ENGINE_ORDER = False
STRICT_ENGINES = ("dv",)
RELAX_OK = os.environ.get("BASS_SIM_FP64") != "1"


def _rank_cols():
    out = {}
    c = 0
    for l in range(NLVL):
        for n in range(NN[l]):
            out[(l, n)] = (c, c + RK[l])
            c += RK[l]
    return out

RANK_COLS = _rank_cols()

def _build_rounds():
    rounds = []
    for g in range(4):
        terms = []
        for j in range(4):
            n = 4 * g + j
            terms.append(dict(kind="leaf", K=4, tile=j, base=32 * (g % 4), n=n))
        rounds.append(terms)
    groups = [[(0, 0), (0, 1), (0, 2), (0, 3)],
              [(0, 4), (0, 5), (0, 6), (0, 7)],
              [(1, 0), (1, 1), (1, 2), (1, 3)],
              [(2, 0), (2, 1), (3, 0)]]
    for kind in ("top", "bot"):
        for gl, grp in enumerate(groups):
            terms = []
            for i, (l, n) in enumerate(grp):
                c0, c1 = RANK_COLS[(l, n)]
                terms.append(dict(kind=kind, K=RK[l], tile=i, base=32 * (gl % 4),
                                  l=l, n=n, c0=c0, c1=c1))
            rounds.append(terms)
    return rounds

ROUNDS = _build_rounds()
NROUND = len(ROUNDS)

def _node_spans(l, n):
    b = BS[l]
    L = slice(2 * b * n, 2 * b * n + b)
    R = slice(2 * b * n + b, 2 * b * (n + 1))
    return L, R


def build_constants(inp):
    UL = [inp[f"L0_attn_UL{l}"] for l in range(NLVL)]
    VR = [inp[f"L0_attn_VR{l}"] for l in range(NLVL)]
    UR = [inp[f"L0_attn_UR{l}"] for l in range(NLVL)]
    VL = [inp[f"L0_attn_VL{l}"] for l in range(NLVL)]

    ULbd = np.zeros((64, 88), np.float32)
    VRbd = np.zeros((64, 88), np.float32)
    URbd = np.zeros((64, 88), np.float32)
    VLbd = np.zeros((64, 88), np.float32)
    msk88 = np.zeros((88, 88), np.float32)
    for l in range(NLVL):
        for n in range(NN[l]):
            L, R = _node_spans(l, n)
            c0, c1 = RANK_COLS[(l, n)]
            ULbd[L, c0:c1] = UL[l]
            VLbd[L, c0:c1] = VL[l]
            VRbd[R, c0:c1] = VR[l]
            URbd[R, c0:c1] = UR[l]
            msk88[c0:c1, c0:c1] = 1.0
    leafm = np.zeros((64, 64), np.float32)
    for n in range(NLEAF):
        leafm[4 * n:4 * n + 4, 4 * n:4 * n + 4] = 1.0

    spvr = np.zeros((64, 512), np.float32)
    spul = np.zeros((64, 512), np.float32)
    spvl = np.zeros((64, 512), np.float32)
    spleaf = np.zeros((64, 512), np.float32)
    for terms in ROUNDS:
        for t in terms:
            tile, base = t["tile"], t["base"]
            c = tile * 128 + base
            if t["kind"] == "leaf":
                n = t["n"]
                for i in range(4):
                    spleaf[4 * n + i, c + i] = 1.0
            elif t["kind"] == "top":
                l, n = t["l"], t["n"]
                L, R = _node_spans(l, n)
                spvr[R, c:c + RK[l]] = VR[l]
            else:
                pass
    for g, terms in enumerate(ROUNDS):
        for t in terms:
            if t["kind"] == "bot":
                l, n = t["l"], t["n"]
                L, R = _node_spans(l, n)
                c = t["tile"] * 128 + t["base"]
                spul[L, c:c + RK[l]] = UL[l]
                spvl[L, c:c + RK[l]] = VL[l]

    proj = np.zeros((128, NROUND * 64), np.float32)
    for g, terms in enumerate(ROUNDS):
        for terms_i, t in enumerate(terms):
            base = 32 * terms_i
            if t["kind"] == "leaf":
                n = t["n"]
                for i in range(4):
                    proj[base + i, g * 64 + 4 * n + i] = 1.0
            else:
                fac = ULbd if t["kind"] == "top" else URbd
                c0, c1 = t["c0"], t["c1"]
                proj[base:base + (c1 - c0), g * 64:(g + 1) * 64] = fac[:, c0:c1].T

    cmask = np.tile((np.arange(CH)[:, None] <= np.arange(CH)[None, :])
                    .astype(ml_dtypes.bfloat16), (1, 4))
    ones64 = np.zeros((128, 128), ml_dtypes.bfloat16)
    ones64[64, :] = 1.0
    cmask = np.concatenate([cmask, ones64], axis=1)

    wqkv = np.concatenate([inp["L0_attn_Wq"], inp["L0_attn_Wk"],
                           inp["L0_attn_Wv"]], axis=1)
    wqkv_r = np.ascontiguousarray(
        wqkv.reshape(8, 128, 192).transpose(1, 0, 2).reshape(128, 8 * 192))
    b1r = np.ascontiguousarray(
        inp["L0_ffn_b1"].reshape(32, 128).T)

    return dict(
        c_wqkv=wqkv_r.astype(np.float32),
        c_wo=np.ascontiguousarray(inp["L0_attn_Wo"]).astype(np.float32),
        c_spvr=spvr, c_spul=spul, c_spvl=spvl, c_spleaf=spleaf,
        c_ulbd=ULbd, c_vrbd=VRbd,
        c_ulbdT=np.ascontiguousarray(ULbd.T), c_vrbdT=np.ascontiguousarray(VRbd.T),
        c_urbdT=np.ascontiguousarray(URbd.T), c_vlbdT=np.ascontiguousarray(VLbd.T),
        c_proj=proj, c_msk88=msk88, c_leafm=leafm, c_cmask=cmask,
        c_idn=np.eye(128, dtype=np.float32),
        c_b1r=b1r.astype(np.float32),
        c_b2=np.ascontiguousarray(inp["L0_ffn_b2"].reshape(1, 1024)).astype(np.float32),
    )


class Prog:
    ENGS = ("pe", "dv", "ac", "po", "sy")

    def __init__(self, nc, sems):
        self.nc = nc
        self.sems = sems
        self.cnt = defaultdict(int)
        self.hwm = defaultdict(lambda: defaultdict(int))
        self.ops = {e: [] for e in self.ENGS}
        self.bankR = defaultdict(list)
        self.bankW = defaultdict(list)

    SELF_SERIAL = ("dv", "ac", "po")
    strict = False
    strict_engines = ()

    def op(self, eng, emit, inc=None, amt=1, tiny=False, relax=False):
        if not RELAX_OK:
            relax = False
        strict_here = self.strict or (eng in self.strict_engines and not relax)
        if (strict_here or tiny) and eng in self.SELF_SERIAL and inc is None:
            inc = eng
        if ((strict_here or tiny) and eng in self.SELF_SERIAL and inc == eng
                and self.cnt[eng] > 0):
            self.wait(eng, (eng, self.cnt[eng]))
        if inc is not None:
            self.cnt[inc] += amt
            val = self.cnt[inc]
            sem = self.sems[inc]
            def f(e, emit=emit, sem=sem, amt=amt):
                emit(e).then_inc(sem, amt)
            self.ops[eng].append(f)
            return (inc, val)
        self.ops[eng].append(lambda e, emit=emit: emit(e))
        return None

    DMA_SEMS = ("dc", "dx0", "dx1", "dw", "dw1", "dw2", "ds")

    def wait(self, eng, ev):
        if ev is None:
            return
        name, val = ev
        if self.hwm[eng][name] >= val:
            return
        if name in self.DMA_SEMS:
            assert val == self.cnt[name], (
                f"partial DMA wait on {name}: {val} != {self.cnt[name]}")
        self.hwm[eng][name] = val
        sem = self.sems[name]
        self.ops[eng].append(lambda e, sem=sem, val=val: e.wait_ge(sem, val))

    def waits(self, eng, evs):
        best = {}
        for ev in evs:
            if ev is None:
                continue
            name, val = ev
            if name not in best or best[name] < val:
                best[name] = val
        for name, val in best.items():
            self.wait(eng, (name, val))

    def bank_write(self, eng, bank):
        for (e2, ev) in self.bankR[bank]:
            if e2 != eng:
                self.wait(eng, ev)
        for (e2, ev) in self.bankW[bank]:
            if e2 != eng:
                self.wait(eng, ev)
        self.bankR[bank] = []

    def bank_note_write(self, eng, bank, ev):
        self.bankW[bank] = [(eng, ev)]

    def bank_read(self, eng, bank):
        for (e2, ev) in self.bankW[bank]:
            if e2 != eng:
                self.wait(eng, ev)

    def bank_note_read(self, eng, bank, ev):
        self.bankR[bank].append((eng, ev))


def build_program(debug_taps=False, do_prep=True, nsuper=NCH // SUP, do_ffn=True, scan_parts=99):
    nc = bass.Bass("TRN2", target_bir_lowering=False, debug=False)

    x = nc.dram_tensor("x", [T, D], F32, kind="ExternalInput").ap()
    W1 = nc.dram_tensor("W1", [FF, D], F32, kind="ExternalInput").ap()
    W2 = nc.dram_tensor("W2", [D, FF], F32, kind="ExternalInput").ap()
    cons = {}
    cons["c_cmask"] = nc.dram_tensor("c_cmask", [128, 640], BF16,
                                     kind="ExternalInput").ap()
    for name, shape in [
        ("c_wqkv", [128, 8 * 192]), ("c_wo", [64, 1024]),
        ("c_spvr", [64, 512]), ("c_spul", [64, 512]), ("c_spvl", [64, 512]),
        ("c_spleaf", [64, 512]), ("c_ulbd", [64, 88]), ("c_vrbd", [64, 88]),
        ("c_ulbdT", [88, 64]), ("c_vrbdT", [88, 64]), ("c_urbdT", [88, 64]),
        ("c_vlbdT", [88, 64]), ("c_proj", [128, NROUND * 64]),
        ("c_msk88", [88, 88]), ("c_leafm", [64, 64]),
        ("c_idn", [128, 128]), ("c_b1r", [128, 32]), ("c_b2", [1, 1024]),
    ]:
        cons[name] = nc.dram_tensor(name, shape, F32, kind="ExternalInput").ap()
    out = nc.dram_tensor("out", [T, D], F32, kind="ExternalOutput").ap()
    taps = {}
    if debug_taps:
        for name, shape in [("dbg_qk", [128, 128]), ("dbg_vt", [64, 128]),
                            ("dbg_y", [64, 128]), ("dbg_den", [1, 128]),
                            ("dbg_ot", [64, 128]), ("dbg_attn", [128, 1024]),
                            ("dbg_x1", [128, 1024]), ("dbg_st", [64, 64]),
                            ("dbg_ht", [128, 256]), ("dbg_w1t", [128, 512]),
                            ("dbg_scale", [1, 2])]:
            taps[name] = nc.dram_tensor(name, shape, F32, kind="ExternalOutput").ap()

    st = contextlib.ExitStack()
    def sb(name, shape, dtype=F32):
        return st.enter_context(nc.sbuf_tensor(name, shape, dtype))
    def ps(name, shape, dtype=F32):
        return st.enter_context(nc.psum_tensor(name, shape, dtype))

    with st:
        sW1T = sb("sW1T", [128, 8 * FF], BF16)
        sW2T = sb("sW2T", [128, 32 * 1024], BF16)
        sHT = sb("sHT", [128, 32 * 256], BF16)
        sHT32 = sHT[:].bitcast(F32)
        sX = sb("sX", [128, 2 * 1024], F32)
        sXT = sb("sXT", [128, 1024], BF16)
        sXBF = sb("sXBF", [128, 1024], BF16)
        sR = sb("sR", [128, 1024], F32)
        sX1 = sb("sX1", [128, 3 * 1024], BF16)
        sX1T = sb("sX1T", [128, 8 * 256], BF16)
        sOUT = sb("sOUT", [128, 1024], F32)
        sQT = sb("sQT", [64, 128], BF16)
        sKT = sb("sKT", [64, 128], BF16)
        sVT = sb("sVT", [64, 128], BF16)
        sKV = sb("sKV", [128, 128], BF16)
        sKR = sb("sKR", [128, 128], BF16)
        sVR_ = sb("sVR_", [128, 128], BF16)
        ssp = {nm: sb("ssp_" + nm, [128, 512], BF16)
               for nm in ("vl", "ql", "v", "q", "k", "qb")}
        sCHAT = sb("sCHAT", [128, 8 * 128], BF16)
        sRDB = sb("sRDB", [1, 128], BF16)
        sJNK = sb("sJNK", [128, 512], BF16)
        sYSP = sb("sYSP", [128, 3 * 512], BF16)
        sOT = sb("sOT", [64, 128], BF16)
        sRD = sb("sRD", [64, 128], BF16)
        sST = sb("sST", [64, 64], F32)
        sSTB = sb("sSTB", [64, 64], BF16)
        sZ = sb("sZ", [64, 1], F32)
        sZB = sb("sZB", [64, 1], BF16)
        sG = sb("sG", [64, 64], BF16)
        sGT = sb("sGT", [64, 64], BF16)
        sT1 = sb("sT1", [64, 88], BF16)
        sT1b = sb("sT1b", [64, 88], BF16)
        sWm = sb("sWm", [88, 88], BF16)
        sWmT = sb("sWmT", [88, 88], BF16)
        sF1 = sb("sF1", [88, 64], BF16)
        sE1 = sb("sE1", [88, 64], BF16)
        sTMP = sb("sTMP", [128, 512], F32)
        sOR = sb("sOR", [1, 128], F32)
        sST1 = sb("sST1", [128, 20], F32)
        sST1I = sST1[:].bitcast(I32)
        sSC = sb("sSC", [4, 4], F32)
        sRS = sb("sRS", [128, 2], F32)
        sABS = sb("sABS", [128, 4], F32)

        sWQKV = sb("sWQKV", [128, 8 * 192], BF16)
        sWOB = sb("sWOB", [128, 1024], BF16)
        sSPc = {nm: sb("sSP_" + nm, [64, 512], BF16)
                for nm in ("vr", "ul", "vl", "leaf")}
        sULbd = sb("sULbd", [64, 88], BF16)
        sVRbd = sb("sVRbd", [64, 88], BF16)
        sULbdT = sb("sULbdT", [88, 64], BF16)
        sVRbdT = sb("sVRbdT", [88, 64], BF16)
        sURbdT = sb("sURbdT", [88, 64], BF16)
        sVLbdT = sb("sVLbdT", [88, 64], BF16)
        sPROJ = sb("sPROJ", [128, NROUND * 64], BF16)
        sMSK88 = sb("sMSK88", [88, 88], F32)
        sLEAFM = sb("sLEAFM", [64, 64], F32)
        sCMASK = sb("sCMASK", [128, 640], BF16)
        sIDN = sb("sIDN", [128, 128], BF16)
        sB1 = sb("sB1", [128, 32], F32)
        sONE = sb("sONE", [128, 1], BF16)
        sONER = sb("sONER", [1, 128], BF16)


        P1 = ps("P1", [128, 512], F32)
        P2 = ps("P2", [128, 512], F32)
        P3 = ps("P3", [128, 512], F32)
        P4 = ps("P4", [128, 512], F32)
        P5 = ps("P5", [128, 512], F32)
        P6 = ps("P6", [128, 512], F32)
        PT = ps("PT", [128, 1024], BF16)
        PS = ps("PS", [128, 512], F32)

        sems = {}
        for nm in ("pe", "dv", "ac", "po", "dc", "dx0", "dx1", "dw", "dw1", "dw2", "ds"):
            sems[nm] = st.enter_context(nc.semaphore(nm))
        b = Prog(nc, sems)
        b.strict = STRICT_ENGINE_ORDER
        b.strict_engines = STRICT_ENGINES

        const_plan = [
            ("c_wqkv", sWQKV, [128, 8 * 192]),
            ("c_wo", None, [64, 1024]),
            ("c_spvr", sSPc["vr"], [64, 512]),
            ("c_spul", sSPc["ul"], [64, 512]),
            ("c_spvl", sSPc["vl"], [64, 512]),
            ("c_spleaf", sSPc["leaf"], [64, 512]),
            ("c_ulbd", sULbd, [64, 88]),
            ("c_vrbd", sVRbd, [64, 88]),
            ("c_ulbdT", sULbdT, [88, 64]),
            ("c_vrbdT", sVRbdT, [88, 64]),
            ("c_urbdT", sURbdT, [88, 64]),
            ("c_vlbdT", sVLbdT, [88, 64]),
            ("c_proj", sPROJ, [128, NROUND * 64]),
            ("c_idn", sIDN, [128, 128]),
            ("c_b2", None, [1, 1024]),
        ]
        f32_plan = [("c_msk88", sMSK88), ("c_leafm", sLEAFM),
                    ("c_b1r", sB1)]
        bf_plan = [("c_cmask", sCMASK)]

        for name, dst in f32_plan + bf_plan:
            b.op("sy", lambda e, name=name, dst=dst:
                 e.dma_start(out=dst[:], in_=cons[name][:]), inc="dc", amt=16)
        cast_done = None
        batch, off = [], 0
        plan = list(const_plan)
        while plan:
            if plan and off + plan[0][2][1] <= 4096:
                name, dst, shp = plan.pop(0)
                ev = b.op("sy", lambda e, name=name, off=off, shp=shp:
                          e.dma_start(out=sHT32[0:shp[0], off:off + shp[1]],
                                      in_=cons[name][:]), inc="dc", amt=16)
                batch.append((name, dst, shp, off, ev))
                off += shp[1]
            if not plan or (plan and off + plan[0][2][1] > 4096):
                b.waits("dv", [bb[4] for bb in batch])
                for name, dst, shp, o2, _ in batch:
                    if dst is None:
                        dst = (sWOB[0:64, :] if name == "c_wo"
                               else sWOB[64:65, :])
                        shp = [shp[0], shp[1]]
                    cast_done = b.op("dv", lambda e, dst=dst, o2=o2, shp=shp:
                                     e.tensor_copy(
                                         out=dst[0:shp[0], 0:shp[1]],
                                         in_=sHT32[0:shp[0], o2:o2 + shp[1]]),
                                     inc="dv")
                if plan:
                    b.wait("sy", cast_done)
                batch, off = [], 0
        b.op("dv", lambda e: e.memset(sONE[:], 1.0))
        b.op("dv", lambda e: e.memset(sONER[:], 1.0))
        b.op("dv", lambda e: e.memset(sABS[:, 2:3], 1.0))
        cast_done = b.op("dv", lambda e: e.memset(sOR[:], 1.0),
                         inc="dv")
        b.op("dv", lambda e: e.memset(sKR[:], 0.0))
        b.op("dv", lambda e: e.memset(sVR_[:], 0.0))
        b.op("dv", lambda e: e.memset(sST[:], 0.0))
        b.op("dv", lambda e: e.memset(sSTB[:], 0.0))
        b.op("dv", lambda e: e.memset(sZ[:], 0.0))
        state_init = b.op("dv", lambda e: e.memset(sZB[:], 0.0), inc="dv")

        b.op("dv", lambda e: e.memset(sABS[:, 0:2], 0.0))
        absinit = b.op("dv", lambda e: e.memset(sABS[:, 3:4], 0.0), inc="dv")
        b.wait("ac", cast_done)
        b.wait("ac", absinit)
        b.wait("po", cast_done)

        W1r = W1.rearrange("(s p) d -> s p d", p=128)
        W2r = W2.rearrange("(s p) f -> s p f", p=128)

        PT6 = P6[:].bitcast(BF16)
        prep_out = {}

        def prep_pass1():
            items = []
            for s in range(0, 32, 2):
                items.append((0, s))
            for s in range(8):
                for h in range(2):
                    items.append((1, (s, h)))
            loadev = [None] * len(items)
            absev = [None] * len(items)
            addev = absinit
            for i, (kind, key) in enumerate(items):
                buf = (i % 2) * 2048
                if i >= 2:
                    b.wait("po", absev[i - 2])
                if kind == 0:
                    def em(e, s=key, buf=buf):
                        return e.dma_start(
                            out=sHT32[:, buf:buf + 2048].rearrange(
                                "p (s d) -> p s d", s=2),
                            in_=W1r[s:s + 2].rearrange("s p d -> p s d"))
                else:
                    s, h = key
                    def em(e, s=s, h=h, buf=buf):
                        return e.dma_start(
                            out=sHT32[:, buf:buf + 2048],
                            in_=W2r[s][:, 2048 * h:2048 * (h + 1)])
                loadev[i] = b.op("po", em, inc="dw" if i % 2 == 0 else "dw1",
                                 amt=16)
                b.wait("ac", loadev[i])
                b.wait("ac", addev)
                absev[i] = b.op("ac", lambda e, buf=buf: e.activation(
                    out=sW2T[:, 0:2048], in_=sHT32[:, buf:buf + 2048],
                    func=AF.Abs, accum_out=sABS[:, 3:4]), inc="ac")
                col = 0 if kind == 0 else 1
                b.wait("dv", absev[i])
                addev = b.op("dv", lambda e, col=col: e.tensor_tensor(
                    out=sABS[:, col:col + 1], in0=sABS[:, col:col + 1],
                    in1=sABS[:, 3:4], op=ALU.add), inc="dv", tiny=True)
                yield
            prep_out["p1"] = addev

        def prep_scale():
            b.waits("pe", [prep_out.get("p1", absinit)])
            b.bank_write("pe", "PS")
            mev = b.op("pe", lambda e: e.matmul(PS[0:1, 0:2], sABS[:, 2:3], sABS[:, 0:2],
                                            start=True, stop=True), inc="pe")
            b.bank_note_write("pe", "PS", mev)
            b.wait("dv", mev)
            b.bank_read("dv", "PS")
            b.op("dv", lambda e: e.tensor_scalar(
            out=sSC[0:1, 0:2], in0=PS[0:1, 0:2], scalar1=1.0 / (FF * D),
            scalar2=1e-8, op0=ALU.mult, op1=ALU.max), tiny=True)
            rev = b.op("dv", lambda e: e.reciprocal(out=sSC[0:1, 2:4],
                                                in_=sSC[0:1, 0:2]),
                       inc="dv", tiny=True)
            b.bank_note_read("dv", "PS", rev)
            if debug_taps:
                b.wait("sy", rev)
                b.op("sy", lambda e: e.dma_start(
                    out=taps["dbg_scale"][:], in_=sSC[0:1, 2:4]),
                    inc="ds", amt=16)
            b.wait("pe", rev)
            b.bank_write("pe", "PS")
            mev = b.op("pe", lambda e: e.matmul(PS[:, 0:2], sOR[:],
                                            sSC[0:1, 2:4], start=True, stop=True),
                   inc="pe")
            b.bank_note_write("pe", "PS", mev)
            b.wait("dv", mev)
            b.bank_read("dv", "PS")
            rsev = b.op("dv", lambda e: e.tensor_copy(out=sRS[:], in_=PS[:, 0:2]),
                        inc="dv")
            b.bank_note_read("dv", "PS", rsev)
            prep_out["rsev"] = rsev


        tern_tmp = sHT32[:, 2048:3072]
        tern_bf = sX1T

        def prep_pass2():
            b.wait("po", prep_out["rsev"])
            units = [("w1", s) for s in range(32)]
            units += [("w2", (s, q)) for q in range(4) for s in range(8)]
            e2s = [None] * len(units)
            e3s = [None] * len(units)
            tevs = [None] * len(units)
            last_evac = None
            BUFS = (0, 1024, 3072)
            for i, (kind, key) in enumerate(units):
                buf = BUFS[i % 3]
                rcol = 0 if kind == "w1" else 1
                if i >= 3:
                    b.wait("po", e2s[i - 3])
                if kind == "w1":
                    def em(e, s=key, buf=buf):
                        return e.dma_start(out=sHT32[:, buf:buf + 1024],
                                           in_=W1r[s])
                else:
                    s, q = key
                    def em(e, s=s, q=q, buf=buf):
                        return e.dma_start(out=sHT32[:, buf:buf + 1024],
                                           in_=W2r[s][:, 1024 * q:1024 * (q + 1)])
                lw = b.op("po", em, inc=("dw", "dw1", "dw2")[i % 3], amt=16)
                obuf = (i % 2) * 1024
                b.wait("dv", lw)
                if i == 0:
                    b.wait("dv", prep_out["rsev"])
                e1 = b.op("dv", lambda e, buf=buf, rcol=rcol: e.tensor_scalar(
                    out=tern_tmp[:, 0:1024],
                    in0=sHT32[:, buf:buf + 1024],
                    scalar1=sRS[:, rcol:rcol + 1], scalar2=-0.5,
                    op0=ALU.mult, op1=ALU.is_le), inc="dv", relax=True)
                if i >= 2:
                    b.wait("dv", tevs[i - 2])
                e2s[i] = b.op("dv", lambda e, buf=buf, obuf=obuf, rcol=rcol:
                              e.tensor_scalar(
                    out=tern_bf[:, obuf:obuf + 1024],
                    in0=sHT32[:, buf:buf + 1024],
                    scalar1=sRS[:, rcol:rcol + 1], scalar2=0.5,
                    op0=ALU.mult, op1=ALU.is_ge), inc="dv", relax=True)
                e3s[i] = b.op("dv", lambda e, obuf=obuf: e.tensor_tensor(
                    out=tern_bf[:, obuf:obuf + 1024],
                    in0=tern_bf[:, obuf:obuf + 1024],
                    in1=tern_tmp[:, 0:1024],
                    op=ALU.subtract), inc="dv", relax=True)
                b.wait("pe", e3s[i])
                b.bank_write("pe", "P6")
                tev = None
                for j in range(8):
                    tev = b.op("pe", lambda e, j=j, obuf=obuf: e.transpose(
                        PT6[:, j * 128:(j + 1) * 128],
                        tern_bf[:, obuf + j * 128:obuf + (j + 1) * 128], sIDN[:]),
                        inc="pe" if j == 7 else None)
                tevs[i] = tev
                b.bank_note_write("pe", "P6", tev)
                b.wait("ac", tev)
                b.bank_read("ac", "P6")
                if kind == "w1":
                    dst = sW1T[:].rearrange("p (j f) -> p j f", j=8)
                    dsl = dst[:, :, 128 * key:128 * key + 128]
                else:
                    s, q = key
                    dst = sW2T[:].rearrange("p (f d) -> p f d", f=32)
                    dsl = dst[:, 8 * q:8 * q + 8, 128 * s:128 * s + 128]
                last_evac = b.op("ac", lambda e, dsl=dsl: e.copy(
                    out=dsl, in_=PT6.rearrange("p (j f) -> p j f", j=8)),
                    inc="ac")
                b.bank_note_read("ac", "P6", last_evac)
                if kind == "w1":
                    prep_out["w1ev"][key] = last_evac
                else:
                    s, q = key
                    if s == 7:
                        prep_out["w2ev"][q] = last_evac
                yield
            prep_out["w1"] = last_evac
            prep_out["w2"] = last_evac

        def prep_gen():
            prep_out["w1ev"] = [None] * 32
            prep_out["w2ev"] = [None] * 4
            if do_prep:
                for _ in prep_pass1():
                    yield
            prep_scale()
            if do_prep:
                for _ in prep_pass2():
                    yield
            else:
                prep_out["w1"] = prep_out["w2"] = prep_out["rsev"]
                prep_out["w1ev"] = [prep_out["rsev"]] * 32
                prep_out["w2ev"] = [prep_out["rsev"]] * 4

        xin = [None, None]
        xin[0] = b.op("sy", lambda e: e.dma_start(
            out=sX[:, 0:1024], in_=x[0:128, :]), inc="dx0", amt=16)

        sx1_consumed = [None] * 3
        sx1_chunk = [-1] * 3
        gen_pool = []
        pool_hooks = {"refill": lambda: None, "done": lambda g: None}
        def advance_pool():
            for g_ in list(gen_pool):
                try:
                    next(g_)
                except StopIteration:
                    gen_pool.remove(g_)
                    pool_hooks["done"](g_)
                    pool_hooks["refill"]()
            if not gen_pool:
                pool_hooks["refill"]()
        sR_last_ac = [None]
        dbg_last = [None]
        sout_stored = [None]
        x_consumed = [None, None]
        xcast_ev = [None, None]
        x1t_free = [None]
        pg_holder = [None]
        zb_ev = [None]
        chat_ring_ev = [None] * 8
        stb_ev = [None]
        chain_gens = []
        chunk_x1 = {}

        def scan_chunk(c):
            guard = 0
            while c >= 3 and sx1_chunk[c % 3] < c - 3:
                advance_pool()
                guard += 1
                assert guard < 100000, "x1 ring consumer never emitted"
            pm, PM = ("P5", P5) if c % 2 == 0 else ("PS", PS)
            zb_prev = zb_ev[0]
            buf = c % 2
            nbuf = (c + 1) % 2
            if c + 1 < NCH:
                b.wait("sy", x_consumed[nbuf])
                xin[nbuf] = b.op("sy", lambda e, c=c, nbuf=nbuf: e.dma_start(
                    out=sX[:, nbuf * 1024:nbuf * 1024 + 1024],
                    in_=x[(c + 1) * 128:(c + 2) * 128, :]),
                    inc="dx0" if nbuf == 0 else "dx1", amt=16)

            xs = sX[:, buf * 1024:(buf + 1) * 1024]
            if c == 0:
                b.wait("ac", xin[buf])
                xcast_ev[buf] = b.op("ac", lambda e, xs=xs: e.copy(
                    out=sXBF[:], in_=xs), inc="ac")
            b.wait("pe", xcast_ev[buf])
            b.bank_write("pe", "PT")
            tev = None
            for j in range(8):
                tev = b.op("pe", lambda e, j=j: e.transpose(
                    PT[:, j * 128:(j + 1) * 128],
                    sXBF[:, j * 128:(j + 1) * 128], sIDN[:]),
                    inc="pe" if j == 7 else None)
            b.bank_note_write("pe", "PT", tev)
            b.wait("ac", tev)
            b.bank_read("ac", "PT")
            xtev = b.op("ac", lambda e: e.copy(out=sXT[:], in_=PT[:]), inc="ac")
            b.bank_note_read("ac", "PT", xtev)
            if c + 1 < NCH:
                nxs = sX[:, nbuf * 1024:(nbuf + 1) * 1024]
                b.wait("ac", xin[nbuf])
                xcast_ev[nbuf] = b.op("ac", lambda e, nxs=nxs: e.copy(
                    out=sXBF[:], in_=nxs), inc="ac")
            yield

            b.wait("pe", xtev)
            b.bank_write("pe", pm)
            qkvev = None
            for which, (r0, c0, wcol, w) in enumerate(
                    [(0, 0, 0, 128), (0, 128, 128, 64)]):
                for j in range(8):
                    qkvev = b.op("pe", lambda e, r0=r0, c0=c0, wcol=wcol,
                                 w=w, j=j: e.matmul(
                        PM[r0:r0 + w, c0:c0 + 128],
                        sWQKV[:, j * 192 + wcol:j * 192 + wcol + w],
                        sXT[:, j * 128:(j + 1) * 128],
                        start=(j == 0), stop=(j == 7)),
                        inc="pe" if (which == 1 and j == 7) else None)
            b.bank_note_write("pe", pm, qkvev)
            yield

            b.wait("dv", qkvev)
            b.bank_read("dv", pm)
            b.op("dv", lambda e: e.tensor_scalar(
                out=sTMP[:, 0:128], in0=PM[:, 0:128], scalar1=0.0, scalar2=None,
                op0=ALU.min), relax=True)
            uev = b.op("dv", lambda e: e.tensor_copy(out=sVT[:], in_=PM[0:64, 128:256]),
                       inc="dv", relax=True)
            b.bank_note_read("dv", pm, uev)
            b.wait("ac", uev)
            b.bank_read("ac", pm)
            thev = b.op("ac", lambda e: e.activation(
                out=sTMP[:, 128:256], in_=sTMP[:, 0:128], func=AF.Tanh, scale=0.5),
                inc="ac")
            rlev = b.op("ac", lambda e: e.activation(
                out=sTMP[:, 256:384], in_=PM[:, 0:128], func=AF.Relu), inc="ac")
            b.bank_note_read("ac", pm, rlev)
            b.waits("dv", [thev, rlev])
            b.op("dv", lambda e: e.tensor_scalar(
                out=sTMP[:, 384:512], in0=sTMP[:, 128:256], scalar1=-1.0,
                scalar2=1.0, op0=ALU.mult, op1=ALU.add), relax=True)
            b.op("dv", lambda e: e.reciprocal(out=sTMP[:, 384:512],
                                              in_=sTMP[:, 384:512]), relax=True)
            b.op("dv", lambda e: e.tensor_scalar(
                out=sTMP[:, 0:128], in0=sTMP[:, 128:256], scalar1=1.0,
                scalar2=None, op0=ALU.add), relax=True)
            b.op("dv", lambda e: e.tensor_tensor(
                out=sTMP[:, 0:128], in0=sTMP[:, 0:128], in1=sTMP[:, 384:512],
                op=ALU.mult), relax=True)
            b.op("dv", lambda e: e.tensor_tensor(
                out=sQT[:], in0=sTMP[0:64, 0:128], in1=sTMP[0:64, 256:384],
                op=ALU.add), relax=True)
            qkev = b.op("dv", lambda e: e.tensor_tensor(
                out=sKT[:], in0=sTMP[64:128, 0:128], in1=sTMP[64:128, 256:384],
                op=ALU.add), inc="dv", relax=True)
            yield

            b.wait("pe", qkev)
            b.bank_write("pe", "PT")
            b.op("pe", lambda e: e.transpose(PT[:, 0:64], sKT[:],
                                             sIDN[0:64, 0:64]))
            kvtev = b.op("pe", lambda e: e.transpose(PT[:, 64:128], sVT[:],
                                                     sIDN[0:64, 0:64]), inc="pe")
            b.bank_note_write("pe", "PT", kvtev)
            b.wait("ac", kvtev)
            b.bank_read("ac", "PT")
            kvev = b.op("ac", lambda e: e.copy(out=sKV[:], in_=PT[:, 0:128]),
                        inc="ac")
            b.bank_note_read("ac", "PT", kvev)

            b.bank_write("pe", pm)
            b.op("pe", lambda e: e.matmul(PM[:, 256:344], sKT[:],
                                          sULbd[0:64, :], start=True, stop=True))
            b.op("pe", lambda e: e.matmul(PM[:, 344:432], sVT[:], sVRbd[0:64, :],
                                          start=True, stop=True))
            chzev = b.op("pe", lambda e: e.matmul(
                PM[:, 0:128], sKT[:], sQT[:], start=True, stop=True), inc="pe")
            b.bank_note_write("pe", pm, chzev)
            b.wait("ac", chzev)
            b.bank_read("ac", pm)
            b.op("ac", lambda e: e.copy(out=sKR[:, 0:88], in_=PM[:, 256:344]))
            krev = b.op("ac", lambda e: e.copy(out=sVR_[:, 0:88], in_=PM[:, 344:432]),
                        inc="ac")
            b.bank_note_read("ac", pm, krev)
            b.wait("dv", krev)
            b.bank_read("dv", pm)
            chzm = b.op("dv", lambda e: e.tensor_tensor(
                out=sCHAT[:, 7 * 128:8 * 128], in0=PM[:, 0:128],
                in1=sCMASK[:, 0:128], op=ALU.mult), inc="dv", relax=True)
            b.bank_note_read("dv", pm, chzm)
            yield

            yield
            b.wait("pe", chzm)
            b.wait("pe", zb_prev)
            b.bank_write("pe", pm)
            b.op("pe", lambda e: e.matmul(PM[0:1, 128:256], sONE[:],
                                          sCHAT[:, 7 * 128:8 * 128],
                                          start=True, stop=False))
            denev = b.op("pe", lambda e: e.matmul(
                PM[0:1, 128:256], sZB[:], sQT[:], start=False, stop=True),
                inc="pe")
            b.bank_note_write("pe", pm, denev)
            b.wait("dv", denev)
            b.bank_read("dv", pm)
            b.op("dv", lambda e: e.tensor_scalar(
                out=sTMP[0:1, 0:128], in0=PM[0:1, 128:256], scalar1=1e-6,
                scalar2=None, op0=ALU.max))
            rdev = b.op("dv", lambda e: e.reciprocal(out=sTMP[0:1, 0:128],
                                                     in_=sTMP[0:1, 0:128]), inc="dv")
            rdbf = b.op("dv", lambda e: e.tensor_copy(
                out=sRDB[:], in_=sTMP[0:1, 0:128]), inc="dv")
            b.bank_note_read("dv", pm, rdbf)
            if debug_taps and c == 0:
                b.wait("dv", dbg_last[0])
                dcp = b.op("dv", lambda e: e.tensor_copy(
                    out=sOUT[0:1, 0:128], in_=sTMP[0:1, 0:128]))
                b.wait("sy", dcp)
                dbg_last[0] = b.op("sy", lambda e: e.dma_start(
                    out=taps["dbg_den"][:], in_=sOUT[0:1, 0:128]),
                    inc="ds", amt=16)

            yield
            b.wait("pe", rdbf)
            b.bank_write("pe", pm)
            rdbev = b.op("pe", lambda e: e.matmul(
                PM[0:64, 128:256], sONER[0:1, 0:64], sRDB[:],
                start=True, stop=True), inc="pe")
            b.bank_note_write("pe", pm, rdbev)
            b.wait("dv", rdbev)
            b.bank_read("dv", pm)
            rdcp = b.op("dv", lambda e: e.tensor_copy(out=sRD[:],
                                                      in_=PM[0:64, 128:256]),
                        inc="dv", relax=True)
            b.bank_note_read("dv", pm, rdcp)
            yield

            b.wait("pe", kvev)
            b.bank_write("pe", pm)
            b.op("pe", lambda e: e.matmul(PM[0:64, 0:64], sKV[:, 0:64],
                                          sKV[:, 64:128], start=True, stop=True))
            gev = b.op("pe", lambda e: e.matmul(
                PM[0:64, 64:128], sKV[:, 64:128], sKV[:, 0:64],
                start=True, stop=True), inc="pe")
            b.bank_note_write("pe", pm, gev)
            b.wait("dv", gev)
            b.bank_read("dv", pm)
            b.op("dv", lambda e: e.tensor_copy(out=sG[:], in_=PM[0:64, 0:64]),
                 relax=True)
            gtev = b.op("dv", lambda e: e.tensor_copy(out=sGT[:],
                                                      in_=PM[0:64, 64:128]),
                        inc="dv", relax=True)
            b.op("dv", lambda e: e.tensor_tensor(
                out=sTMP[0:64, 0:64], in0=PM[0:64, 64:128], in1=sLEAFM[:],
                op=ALU.mult), tiny=True)
            stleaf = b.op("dv", lambda e: e.tensor_tensor(
                out=sST[:], in0=sST[:], in1=sTMP[0:64, 0:64], op=ALU.add),
                inc="dv", tiny=True)
            b.bank_note_read("dv", pm, stleaf)
            b.op("dv", lambda e: e.tensor_reduce(
                out=sTMP[0:64, 64:65], in_=sKT[:], axis=mybir.AxisListType.X,
                op=ALU.add), tiny=True)
            zev = b.op("dv", lambda e: e.tensor_tensor(
                out=sZ[:], in0=sZ[:], in1=sTMP[0:64, 64:65], op=ALU.add),
                inc="dv", tiny=True)
            zbev = b.op("dv", lambda e: e.tensor_copy(out=sZB[:], in_=sZ[:]),
                        inc="dv", tiny=True)
            zb_ev[0] = zbev
            yield

            b.wait("pe", qkev)
            spev = {}
            srcs = dict(vl=(sSPc["leaf"], sVT[0:64, :]),
                        ql=(sSPc["leaf"], sQT[:]),
                        v=(sSPc["vr"], sVT[0:64, :]),
                        q=(sSPc["vr"], sQT[:]),
                        k=(sSPc["ul"], sKT[:]),
                        qb=(sSPc["vl"], sQT[:]))
            names = list(srcs.keys())
            for idx, nm in enumerate(names):
                pk = "P1" if idx % 2 == 0 else "P2"
                P = P1 if idx % 2 == 0 else P2
                con, src_ = srcs[nm]
                b.bank_write("pe", pk)
                ev = None
                for tl in range(4):
                    ev = b.op("pe", lambda e, con=con, src_=src_, tl=tl, P=P:
                              e.matmul(P[:, tl * 128:(tl + 1) * 128],
                                       con[:, tl * 128:(tl + 1) * 128], src_,
                                       start=True, stop=True),
                              inc="pe" if tl == 3 else None)
                b.bank_note_write("pe", pk, ev)
                b.wait("ac", ev)
                b.bank_read("ac", pk)
                sev = b.op("ac", lambda e, nm=nm, P=P: e.copy(
                    out=ssp[nm][:], in_=P[:]), inc="ac")
                b.bank_note_read("ac", pk, sev)
                spev[nm] = sev
                if idx % 2 == 1:
                    yield

            ysp_evts = []
            ysp_sb_slices = []
            chat_ring_ev[7] = denev

            def emit_coeff(g):
                terms = ROUNDS[g]
                pk = "P1" if g % 2 == 0 else "P2"
                P = P1 if g % 2 == 0 else P2
                b.waits("pe", [spev[t_side] for t_side in
                               ({"leaf": ("vl", "ql"), "top": ("v", "q"),
                                 "bot": ("k", "qb")}[terms[0]["kind"]])])
                b.bank_write("pe", pk)
                ev = None
                for i, t_ in enumerate(terms):
                    lhs_nm, rhs_nm = {"leaf": ("vl", "ql"), "top": ("v", "q"),
                                      "bot": ("k", "qb")}[t_["kind"]]
                    tl, ba, K = t_["tile"], t_["base"], t_["K"]
                    def em(e, lhs_nm=lhs_nm, rhs_nm=rhs_nm, tl=tl, ba=ba, K=K,
                           i=i, P=P):
                        return e.matmul(
                            P[:, i * 128:(i + 1) * 128],
                            ssp[lhs_nm][ba:ba + K, tl * 128:(tl + 1) * 128],
                            ssp[rhs_nm][ba:ba + K, tl * 128:(tl + 1) * 128],
                            start=True, stop=True,
                            tile_position=(ba, 0))
                    ev = b.op("pe", em, inc="pe" if i == len(terms) - 1 else None)
                b.bank_note_write("pe", pk, ev)
                return ev

            def emit_mask(g, ev):
                terms = ROUNDS[g]
                nt = len(terms)
                pk = "P1" if g % 2 == 0 else "P2"
                P = P1 if g % 2 == 0 else P2
                b.wait("dv", ev)
                b.bank_read("dv", pk)
                slot = (g * 4) % 8
                for i in range(nt):
                    b.wait("dv", chat_ring_ev[(slot + i) % 8])
                meva = b.op("dv", lambda e, slot=slot, P=P: e.tensor_tensor(
                    out=sCHAT[:, slot * 128:(slot + 2) * 128],
                    in0=P[:, 0:256],
                    in1=sCMASK[:, 0:256],
                    op=ALU.mult), inc="dv", relax=True)
                mevb = b.op("dv", lambda e, nt=nt, slot=slot, P=P: e.tensor_tensor(
                    out=sCHAT[:, (slot + 2) * 128:(slot + nt) * 128],
                    in0=P[:, 256:nt * 128],
                    in1=sCMASK[:, 256:nt * 128],
                    op=ALU.mult), inc="dv", relax=True)
                b.bank_note_read("dv", pk, mevb)
                return (meva, mevb)

            def emit_apply(g, mev2):
                terms = ROUNDS[g]
                nt = len(terms)
                slot = (g * 4) % 8
                ypos = g % 4
                b.wait("pe", krev)
                b.wait("pe", mev2[0])
                if ypos == 0:
                    b.bank_write("pe", "P3")
                aev = None
                for i, t_ in enumerate(terms):
                    if i == 2:
                        b.wait("pe", mev2[1])
                    if t_["kind"] == "leaf":
                        lhs = sKV[:, 4 * t_["n"]:4 * t_["n"] + 32]
                    elif t_["kind"] == "top":
                        lhs = sKR[:, t_["c0"]:t_["c0"] + 32]
                    else:
                        lhs = sVR_[:, t_["c0"]:t_["c0"] + 32]
                    def em(e, lhs=lhs, i=i, slot=slot, ypos=ypos):
                        return e.matmul(
                            P3[32 * i:32 * i + 32, ypos * 128:(ypos + 1) * 128],
                            lhs, sCHAT[:, (slot + i) * 128:(slot + i + 1) * 128],
                            start=True, stop=True, tile_position=(0, 32 * i))
                    aev = b.op("pe", em, inc="pe" if (i == nt - 1 and nt == 4)
                               else None)
                if nt == 3:
                    def em(e, slot=slot, ypos=ypos):
                        return e.matmul(
                            P3[96:128, ypos * 128:(ypos + 1) * 128],
                            sKR[:, 96:128],
                            sCHAT[:, slot * 128:(slot + 1) * 128],
                            start=True, stop=True, tile_position=(0, 96))
                    aev = b.op("pe", em, inc="pe")
                for i in range(nt):
                    chat_ring_ev[(slot + i) % 8] = aev
                if ypos == 3 or g == NROUND - 1:
                    b.bank_note_write("pe", "P3", aev)
                    third = g // 4
                    b.wait("ac", aev)
                    b.bank_read("ac", "P3")
                    yev = b.op("ac", lambda e, third=third: e.copy(
                        out=sYSP[:, third * 512:(third + 1) * 512], in_=P3[:]),
                        inc="ac")
                    b.bank_note_read("ac", "P3", yev)
                    ysp_evts.append(yev)
                    for gg in range(g - ypos, g + 1):
                        ysp_sb_slices.append(
                            (sYSP[:, third * 512 + (gg % 4) * 128:
                                  third * 512 + (gg % 4 + 1) * 128], gg))

            cev = {0: emit_coeff(0)}
            for g in range(NROUND):
                if g + 1 < NROUND:
                    cev[g + 1] = emit_coeff(g + 1)
                mev2 = emit_mask(g, cev[g])
                emit_apply(g, mev2)
                yield

            while chain_gens:
                g0 = chain_gens[0]
                try:
                    next(g0)
                except StopIteration:
                    chain_gens.pop(0)
            b.wait("pe", stb_ev[0])
            b.waits("pe", ysp_evts)
            b.waits("pe", [stleaf, rdcp])
            b.bank_write("pe", pm)
            b.op("pe", lambda e: e.matmul(PM[0:64, 384:512], sSTB[:],
                                          sQT[:], start=True, stop=False))
            yev = None
            for (ysl, gg) in ysp_sb_slices:
                yev = b.op("pe", lambda e, ysl=ysl, gg=gg: e.matmul(
                    PM[0:64, 384:512], sPROJ[:, gg * 64:(gg + 1) * 64], ysl,
                    start=False, stop=(gg == NROUND - 1)),
                    inc="pe" if gg == NROUND - 1 else None)
            b.bank_note_write("pe", pm, yev)
            if debug_taps and c == 0:
                b.wait("dv", yev)
                b.bank_read("dv", pm)
                b.wait("dv", dbg_last[0])
                dvy = b.op("dv", lambda e: e.tensor_copy(out=sOUT[0:64, 0:128],
                                                         in_=PM[0:64, 384:512]),
                           inc="dv")
                b.bank_note_read("dv", pm, dvy)
                b.wait("sy", dvy)
                dbg_last[0] = b.op("sy", lambda e: e.dma_start(
                    out=taps["dbg_y"][:], in_=sOUT[0:64, 0:128]),
                    inc="ds", amt=16)

            b.wait("dv", yev)
            b.wait("dv", rdcp)
            b.bank_read("dv", pm)
            otev = b.op("dv", lambda e: e.tensor_tensor(
                out=sOT[:], in0=PM[0:64, 384:512], in1=sRD[:], op=ALU.mult),
                inc="dv", relax=True)
            b.bank_note_read("dv", pm, otev)
            yield

            yield
            b.wait("pe", otev)
            b.bank_write("pe", pm)
            aaev = b.op("pe", lambda e: e.matmul(
                PM[:, 0:512], sOT[:], sWOB[0:64, 0:512], start=True, stop=True),
                inc="pe")
            b.bank_note_write("pe", pm, aaev)
            b.waits("dv", [aaev])
            b.wait("dv", sR_last_ac[0])
            b.bank_read("dv", pm)
            r1a = b.op("dv", lambda e, xs=xs: e.scalar_tensor_tensor(
                out=sR[:, 0:512], in0=xs[:, 0:512], scalar=1.0, in1=PM[:, 0:512],
                op0=ALU.mult, op1=ALU.add, accum_out=sST1[:, 0:1]), inc="dv",
                relax=True)
            b.bank_note_read("dv", pm, r1a)
            yield
            b.wait("pe", r1a)
            b.bank_write("pe", pm)
            abev = b.op("pe", lambda e: e.matmul(
                PM[:, 0:512], sOT[:], sWOB[0:64, 512:1024], start=True, stop=True),
                inc="pe")
            b.bank_note_write("pe", pm, abev)
            b.wait("dv", abev)
            b.bank_read("dv", pm)
            r1ev = b.op("dv", lambda e, xs=xs: e.scalar_tensor_tensor(
                out=sR[:, 512:1024], in0=xs[:, 512:1024], scalar=1.0,
                in1=PM[:, 0:512], op0=ALU.mult, op1=ALU.add,
                accum_out=sST1[:, 1:2]), inc="dv", relax=True)
            b.bank_note_read("dv", pm, r1ev)
            x_consumed[buf] = r1ev
            if debug_taps and c == 0:
                b.wait("dv", dbg_last[0])
                dra = b.op("dv", lambda e: e.tensor_copy(
                    out=sOUT[:, 0:1024], in_=sR[:]))
                b.wait("sy", dra)
                dbg_last[0] = b.op("sy", lambda e: e.dma_start(
                    out=taps["dbg_attn"][:], in_=sOUT[:, 0:1024]),
                    inc="ds", amt=16)
            b.wait("ac", r1ev)
            sq1 = b.op("ac", lambda e: e.activation(
                out=sJNK[:], in_=sR[:, 0:512],
                func=AF.Square, accum_out=sST1[:, 2:3]), inc="ac")
            b.wait("ac", sq1)
            sqev = b.op("ac", lambda e: e.activation(
                out=sJNK[:], in_=sR[:, 512:1024],
                func=AF.Square, accum_out=sST1[:, 8:9]), inc="ac")
            sR_last_ac[0] = sqev
            b.wait("dv", sqev)
            b.op("dv", lambda e: e.tensor_tensor(
                out=sST1[:, 2:3], in0=sST1[:, 2:3], in1=sST1[:, 8:9], op=ALU.add), tiny=True)
            b.op("dv", lambda e: e.tensor_tensor(
                out=sST1[:, 0:1], in0=sST1[:, 0:1], in1=sST1[:, 1:2], op=ALU.add), tiny=True)
            b.op("dv", lambda e: e.tensor_scalar(
                out=sST1[:, 3:4], in0=sST1[:, 0:1], scalar1=1.0 / D, scalar2=None,
                op0=ALU.mult), tiny=True)
            b.op("dv", lambda e: e.tensor_tensor(
                out=sST1[:, 4:5], in0=sST1[:, 3:4], in1=sST1[:, 3:4], op=ALU.mult), tiny=True)
            b.op("dv", lambda e: e.tensor_scalar(
                out=sST1[:, 2:3], in0=sST1[:, 2:3], scalar1=1.0 / D,
                scalar2=None, op0=ALU.mult), tiny=True)
            b.op("dv", lambda e: e.tensor_tensor(
                out=sST1[:, 4:5], in0=sST1[:, 2:3], in1=sST1[:, 4:5],
                op=ALU.subtract), tiny=True)
            b.op("dv", lambda e: e.tensor_scalar(
                out=sST1[:, 4:5], in0=sST1[:, 4:5], scalar1=1e-5, scalar2=None,
                op0=ALU.add), tiny=True)
            _rsqrt(b, sST1, sST1I, 4, 5)
            b.op("dv", lambda e: e.tensor_tensor(
                out=sST1[:, 6:7], in0=sST1[:, 3:4], in1=sST1[:, 5:6], op=ALU.mult), tiny=True)
            b.op("dv", lambda e: e.tensor_scalar(
                out=sST1[:, 6:7], in0=sST1[:, 6:7], scalar1=-1.0, scalar2=None,
                op0=ALU.mult), tiny=True)
            b.wait("dv", sx1_consumed[c % 3])
            x1ev = b.op("dv", lambda e, c=c: e.tensor_scalar(
                out=sX1[:, (c % 3) * 1024:(c % 3 + 1) * 1024], in0=sR[:],
                scalar1=sST1[:, 5:6], scalar2=sST1[:, 6:7],
                op0=ALU.mult, op1=ALU.add), inc="dv")
            chunk_x1[c] = x1ev
            if debug_taps and c == 0:
                b.wait("dv", dbg_last[0])
                dx1 = b.op("dv", lambda e: e.tensor_copy(out=sOUT[:, 0:512],
                                                         in_=sX1[:, 0:512]))
                b.wait("sy", dx1)
                dbg_last[0] = b.op("sy", lambda e: e.dma_start(
                    out=taps["dbg_x1"][:, 0:512], in_=sOUT[:, 0:512]),
                    inc="ds", amt=16)
            yield

            def chain_gen(pm=pm, PM=PM, gtev=gtev, zev=zev, c=c):
                def chain_mm(region, lhsT, rhs, dst=None, mask=None):
                    b.bank_write("pe", pm)
                    e1 = b.op("pe", lambda e, region=region, lhsT=lhsT, rhs=rhs:
                              e.matmul(region, lhsT, rhs, start=True, stop=True),
                              inc="pe")
                    b.bank_note_write("pe", pm, e1)
                    b.wait("dv", e1)
                    b.bank_read("dv", pm)
                    if mask is None:
                        c1 = b.op("dv", lambda e, dst=dst, region=region:
                                  e.tensor_copy(out=dst, in_=region), inc="dv",
                                  relax=True)
                    else:
                        c1 = b.op("dv", lambda e, dst=dst, region=region, mask=mask:
                                  e.tensor_tensor(out=dst, in0=region, in1=mask,
                                                  op=ALU.mult), inc="dv",
                                  relax=True)
                    b.bank_note_read("dv", pm, c1)
                    return c1

                b.wait("pe", gtev)
                chain_mm(PM[0:64, 0:88], sG[:], sULbd[0:64, :], dst=sT1[:])
                yield
                chain_mm(PM[0:64, 128:216], sGT[:], sVRbd[0:64, :], dst=sT1b[:])
                yield
                chain_mm(PM[0:88, 256:344], sVRbd[0:64, :], sT1[:], dst=sWm[:],
                         mask=sMSK88[:])
                yield
                chain_mm(PM[0:88, 344:432], sULbd[0:64, :], sT1b[:], dst=sWmT[:],
                         mask=sMSK88[:])
                yield
                chain_mm(PM[0:88, 0:64], sWmT[:], sULbdT[:], dst=sF1[:])
                yield
                chain_mm(PM[0:88, 128:192], sWm[:], sURbdT[:], dst=sE1[:])
                yield
                b.bank_write("pe", pm)
                b.op("pe", lambda e: e.matmul(PM[0:64, 256:320], sVRbdT[:], sF1[:],
                                              start=True, stop=False))
                e7 = b.op("pe", lambda e: e.matmul(PM[0:64, 256:320], sVLbdT[:],
                                                   sE1[:], start=False, stop=True),
                          inc="pe")
                b.bank_note_write("pe", pm, e7)
                b.wait("dv", e7)
                b.wait("dv", zev)
                b.bank_read("dv", pm)
                stev = b.op("dv", lambda e: e.tensor_tensor(
                    out=sST[:], in0=sST[:], in1=PM[0:64, 256:320], op=ALU.add),
                    inc="dv", tiny=True)
                b.bank_note_read("dv", pm, stev)
                stbev = b.op("dv", lambda e: e.tensor_copy(out=sSTB[:], in_=sST[:]),
                             inc="dv", tiny=True)
                stb_ev[0] = stbev
                if debug_taps and c == 0:
                    b.wait("dv", dbg_last[0])
                    dst_ = b.op("dv", lambda e: e.tensor_copy(out=sOUT[0:64, 0:64],
                                                              in_=sST[:]))
                    b.wait("sy", dst_)
                    dbg_last[0] = b.op("sy", lambda e: e.dma_start(
                        out=taps["dbg_st"][:], in_=sOUT[0:64, 0:64]),
                        inc="ds", amt=16)

            chain_gens.append(chain_gen())

        def ffn_super(sc):
            c0 = 2 * sc
            x1evs = [chunk_x1[c0], chunk_x1[c0 + 1]]
            evx = []
            for ci in range(SUP):
                c = c0 + ci
                b.wait("pe", x1evs[ci])
                b.bank_write("pe", "PT")
                tev = None
                for j in range(8):
                    def em(e, c=c, j=j):
                        return e.transpose(
                            PT[:, j * 128:(j + 1) * 128],
                            sX1[:, (c % 3) * 1024 + j * 128:
                                (c % 3) * 1024 + (j + 1) * 128], sIDN[:])
                    tev = b.op("pe", em, inc="pe" if j == 7 else None)
                b.bank_note_write("pe", "PT", tev)
                b.wait("ac", tev)
                b.wait("ac", x1t_free[0])
                b.bank_read("ac", "PT")
                dst = sX1T[:].rearrange("p (j t) -> p j t", j=8)
                eev = b.op("ac", lambda e, ci=ci, dst=dst: e.copy(
                    out=dst[:, :, ci * 128:(ci + 1) * 128],
                    in_=PT[:].rearrange("p (j t) -> p j t", j=8)), inc="ac")
                b.bank_note_read("ac", "PT", eev)
                evx.append(eev)
                yield

            b.waits("pe", evx)
            for i in range(32):
                while prep_out["w1ev"][i] is None:
                    try:
                        next(pg_holder[0])
                    except StopIteration:
                        break
                b.wait("pe", prep_out["w1ev"][i])
                pk = "P6" if i % 2 == 0 else "P4"
                P = P6 if i % 2 == 0 else P4
                b.bank_write("pe", pk)
                ev = None
                for j in range(8):
                    def em(e, i=i, j=j, P=P):
                        return e.matmul(
                            P[:, 0:256],
                            sW1T[:, j * FF + i * 128:j * FF + i * 128 + 128],
                            sX1T[:, j * 256:(j + 1) * 256],
                            start=(j == 0), stop=(j == 7))
                    ev = b.op("pe", em, inc="pe" if j == 7 else None)
                b.bank_note_write("pe", pk, ev)
                b.wait("ac", ev)
                b.bank_read("ac", pk)
                gev = b.op("ac", lambda e, i=i, P=P: e.activation(
                    out=sHT[:, i * 256:(i + 1) * 256], in_=P[:, 0:256],
                    func=AF.Gelu, bias=sB1[:, i:i + 1]), inc="ac")
                b.bank_note_read("ac", pk, gev)
                yield
            x1t_free[0] = ev
            hlast = gev
            if debug_taps and sc == 0:
                b.wait("dv", hlast)
                b.wait("dv", dbg_last[0])
                dh = b.op("dv", lambda e: e.tensor_copy(out=sOUT[:, 0:256],
                                                        in_=sHT[:, 0:256]),
                          inc="dv")
                b.wait("sy", dh)
                dbg_last[0] = b.op("sy", lambda e: e.dma_start(
                    out=taps["dbg_ht"][:], in_=sOUT[:, 0:256]),
                    inc="ds", amt=16)

            b.wait("pe", hlast)
            for ci in range(SUP):
                c = c0 + ci
                for half in range(2):
                    pk2, PQ = ("P4", P4) if half == 0 else ("P6", P6)
                    b.bank_write("pe", pk2)
                    b.op("pe", lambda e, half=half, PQ=PQ: e.matmul(
                        PQ[:], sCMASK[64:65, 512:640],
                        sWOB[64:65, half * 512:(half + 1) * 512],
                        start=True, stop=False))
                    ev = None
                    for i in range(32):
                        while prep_out["w2ev"][i // 8] is None:
                            try:
                                next(pg_holder[0])
                            except StopIteration:
                                break
                        b.wait("pe", prep_out["w2ev"][i // 8])
                        def em(e, i=i, ci=ci, half=half, PQ=PQ):
                            return e.matmul(
                                PQ[:],
                                sHT[:, i * 256 + ci * 128:i * 256 + ci * 128 + 128],
                                sW2T[:, i * 1024 + half * 512:
                                     i * 1024 + (half + 1) * 512],
                                start=False, stop=(i == 31))
                        ev = b.op("pe", em, inc="pe" if i == 31 else None)
                    b.bank_note_write("pe", pk2, ev)
                    b.wait("dv", ev)
                    if half == 0:
                        b.wait("dv", sout_stored[0])
                    b.bank_read("dv", pk2)
                    r2ev = b.op("dv", lambda e, c=c, half=half, PQ=PQ:
                                e.scalar_tensor_tensor(
                        out=sOUT[:, half * 512:(half + 1) * 512],
                        in0=sX1[:, (c % 3) * 1024 + half * 512:
                               (c % 3) * 1024 + (half + 1) * 512],
                        scalar=1.0, in1=PQ[:], op0=ALU.mult, op1=ALU.add,
                        accum_out=sST1[:, 9 + half:10 + half]), inc="dv",
                        relax=True)
                    b.bank_note_read("dv", pk2, r2ev)
                    yield
                sx1_consumed[c % 3] = r2ev
                sx1_chunk[c % 3] = c
                b.wait("ac", r2ev)
                sq1 = b.op("ac", lambda e: e.activation(
                    out=sJNK[:], in_=sOUT[:, 0:512],
                    func=AF.Square, accum_out=sST1[:, 11:12]), inc="ac")
                b.wait("ac", sq1)
                sqev = b.op("ac", lambda e: e.activation(
                    out=sJNK[:], in_=sOUT[:, 512:1024],
                    func=AF.Square, accum_out=sST1[:, 12:13]), inc="ac")
                b.wait("dv", sqev)
                b.op("dv", lambda e: e.tensor_tensor(
                    out=sST1[:, 11:12], in0=sST1[:, 11:12], in1=sST1[:, 12:13],
                    op=ALU.add), tiny=True)
                b.op("dv", lambda e: e.tensor_tensor(
                    out=sST1[:, 9:10], in0=sST1[:, 9:10], in1=sST1[:, 10:11],
                    op=ALU.add), tiny=True)
                b.op("dv", lambda e: e.tensor_scalar(
                    out=sST1[:, 13:14], in0=sST1[:, 9:10], scalar1=1.0 / D,
                    scalar2=None, op0=ALU.mult), tiny=True)
                b.op("dv", lambda e: e.tensor_tensor(
                    out=sST1[:, 14:15], in0=sST1[:, 13:14], in1=sST1[:, 13:14],
                    op=ALU.mult), tiny=True)
                b.op("dv", lambda e: e.tensor_scalar(
                    out=sST1[:, 11:12], in0=sST1[:, 11:12], scalar1=1.0 / D,
                    scalar2=None, op0=ALU.mult), tiny=True)
                b.op("dv", lambda e: e.tensor_tensor(
                    out=sST1[:, 14:15], in0=sST1[:, 11:12], in1=sST1[:, 14:15],
                    op=ALU.subtract), tiny=True)
                b.op("dv", lambda e: e.tensor_scalar(
                    out=sST1[:, 14:15], in0=sST1[:, 14:15], scalar1=1e-5,
                    scalar2=None, op0=ALU.add), tiny=True)
                _rsqrt(b, sST1, sST1I, 14, 15, scr=17)
                b.op("dv", lambda e: e.tensor_tensor(
                    out=sST1[:, 16:17], in0=sST1[:, 13:14], in1=sST1[:, 15:16],
                    op=ALU.mult), tiny=True)
                nev = b.op("dv", lambda e: e.tensor_scalar(
                    out=sST1[:, 16:17], in0=sST1[:, 16:17], scalar1=-1.0,
                    scalar2=None, op0=ALU.mult), inc="dv", tiny=True)
                b.wait("ac", nev)
                if debug_taps:
                    b.wait("ac", dbg_last[0])
                oev = b.op("ac", lambda e: e.activation(
                    out=sOUT[:], in_=sOUT[:],
                    func=AF.Identity, bias=sST1[:, 16:17], scale=sST1[:, 15:16]),
                    inc="ac")
                b.wait("sy", oev)
                sout_stored[0] = b.op("sy", lambda e, c=c: e.dma_start(
                    out=out[c * 128:(c + 1) * 128, :], in_=sOUT[:]),
                    inc="ds", amt=16)
                yield

        b.wait("pe", state_init)
        b.wait("dv", state_init)

        pg = prep_gen()
        pg_holder[0] = pg
        others = gen_pool
        others.append(pg)
        pending = []
        done_ffn = [0]

        def refill():
            if pg not in others and pending and len(others) == 0:
                others.append(ffn_super(pending.pop(0)))

        def on_done(g_):
            if g_ is not pg:
                done_ffn[0] += 1

        pool_hooks["refill"] = refill
        pool_hooks["done"] = on_done

        def pump(target):
            while True:
                try:
                    next(target)
                except StopIteration:
                    return
                for g_ in list(chain_gens):
                    try:
                        next(g_)
                    except StopIteration:
                        chain_gens.remove(g_)
                advance_pool()

        for sc in range(nsuper):
            for ci in range(SUP):
                pump(scan_chunk(2 * sc + ci))
            pending.append(sc)
            refill()
        while others or pending:
            advance_pool()
        assert not pending

        b.wait("sy", sout_stored[0])
        total_st = b.cnt["ds"]
        b.ops["sy"].append(lambda e: e.wait_ge(sems["ds"], total_st))

        with nc.Block() as blk:
            @blk.sync
            def _(e):
                for f in b.ops["sy"]:
                    f(e)

            @blk.vector
            def _(e):
                for f in b.ops["dv"]:
                    f(e)

            @blk.scalar
            def _(e):
                for f in b.ops["ac"]:
                    f(e)

            @blk.gpsimd
            def _(e):
                for f in b.ops["po"]:
                    f(e)

            @blk.tensor
            def _(e):
                for f in b.ops["pe"]:
                    f(e)

    return nc


def _rsqrt(b, sST1, sST1I, src, dst, scr=7):
    b.op("dv", lambda e: e.tensor_scalar(
        out=sST1I[:, dst:dst + 1], in0=sST1I[:, src:src + 1], scalar1=1,
        scalar2=None, op0=ALU.arith_shift_right), tiny=True)
    b.op("dv", lambda e: e.tensor_scalar(
        out=sST1I[:, dst:dst + 1], in0=sST1I[:, dst:dst + 1], scalar1=-1,
        scalar2=MAGIC, op0=ALU.mult, op1=ALU.add), tiny=True)
    for _ in range(3):
        b.op("dv", lambda e: e.tensor_tensor(
            out=sST1[:, scr:scr + 1], in0=sST1[:, dst:dst + 1], in1=sST1[:, dst:dst + 1],
            op=ALU.mult), tiny=True)
        b.op("dv", lambda e: e.tensor_tensor(
            out=sST1[:, scr:scr + 1], in0=sST1[:, scr:scr + 1], in1=sST1[:, src:src + 1],
            op=ALU.mult), tiny=True)
        b.op("dv", lambda e: e.tensor_scalar(
            out=sST1[:, scr:scr + 1], in0=sST1[:, scr:scr + 1], scalar1=-0.5, scalar2=1.5,
            op0=ALU.mult, op1=ALU.add), tiny=True)
        b.op("dv", lambda e: e.tensor_tensor(
            out=sST1[:, dst:dst + 1], in0=sST1[:, dst:dst + 1], in1=sST1[:, scr:scr + 1],
            op=ALU.mult), tiny=True)



_NC_CACHE = {}

def kernel(**inputs):
    key = ("prog", DEBUG_TAPS)
    if key not in _NC_CACHE:
        _NC_CACHE[key] = build_program(debug_taps=DEBUG_TAPS)
    nc = _NC_CACHE[key]
    consts = build_constants(inputs)
    x = np.ascontiguousarray(inputs["L0_attn_x"].astype(np.float32, copy=False))
    shared = dict(W1=np.ascontiguousarray(inputs["L0_ffn_W1"]),
                  W2=np.ascontiguousarray(inputs["L0_ffn_W2"]), **consts)
    in_maps = [dict(x=np.ascontiguousarray(x[bb]), **shared) for bb in range(8)]
    res = run_bass_kernel_spmd(nc, in_maps, core_ids=list(range(8)))
    out = np.stack([res.results[bb]["out"] for bb in range(8)], axis=0)
    return out.astype(np.float32, copy=False)
